# version 01
# baseline (speedup 1.0000x reference)
import numpy as np
import ml_dtypes
from contextlib import ExitStack

import concourse.bass as bass
import concourse.mybir as mybir
from concourse.bass_utils import run_bass_kernel_spmd

F32 = mybir.dt.float32
BF16 = mybir.dt.bfloat16
AF = mybir.ActivationFunctionType
ALU = mybir.AluOpType
AX = mybir.AxisListType
bfdt = ml_dtypes.bfloat16

C = 1024
DH = 64
HP = 8
DBG = {}


class Chain:
    def __init__(self, sem, step=1):
        self.sem = sem
        self.n = 0
        self.step = step

    def inc(self, inst, k=None):
        k = self.step if k is None else k
        inst.then_inc(self.sem, k)
        self.n += k
        return self.n


class Floors:
    def __init__(self):
        self.f = {}

    def wait(self, eng, key, chain, n):
        if n <= 0:
            return
        d = self.f.setdefault(key, {})
        if n > d.get(id(chain), 0):
            assert n <= chain.n, f"wait {n} > emitted {chain.n} ({key})"
            eng.wait_ge(chain.sem, n)
            d[id(chain)] = n


def build(T_ALL=8192, T_OWN=4096, nstage=99):
    KC = T_ALL // 128
    NSLAB = T_ALL // 512
    NQB = T_OWN // 512
    NOWN = T_OWN // 128
    NG = KC // 2

    nc = bass.Bass("TRN2", target_bir_lowering=False)

    xb = nc.dram_tensor("xb", [T_ALL, C], F32, kind="ExternalInput")
    qkv_w = nc.dram_tensor("qkv_w", [3 * C, C], F32, kind="ExternalInput")
    proj_w = nc.dram_tensor("proj_w", [C, C], F32, kind="ExternalInput")
    proj_b = nc.dram_tensor("proj_b", [1, C], F32, kind="ExternalInput")
    qnw = nc.dram_tensor("qnw", [1, DH], F32, kind="ExternalInput")
    knw = nc.dram_tensor("knw", [1, DH], F32, kind="ExternalInput")
    cbw = nc.dram_tensor("cbw", [C, C], F32, kind="ExternalInput")
    scb = nc.dram_tensor("scb", [1, C], F32, kind="ExternalInput")
    fbias = nc.dram_tensor("fbias", [1, C], F32, kind="ExternalInput")
    ident_in = nc.dram_tensor("ident_in", [128, 128], BF16, kind="ExternalInput")

    out = nc.dram_tensor("out", [T_OWN, C], F32, kind="ExternalOutput")

    xnT_h = nc.dram_tensor("xnT_h", [8, 128, T_ALL], BF16, kind="Internal")
    g_h = nc.dram_tensor("g_h", [T_OWN, C], F32, kind="Internal")
    oT_h = nc.dram_tensor("oT_h", [HP, 128, T_OWN], BF16, kind="Internal")

    with ExitStack() as ctx:
        sb = lambda n_, s_, d_: ctx.enter_context(nc.sbuf_tensor(n_, s_, d_))
        pt = lambda n_, s_, d_: ctx.enter_context(nc.psum_tensor(n_, s_, d_))
        mksem = lambda n_: ctx.enter_context(nc.semaphore(n_))

        identb = sb("identb", [128, 128], BF16)
        onesf = sb("onesf", [1, 128], F32)
        ones8 = sb("ones8", [1, 128], F32)
        ones127 = sb("ones127", [1, 128], F32)
        epsLN = sb("epsLN", [128, 1], F32)
        epsRMS = sb("epsRMS", [128, 1], F32)
        epsTINY = sb("epsTINY", [1, 1], F32)
        wrow = sb("wrow", [1, 256], F32)
        srow = sb("srow", [1, C], F32)
        frow = sb("frow", [1, C], F32)
        prow = sb("prow", [1, C], F32)
        wmul = sb("wmul", [128, 256], F32)
        scbB = sb("scbB", [128, C], F32)
        fbiasB = sb("fbiasB", [128, C], F32)
        projbB = sb("projbB", [128, C], F32)

        xt = [sb(f"xt{j}", [128, C], F32) for j in range(2)]
        lnsum = sb("lnsum", [128, 1], F32)

        xss = sb("xss", [128, 1], F32)
        lnmean = sb("lnmean", [128, 1], F32)
        lnex2 = sb("lnex2", [128, 1], F32)
        lnm2 = sb("lnm2", [128, 1], F32)
        lnvar = sb("lnvar", [128, 1], F32)
        lnl = sb("lnl", [128, 1], F32)
        lnr = sb("lnr", [128, 1], F32)
        xn_tok = [sb(f"xn_tok{j}", [128, C], BF16) for j in range(2)]
        xnT_st = [sb(f"xnT_st{j}", [128, C], BF16) for j in range(2)]

        wstage = sb("wstage", [128, C], F32)
        wstage_b = sb("wstage_b", [128, C], BF16)
        cbT = sb("cbT", [128, 8 * C], BF16)
        projT = cbT
        wt_buf = sb("wt_buf", [128, 8 * 384], BF16)


        gtile = [sb(f"gtile{j}", [128, C], F32) for j in range(2)]
        gld = [sb(f"gld{j}", [128, C], F32) for j in range(2)]

        xslab = [sb(f"xslab{j}", [128, 8 * 512], BF16) for j in range(2)]
        kT = sb("kT", [128, T_ALL], BF16)
        qT = sb("qT", [128, T_OWN], BF16)
        vbuf = sb("vbuf", [128, 2 * KC * 65], BF16)
        qkvf4 = sb("qkvf4", [128, 4 * 384], F32)
        sqt = sb("sqt", [128, 256], F32)
        rms16 = sb("rms16", [128, 16], F32)
        rstdL = sb("rstdL", [128, 16], F32)
        rstd16 = sb("rstd16", [128, 16], F32)
        tmn = sb("tmn", [128, 256], F32)
        kq_hat = sb("kq_hat", [128, 256], BF16)
        sqm = sb("sqm", [128, 256], BF16)
        m4 = sb("m4", [128, 4], F32)
        m4b = sb("m4b", [128, 4], BF16)
        mx4 = sb("mx4", [4, 1], F32)
        maxacc = sb("maxacc", [4, 1], F32)
        maxaccb = sb("maxaccb", [4, 1], BF16)
        mrow = sb("mrow", [1, 4], F32)
        mab = sb("mab", [1, 2], F32)
        mln = sb("mln", [1, 2], F32)
        negm2 = sb("negm2", [1, 2], F32)
        negMt = sb("negMt", [128, 2], F32)

        ptb = [sb(f"ptb{j}", [128, 1024], BF16) for j in range(4)]
        d01 = sb("d01", [1, 1024], F32)
        ld01 = sb("ld01", [1, 1024], F32)
        r01 = sb("r01", [1, 1024], F32)
        rb0 = sb("rb0", [64, 512], F32)
        rb1 = sb("rb1", [64, 512], F32)
        obuf = [sb(f"obuf{j}", [128, 512], BF16) for j in range(2)]

        ostage = [sb(f"ostage{j}", [128, 8 * 128], BF16) for j in range(2)]
        xld = xt
        s1t = sb("s1t", [128, C], F32)
        s2t = sb("s2t", [128, C], F32)
        outt = [sb(f"outt{j}", [128, C], F32) for j in range(2)]
        xsq = s1t
        fo = s2t
        fo2 = outt[0]

        psA = pt("psA", [128, 1024], F32)
        psB = pt("psB", [128, 1024], F32)
        psO0 = pt("psO0", [65, 512], F32)
        psO1 = pt("psO1", [65, 512], F32)
        psT0 = pt("psT0", [128, 128], BF16)
        psT1 = pt("psT1", [128, 128], BF16)
        psTs = (psT0, psT1)

        c_ldc = Chain(mksem("ldc"), 16)
        c_ldw = Chain(mksem("ldw"), 16)
        c_ldx = [Chain(mksem(f"ldx{j}"), 16) for j in range(2)]
        c_lds = [Chain(mksem(f"lds{j}"), 16) for j in range(2)]
        c_ldg = [Chain(mksem(f"ldg{j}"), 16) for j in range(2)]
        c_ld5 = [Chain(mksem(f"ld5{j}"), 16) for j in range(2)]
        c_stx = [Chain(mksem(f"stx{j}"), 16) for j in range(2)]
        c_stg = [Chain(mksem(f"stg{j}"), 16) for j in range(2)]
        c_sto = [Chain(mksem(f"sto{j}"), 16) for j in range(2)]
        c_stu = [Chain(mksem(f"stu{j}"), 16) for j in range(2)]
        tq = Chain(mksem("tq"))
        vq = Chain(mksem("vq"))
        aq = Chain(mksem("aq"))

        FL = Floors()
        block = ctx.enter_context(nc.Block())
        SYNC, GP, T, V, A = block.sync, block.gpsimd, block.tensor, block.vector, block.scalar

        free = {
            "xt0": (vq, 0), "xt1": (vq, 0),
            "xn0": (tq, 0), "xn1": (tq, 0),
            "xnTst0_dma": (c_stx[0], 0), "xnTst1_dma": (c_stx[1], 0),
            "xnTst0_ffn": (tq, 0), "xnTst1_ffn": (tq, 0),
            "psB_v": (vq, 0),
            "gtile0": (c_stg[0], 0), "gtile1": (c_stg[1], 0),
            "wstage": (vq, 0), "wstage_b": (tq, 0),
            "psT0": (vq, 0), "psT1": (vq, 0),
            "wt_buf": (tq, 0), "wbT": (tq, 0),
            "xslab0": (tq, 0), "xslab1": (tq, 0),
            "psQ": (vq, 0),
            "kT": (tq, 0), "qT": (tq, 0), "vbuf": (tq, 0),
            "negMt": (aq, 0),
            "maxacc": (vq, 0),
            "gld0": (aq, 0), "gld1": (aq, 0),
            "obuf0": (c_sto[0], 0), "obuf1": (c_sto[1], 0),
            "ostage0": (tq, 0), "ostage1": (tq, 0),

            "outt0": (c_stu[0], 0), "outt1": (c_stu[1], 0),
        }

        def FR(name):
            return free[name]

        def setF(name, chain):
            free[name] = (chain, chain.n)

        def wf(eng, key, name):
            ch, n = free[name]
            FL.wait(eng, key, ch, n)

        @SYNC
        def _(sync):
            c_ldc.inc(sync.dma_start(identb[:], ident_in.ap()))
            c_ldc.inc(sync.dma_start(wrow[0:1, 0:64], knw.ap()))
            c_ldc.inc(sync.dma_start(wrow[0:1, 64:128], knw.ap()))
            c_ldc.inc(sync.dma_start(wrow[0:1, 128:192], qnw.ap()))
            c_ldc.inc(sync.dma_start(wrow[0:1, 192:256], qnw.ap()))
            c_ldc.inc(sync.dma_start(srow[:], scb.ap()))
            c_ldc.inc(sync.dma_start(frow[:], fbias.ap()))
            c_ldc.inc(sync.dma_start(prow[:], proj_b.ap()))
        ld_consts = c_ldc.n

        @V
        def _(vector):
            vq.inc(vector.memset(onesf[:], 1.0))
            vq.inc(vector.memset(ones8[:], 0.125))
            vq.inc(vector.memset(ones127[:], 1.0 / 127.0))
            vq.inc(vector.memset(epsLN[:], 1e-5))
            vq.inc(vector.memset(epsRMS[:], 1e-6))
            vq.inc(vector.memset(epsTINY[:], 1e-30))
        v_consts = vq.n

        if DBG.get("s0_bc", True):
            @T
            def _(tensor):
                FL.wait(tensor, "t", c_ldc, ld_consts)
                FL.wait(tensor, "t", vq, v_consts)
                tensor.matmul(psB[:, 0:128], onesf[:], wrow[0:1, 0:128], start=True, stop=True)
                tq.inc(tensor.matmul(psB[:, 128:256], ones8[:], wrow[0:1, 128:256],
                                     start=True, stop=True))
                tensor.matmul(psA[:, 0:512], ones127[:], srow[0:1, 0:512], start=True, stop=True)
                tq.inc(tensor.matmul(psA[:, 512:1024], ones127[:], srow[0:1, 512:1024],
                                     start=True, stop=True))
            t_bc1 = tq.n

            @V
            def _(vector):
                FL.wait(vector, "v", tq, t_bc1)
                vq.inc(vector.tensor_copy(wmul[:], psB[:, 0:256]))
                vq.inc(vector.tensor_copy(scbB[:], psA[:]))
            v_bc1 = vq.n

            @T
            def _(tensor):
                FL.wait(tensor, "t", vq, v_bc1)
                tensor.matmul(psB[:, 0:512], onesf[:], frow[0:1, 0:512], start=True, stop=True)
                tensor.matmul(psB[:, 512:1024], onesf[:], frow[0:1, 512:1024],
                              start=True, stop=True)
                tensor.matmul(psA[:, 0:512], onesf[:], prow[0:1, 0:512], start=True, stop=True)
                tq.inc(tensor.matmul(psA[:, 512:1024], onesf[:], prow[0:1, 512:1024],
                                     start=True, stop=True))
            t_bc2 = tq.n

            @V
            def _(vector):
                FL.wait(vector, "v", tq, t_bc2)
                vq.inc(vector.tensor_copy(fbiasB[:], psB[:]))
                vq.inc(vector.tensor_copy(projbB[:], psA[:]))

        def load_wT(dst_sb, src_dram, rows0, dst_cc_stride, dst_off, dst_free_key=None):
            @SYNC
            def _(sync):
                wf(sync, "s", "wstage")
                c_ldw.inc(sync.dma_start(wstage[:], src_dram.ap()[rows0:rows0 + 128, :]))
            mark_ld = c_ldw.n

            @V
            def _(vector):
                FL.wait(vector, "v", c_ldw, mark_ld)
                wf(vector, "v", "wstage_b")
                vq.inc(vector.tensor_copy(wstage_b[:], wstage[:]))
            setF("wstage", vq)
            mark_cast = vq.n

            for cc in range(8):
                slot = cc % 2

                @T
                def _(tensor, cc=cc, slot=slot):
                    FL.wait(tensor, "t", vq, mark_cast)
                    wf(tensor, "t", f"psT{slot}")
                    if dst_free_key:
                        wf(tensor, "t", dst_free_key)
                    tq.inc(tensor.transpose(psTs[slot][:],
                                            wstage_b[:, cc * 128:(cc + 1) * 128],
                                            identb[:]))
                mark_t = tq.n
                if cc == 7:
                    setF("wstage_b", tq)

                @V
                def _(vector, cc=cc, slot=slot):
                    FL.wait(vector, "v", tq, mark_t)
                    vq.inc(vector.tensor_copy(
                        dst_sb[:, cc * dst_cc_stride + dst_off:
                               cc * dst_cc_stride + dst_off + 128],
                        psTs[slot][:]))
                free[f"psT{slot}"] = (vq, vq.n)
            return vq.n

        cb_ready = 0
        for r in range(8 if DBG.get("s0_cbt", True) else 0):
            cb_ready = load_wT(cbT, cbw, r * 128, C, r * 128)

        for i in range(min(KC if nstage >= 1 else 0, DBG.get("s1_chunks", 10**9))):
            j = i % 2
            own = i < NOWN

            @SYNC
            def _(sync, i=i, j=j):
                wf(sync, "s", f"xt{j}")
                c_ldx[j].inc(sync.dma_start(xt[j][:], xb.ap()[i * 128:(i + 1) * 128, :]))
            mark_x = c_ldx[j].n

            @V
            def _(vector, j=j):
                FL.wait(vector, "v", c_ldx[j], mark_x)
                vq.inc(vector.reduce_sum(lnsum[:], xt[j][:], axis=AX.X))
                vq.inc(vector.tensor_mul(xsq[:], xt[j][:], xt[j][:]))
                FL.wait(vector, "v", vq, vq.n)
                vq.inc(vector.reduce_sum(xss[:], xsq[:], axis=AX.X))
                vq.inc(vector.tensor_scalar_mul(lnmean[:], lnsum[:], 1.0 / C))
                FL.wait(vector, "v", vq, vq.n)
                vq.inc(vector.tensor_scalar_mul(lnex2[:], xss[:], 1.0 / C))
                vq.inc(vector.tensor_mul(lnm2[:], lnmean[:], lnmean[:]))
                FL.wait(vector, "v", vq, vq.n)
                vq.inc(vector.tensor_sub(lnvar[:], lnex2[:], lnm2[:]))
            mark_var = vq.n

            @A
            def _(scalar):
                FL.wait(scalar, "a", vq, mark_var)
                aq.inc(scalar.activation(lnl[:], lnvar[:], AF.Ln, bias=epsLN[:], scale=1.0))
                FL.wait(scalar, "a", aq, aq.n)
                aq.inc(scalar.activation(lnr[:], lnl[:], AF.Exp, scale=-0.5))
            mark_rstd = aq.n

            @V
            def _(vector, j=j):
                FL.wait(vector, "v", aq, mark_rstd)
                wf(vector, "v", f"xn{j}")
                vq.inc(vector.tensor_scalar(xn_tok[j][:], xt[j][:], lnmean[:], lnr[:],
                                            ALU.subtract, ALU.mult))
            mark_xn = vq.n
            free[f"xt{j}"] = (vq, vq.n)

            for cc in range(8 if DBG.get("s1_tp", True) else 0):
                slot = cc % 2

                @T
                def _(tensor, cc=cc, j=j, slot=slot):
                    FL.wait(tensor, "t", vq, mark_xn)
                    wf(tensor, "t", f"psT{slot}")
                    wf(tensor, "t", f"xnTst{j}_dma")
                    wf(tensor, "t", f"xnTst{j}_ffn")
                    tq.inc(tensor.transpose(psTs[slot][:],
                                            xn_tok[j][:, cc * 128:(cc + 1) * 128],
                                            identb[:]))
                mark_t = tq.n
                if cc == 7:
                    free[f"xn{j}"] = (tq, tq.n)

                @V
                def _(vector, cc=cc, j=j, slot=slot):
                    FL.wait(vector, "v", tq, mark_t)
                    vq.inc(vector.tensor_copy(xnT_st[j][:, cc * 128:(cc + 1) * 128],
                                              psTs[slot][:]))
                free[f"psT{slot}"] = (vq, vq.n)
            mark_xnT = vq.n

            @GP
            def _(gp, i=i, j=j):
                if not DBG.get("s1_dma", True):
                    return
                FL.wait(gp, "g", vq, mark_xnT)
                c_stx[j].inc(gp.dma_start(
                    xnT_h.ap()[:, :, i * 128:(i + 1) * 128].rearrange("c p t -> p c t"),
                    xnT_st[j].rearrange("p (c t) -> p c t", t=128)))
            free[f"xnTst{j}_dma"] = (c_stx[j], c_stx[j].n)

            if own and DBG.get("s1_ffn", True):
                @T
                def _(tensor, j=j):
                    FL.wait(tensor, "t", vq, mark_xnT)
                    FL.wait(tensor, "t", vq, cb_ready)
                    wf(tensor, "t", "psB_v")
                    last = None
                    for half in range(2):
                        for cc in range(8):
                            last = tensor.matmul(
                                psB[:, half * 512:(half + 1) * 512],
                                xnT_st[j][:, cc * 128:(cc + 1) * 128],
                                cbT[:, cc * C + half * 512: cc * C + half * 512 + 512],
                                start=(cc == 0), stop=(cc == 7))
                    tq.inc(last)
                mark_ffn = tq.n
                free[f"xnTst{j}_ffn"] = (tq, tq.n)

                @V
                def _(vector, j=j):
                    FL.wait(vector, "v", tq, mark_ffn)
                    vq.inc(vector.tensor_mul(fo[:], psB[:], scbB[:]))
                    FL.wait(vector, "v", vq, vq.n)
                    vq.inc(vector.tensor_add(fo2[:], fo[:], fbiasB[:]))
                    FL.wait(vector, "v", vq, vq.n)
                    wf(vector, "v", f"gtile{j}")
                    vq.inc(vector.tensor_copy(gtile[j][:], fo2[:]))
                free["psB_v"] = (vq, vq.n)
                mark_g = vq.n

                @GP
                def _(gp, i=i, j=j):
                    FL.wait(gp, "g", vq, mark_g)
                    c_stg[j].inc(gp.dma_start(g_h.ap()[i * 128:(i + 1) * 128, :], gtile[j][:]))
                free[f"gtile{j}"] = (c_stg[j], c_stg[j].n)

        stg_totals = [c_stg[0].n, c_stg[1].n]
        stx_totals = [c_stx[0].n, c_stx[1].n]
        free["wbT"] = (tq, tq.n)
        free["outt0"] = (vq, vq.n)
        free["xt0"] = (vq, vq.n)
        free["xt1"] = (vq, vq.n)

        for i in range(NOWN if nstage >= 2 else 0):
            j = i % 2

            @SYNC
            def _(sync, i=i, j=j):
                FL.wait(sync, "s", c_stg[0], stg_totals[0])
                FL.wait(sync, "s", c_stg[1], stg_totals[1])
                wf(sync, "s", f"gld{j}")
                c_ldg[j].inc(sync.dma_start(gld[j][:], g_h.ap()[i * 128:(i + 1) * 128, :]))
            mark_g = c_ldg[j].n

            @A
            def _(scalar, j=j):
                FL.wait(scalar, "a", c_ldg[j], mark_g)
                wf(scalar, "a", f"gtile{j}")
                aq.inc(scalar.activation(gtile[j][:], gld[j][:], AF.Silu, scale=1.0))
            free[f"gld{j}"] = (aq, aq.n)
            mark_s = aq.n

            @GP
            def _(gp, i=i, j=j):
                FL.wait(gp, "g", aq, mark_s)
                c_stg[j].inc(gp.dma_start(g_h.ap()[i * 128:(i + 1) * 128, :], gtile[j][:]))
            free[f"gtile{j}"] = (c_stg[j], c_stg[j].n)

        stg_silu_totals = [c_stg[0].n, c_stg[1].n]

        free["rms16"] = (aq, 0)
        free["qkvf4"] = (vq, 0)
        prev_ofree = (vq, 0)
        prev_bankfree = (vq, 0)
        last_expA = 0
        last_expB = 0
        pv_last = {0: 0, 1: 0, 2: 0, 3: 0}

        for hp in range(HP if nstage >= 3 else 0):
            aq_hpstart = aq.n
            for gidx, rows0 in enumerate((C + hp * 128, hp * 128, 2 * C + hp * 128)):
                load_wT(wt_buf, qkv_w, rows0, 384, gidx * 128, dst_free_key="wt_buf")
            mark_wt = vq.n

            @V
            def _(vector):
                wf(vector, "v", "maxacc")
                vq.inc(vector.memset(maxacc[:], 0.0))
                wf(vector, "v", "vbuf")
                vq.inc(vector.memset(
                    vbuf.rearrange("p (c j) -> p c j", j=65)[:, :, 64:65], 1.0))
            mark_vones = vq.n

            for s in range(NSLAB):
                jsl = s % 2

                @SYNC
                def _(sync, s=s, jsl=jsl):
                    wf(sync, "s", f"xslab{jsl}")
                    FL.wait(sync, "s", c_stx[0], stx_totals[0])
                    FL.wait(sync, "s", c_stx[1], stx_totals[1])
                    for cc in range(8):
                        c_lds[jsl].inc(sync.dma_start(
                            xslab[jsl][:, cc * 512:(cc + 1) * 512],
                            xnT_h.ap()[cc, :, s * 512:(s + 1) * 512]))
                mark_slab = c_lds[jsl].n

                for c4 in range(4):
                    i = s * 4 + c4

                    @T
                    def _(tensor, c4=c4, jsl=jsl):
                        FL.wait(tensor, "t", c_lds[jsl], mark_slab)
                        FL.wait(tensor, "t", vq, mark_wt)
                        wf(tensor, "t", "psQ")
                        FL.wait(tensor, "t", aq, aq_hpstart)
                        last = None
                        for cc in range(8):
                            last = tensor.matmul(
                                psB[:, 0:384],
                                xslab[jsl][:, cc * 512 + c4 * 128: cc * 512 + c4 * 128 + 128],
                                wt_buf[:, cc * 384: cc * 384 + 384],
                                start=(cc == 0), stop=(cc == 7))
                        tq.inc(last)
                    mark_qkv = tq.n
                    if s == NSLAB - 1 and c4 == 3:
                        setF("wt_buf", tq)

                    @V
                    def _(vector, c4=c4):
                        FL.wait(vector, "v", tq, mark_qkv)
                        wf(vector, "v", "qkvf4")
                        vq.inc(vector.tensor_copy(qkvf4[:, c4 * 384:(c4 + 1) * 384],
                                                  psB[:, 0:384]))
                        FL.wait(vector, "v", vq, vq.n)
                        vq.inc(vector.tensor_mul(sqt[:],
                                                 qkvf4[:, c4 * 384:c4 * 384 + 256],
                                                 qkvf4[:, c4 * 384:c4 * 384 + 256]))
                        FL.wait(vector, "v", vq, vq.n)
                        wf(vector, "v", "rms16")
                        vq.inc(vector.reduce_sum(rms16[:, c4 * 4:(c4 + 1) * 4],
                                                 sqt.rearrange("p (g d) -> p g d", d=DH),
                                                 axis=AX.X))
                    free["psQ"] = (vq, vq.n)
                mark_rms = vq.n
                free[f"xslab{jsl}"] = (tq, tq.n)

                @A
                def _(scalar):
                    FL.wait(scalar, "a", vq, mark_rms)
                    aq.inc(scalar.activation(rstdL[:], rms16[:], AF.Ln,
                                             bias=epsRMS[:], scale=1.0 / DH))
                    FL.wait(scalar, "a", aq, aq.n)
                    aq.inc(scalar.activation(rstd16[:], rstdL[:], AF.Exp, scale=-0.5))
                free["rms16"] = (aq, aq.n)
                mark_rstd = aq.n

                for c4 in range(4):
                    i = s * 4 + c4
                    own = i < NOWN

                    @V
                    def _(vector, c4=c4, i=i):
                        FL.wait(vector, "v", aq, mark_rstd)
                        base = c4 * 384
                        rb_ = c4 * 4
                        for g_ in range(4):
                            vq.inc(vector.tensor_scalar(
                                tmn[:, g_ * 64:(g_ + 1) * 64],
                                qkvf4[:, base + g_ * 64: base + (g_ + 1) * 64],
                                rstd16[:, rb_ + g_: rb_ + g_ + 1], None, ALU.mult))
                        FL.wait(vector, "v", vq, vq.n)
                        vq.inc(vector.tensor_mul(kq_hat[:], tmn[:], wmul[:]))
                        wf(vector, "v", "vbuf")
                        dst = vbuf.rearrange("p (h k j) -> p h k j", h=2, j=65)[
                            :, :, i:i + 1, 0:64].rearrange("p h k j -> p (h k) j")
                        srcv = qkvf4[:, base + 256: base + 384].rearrange(
                            "p (h j) -> p h j", h=2)
                        vq.inc(vector.tensor_copy(dst, srcv))
                    mark_kqv = vq.n
                    if c4 == 3:
                        free["qkvf4"] = (vq, vq.n)

                    @T
                    def _(tensor, i=i, own=own):
                        FL.wait(tensor, "t", vq, mark_kqv)
                        wf(tensor, "t", "psT0")
                        wf(tensor, "t", "kT")
                        tq.inc(tensor.transpose(psT0[:], kq_hat[:, 0:128], identb[:]))
                        if own:
                            wf(tensor, "t", "psT1")
                            wf(tensor, "t", "qT")
                            tq.inc(tensor.transpose(psT1[:], kq_hat[:, 128:256],
                                                    identb[:]))
                    mark_tp = tq.n

                    @V
                    def _(vector, i=i, own=own):
                        FL.wait(vector, "v", tq, mark_tp)
                        vq.inc(vector.tensor_copy(kT[:, i * 128:(i + 1) * 128], psT0[:]))
                        free["psT0"] = (vq, vq.n)
                        if own:
                            vq.inc(vector.tensor_copy(qT[:, i * 128:(i + 1) * 128],
                                                      psT1[:]))
                            free["psT1"] = (vq, vq.n)
                        vq.inc(vector.tensor_mul(sqm[:], kq_hat[:], kq_hat[:]))
                        FL.wait(vector, "v", vq, vq.n)
                        vq.inc(vector.reduce_sum(m4[:], sqm.rearrange("p (g d) -> p g d", d=DH),
                                                 axis=AX.X))
                        FL.wait(vector, "v", vq, vq.n)
                        vq.inc(vector.tensor_copy(m4b[:], m4[:]))
                    mark_m4 = vq.n

                    @T
                    def _(tensor, i=i):
                        FL.wait(tensor, "t", vq, mark_m4)
                        wf(tensor, "t", "psT0")
                        tq.inc(tensor.transpose(psT0[0:4, 0:128], m4b[:], identb[:]))
                    mark_mt = tq.n

                    @V
                    def _(vector, i=i, own=own):
                        FL.wait(vector, "v", tq, mark_mt)
                        vq.inc(vector.reduce_max(mx4[:], psT0[0:4, 0:128], axis=AX.X))
                        free["psT0"] = (vq, vq.n)
                        FL.wait(vector, "v", vq, vq.n)
                        wf(vector, "v", "maxacc")
                        if own:
                            vq.inc(vector.tensor_max(maxacc[:], maxacc[:], mx4[:]))
                        else:
                            vq.inc(vector.tensor_max(maxacc[0:2, :], maxacc[0:2, :],
                                                     mx4[0:2, :]))
                        free["maxacc"] = (vq, vq.n)

            vq_all = vq.n

            @V
            def _(vector):
                FL.wait(vector, "v", vq, vq_all)
                vq.inc(vector.tensor_copy(maxaccb[:], maxacc[:]))
            mark_mab = vq.n

            @T
            def _(tensor):
                FL.wait(tensor, "t", vq, mark_mab)
                wf(tensor, "t", "psT0")
                tq.inc(tensor.transpose(psT0[0:1, 0:4], maxaccb[:], identb[0:4, 0:4]))
            mark_mat = tq.n

            @V
            def _(vector):
                FL.wait(vector, "v", tq, mark_mat)
                vq.inc(vector.tensor_copy(mrow[:], psT0[0:1, 0:4]))
                free["psT0"] = (vq, vq.n)
                FL.wait(vector, "v", vq, vq.n)
                vq.inc(vector.tensor_mul(mab[:], mrow[0:1, 0:2], mrow[0:1, 2:4]))
            mark_ab = vq.n

            @A
            def _(scalar):
                FL.wait(scalar, "a", vq, mark_ab)
                aq.inc(scalar.activation(mln[:], mab[:], AF.Ln, bias=epsTINY[:], scale=1.0))
                FL.wait(scalar, "a", aq, aq.n)
                aq.inc(scalar.activation(negm2[:], mln[:], AF.Exp, scale=0.5))
            mark_mh = aq.n

            @T
            def _(tensor):
                FL.wait(tensor, "t", aq, mark_mh)
                wf(tensor, "t", "psQ")
                tensor.matmul(psB[:, 384:385], onesf[:], negm2[0:1, 0:1], start=True, stop=True)
                tq.inc(tensor.matmul(psB[:, 385:386], onesf[:], negm2[0:1, 1:2],
                                     start=True, stop=True))
            mark_mq = tq.n

            @V
            def _(vector):
                FL.wait(vector, "v", tq, mark_mq)
                wf(vector, "v", "negMt")
                vq.inc(vector.tensor_scalar_mul(negMt[:], psB[:, 384:386], -1.0))
            free["psQ"] = (vq, vq.n)
            mark_negM = vq.n

            for qb in range(NQB if nstage >= 4 else 0):
                pv_mark_A = {}
                pv_mark_B = {}
                s_mark_A = {}
                s_mark_B = {}
                exp_mark_A = {}
                exp_mark_B = {}
                for g in range(NG + 1):
                    if g < NG:
                        par = g % 2

                        @T
                        def _(tensor, g=g, qb=qb):
                            FL.wait(tensor, "t", aq, last_expB)
                            FL.wait(tensor, "t", vq, mark_negM)
                            FL.wait(tensor, "t", vq, prev_bankfree[1])
                            kc0, kc1 = 2 * g, 2 * g + 1
                            tensor.matmul(psA[:, 0:512],
                                          kT[0:64, kc0 * 128:(kc0 + 1) * 128],
                                          qT[0:64, qb * 512:(qb + 1) * 512],
                                          start=True, stop=True)
                            tq.inc(tensor.matmul(psA[:, 512:1024],
                                                 kT[0:64, kc1 * 128:(kc1 + 1) * 128],
                                                 qT[0:64, qb * 512:(qb + 1) * 512],
                                                 start=True, stop=True))
                            tensor.matmul(psB[:, 0:512],
                                          kT[64:128, kc0 * 128:(kc0 + 1) * 128],
                                          qT[64:128, qb * 512:(qb + 1) * 512],
                                          start=True, stop=True, tile_position=(64, 0))
                            tq.inc(tensor.matmul(psB[:, 512:1024],
                                                 kT[64:128, kc1 * 128:(kc1 + 1) * 128],
                                                 qT[64:128, qb * 512:(qb + 1) * 512],
                                                 start=True, stop=True, tile_position=(64, 0)))
                        s_mark_A[g] = tq.n - 1
                        s_mark_B[g] = tq.n

                        @A
                        def _(scalar, g=g, qb=qb, par=par):
                            FL.wait(scalar, "a", vq, mark_negM)
                            FL.wait(scalar, "a", tq, s_mark_A[g])
                            FL.wait(scalar, "a", tq, pv_last[par])
                            aq.inc(scalar.activation(ptb[par][:], psA[:], AF.Exp,
                                                     bias=negMt[:, 0:1], scale=1.0))
                            FL.wait(scalar, "a", tq, s_mark_B[g])
                            FL.wait(scalar, "a", tq, pv_last[2 + par])
                            aq.inc(scalar.activation(ptb[2 + par][:], psB[:], AF.Exp,
                                                     bias=negMt[:, 1:2], scale=1.0))
                        exp_mark_A[g] = aq.n - 1
                        exp_mark_B[g] = aq.n
                        last_expA, last_expB = aq.n - 1, aq.n

                    if g >= 1:
                        gp_ = g - 1
                        par = gp_ % 2

                        @T
                        def _(tensor, g=gp_, qb=qb, par=par):
                            FL.wait(tensor, "t", aq, exp_mark_A[g])
                            FL.wait(tensor, "t", vq, prev_ofree[1])
                            kc0, kc1 = 2 * g, 2 * g + 1
                            tensor.matmul(psO0[:],
                                          vbuf[:, kc0 * 65: kc0 * 65 + 65],
                                          ptb[par][:, 0:512],
                                          start=(g == 0), stop=False)
                            tq.inc(tensor.matmul(psO0[:],
                                                 vbuf[:, kc1 * 65: kc1 * 65 + 65],
                                                 ptb[par][:, 512:1024],
                                                 start=False, stop=(g == NG - 1)))
                            FL.wait(tensor, "t", aq, exp_mark_B[g])
                            tensor.matmul(psO1[:],
                                          vbuf[:, (KC + kc0) * 65: (KC + kc0) * 65 + 65],
                                          ptb[2 + par][:, 0:512],
                                          start=(g == 0), stop=False)
                            tq.inc(tensor.matmul(psO1[:],
                                                 vbuf[:, (KC + kc1) * 65: (KC + kc1) * 65 + 65],
                                                 ptb[2 + par][:, 512:1024],
                                                 start=False, stop=(g == NG - 1)))
                        pv_mark_A[gp_] = tq.n - 1
                        pv_mark_B[gp_] = tq.n
                        pv_last[par] = tq.n - 1
                        pv_last[2 + par] = tq.n

                mark_pv_last = tq.n

                @V
                def _(vector, qb=qb):
                    FL.wait(vector, "v", tq, mark_pv_last)
                    vq.inc(vector.tensor_copy(d01[0:1, 0:512], psO0[64:65, :]))
                    vq.inc(vector.tensor_copy(d01[0:1, 512:1024], psO1[64:65, :]))
                mark_d = vq.n

                @A
                def _(scalar, qb=qb):
                    FL.wait(scalar, "a", vq, mark_d)
                    aq.inc(scalar.activation(ld01[:], d01[:], AF.Ln, scale=1.0))
                    FL.wait(scalar, "a", aq, aq.n)
                    aq.inc(scalar.activation(r01[:], ld01[:], AF.Exp, scale=-1.0))
                mark_r = aq.n

                @T
                def _(tensor, qb=qb):
                    FL.wait(tensor, "t", aq, mark_r)
                    tensor.matmul(psA[0:64, 0:512], onesf[0:1, 0:64], r01[0:1, 0:512],
                                  start=True, stop=True)
                    tq.inc(tensor.matmul(psB[0:64, 0:512], onesf[0:1, 0:64],
                                         r01[0:1, 512:1024], start=True, stop=True))
                mark_bc = tq.n

                @V
                def _(vector, qb=qb, hp=hp):
                    FL.wait(vector, "v", tq, mark_bc)
                    vq.inc(vector.tensor_copy(rb0[:], psA[0:64, 0:512]))
                    vq.inc(vector.tensor_copy(rb1[:], psB[0:64, 0:512]))
                    FL.wait(vector, "v", vq, vq.n)
                    jo = qb % 2
                    wf(vector, "v", f"obuf{jo}")
                    vq.inc(vector.tensor_mul(obuf[jo][0:64, :], psO0[0:64, :], rb0[:]))
                    vq.inc(vector.tensor_mul(obuf[jo][64:128, :], psO1[0:64, :], rb1[:]))
                prev_bankfree = (vq, vq.n)
                prev_ofree = (vq, vq.n)
                mark_ob = vq.n

                @GP
                def _(gp, qb=qb, hp=hp):
                    FL.wait(gp, "g", vq, mark_ob)
                    jo = qb % 2
                    c_sto[jo].inc(gp.dma_start(oT_h.ap()[hp, :, qb * 512:(qb + 1) * 512],
                                               obuf[jo][:]))
                free[f"obuf{qb % 2}"] = (c_sto[qb % 2], c_sto[qb % 2].n)

            setF("kT", tq)
            setF("qT", tq)
            setF("vbuf", tq)
            free["negMt"] = (aq, aq.n)

        sto_totals = [c_sto[0].n, c_sto[1].n]

        for r in range(8 if DBG.get("s5_prep", True) else 0):
            load_wT(projT, proj_w, r * 128, C, r * 128, dst_free_key="wbT")
        mark_pj = vq.n

        for i in range(NOWN if nstage >= 5 else 0):
            j = i % 2

            @SYNC
            def _(sync, i=i, j=j):
                FL.wait(sync, "s", c_sto[0], sto_totals[0])
                FL.wait(sync, "s", c_sto[1], sto_totals[1])
                FL.wait(sync, "s", c_stg[0], stg_silu_totals[0])
                FL.wait(sync, "s", c_stg[1], stg_silu_totals[1])
                wf(sync, "s", f"ostage{j}")
                for hp in range(HP):
                    c_ld5[j].inc(sync.dma_start(ostage[j][:, hp * 128:(hp + 1) * 128],
                                                oT_h.ap()[hp, :, i * 128:(i + 1) * 128]))
                wf(sync, "s", f"gld{j}")
                c_ld5[j].inc(sync.dma_start(gld[j][:], g_h.ap()[i * 128:(i + 1) * 128, :]))
                wf(sync, "s", f"xt{j}")
                c_ld5[j].inc(sync.dma_start(xld[j][:], xb.ap()[i * 128:(i + 1) * 128, :]))
            mark_ld = c_ld5[j].n

            @T
            def _(tensor, i=i, j=j):
                FL.wait(tensor, "t", c_ld5[j], mark_ld)
                FL.wait(tensor, "t", vq, mark_pj)
                FL.wait(tensor, "t", vq, free["psA_s5"][1] if "psA_s5" in free else 0)
                last = None
                for half in range(2):
                    for cc in range(8):
                        last = tensor.matmul(
                            psA[:, half * 512:(half + 1) * 512],
                            ostage[j][:, cc * 128:(cc + 1) * 128],
                            projT[:, cc * C + half * 512: cc * C + half * 512 + 512],
                            start=(cc == 0), stop=(cc == 7))
                tq.inc(last)
            free[f"ostage{j}"] = (tq, tq.n)
            mark_proj = tq.n

            @V
            def _(vector, i=i, j=j):
                FL.wait(vector, "v", tq, mark_proj)
                vq.inc(vector.tensor_add(s1t[:], psA[:], projbB[:]))
                FL.wait(vector, "v", vq, vq.n)
                vq.inc(vector.tensor_mul(s2t[:], s1t[:], gld[j][:]))
                FL.wait(vector, "v", vq, vq.n)
                wf(vector, "v", f"outt{j}")
                vq.inc(vector.tensor_add(outt[j][:], s2t[:], xld[j][:]))
            free["psA_s5"] = (vq, vq.n)
            free[f"gld{j}"] = (vq, vq.n)
            free[f"xt{j}"] = (vq, vq.n)
            mark_out = vq.n

            @GP
            def _(gp, i=i, j=j):
                FL.wait(gp, "g", vq, mark_out)
                c_stu[j].inc(gp.dma_start(out.ap()[i * 128:(i + 1) * 128, :], outt[j][:]))
            free[f"outt{j}"] = (c_stu[j], c_stu[j].n)

        stu_totals = [c_stu[0].n, c_stu[1].n]

        @GP
        def _(gp):
            gp.wait_ge(c_stu[0].sem, stu_totals[0])
            gp.wait_ge(c_stu[1].sem, stu_totals[1])

        @SYNC
        def _(sync):
            sync.wait_ge(c_stu[0].sem, stu_totals[0])
            sync.wait_ge(c_stu[1].sem, stu_totals[1])

    return nc



def _shard_inputs(L0_attn_x, L0_attn_qkv_w, L0_attn_proj_w, L0_attn_proj_b,
                  L0_attn_q_norm_w, L0_attn_k_norm_w, L0_ffn_CB, L0_ffn_SCB,
                  L0_ffn_bias):
    x = np.ascontiguousarray(np.asarray(L0_attn_x, dtype=np.float32))
    B, N, Cc = x.shape
    common = {
        "qkv_w": np.ascontiguousarray(np.asarray(L0_attn_qkv_w, np.float32)),
        "proj_w": np.ascontiguousarray(np.asarray(L0_attn_proj_w, np.float32)),
        "proj_b": np.asarray(L0_attn_proj_b, np.float32).reshape(1, Cc),
        "qnw": np.asarray(L0_attn_q_norm_w, np.float32).reshape(1, DH),
        "knw": np.asarray(L0_attn_k_norm_w, np.float32).reshape(1, DH),
        "cbw": np.ascontiguousarray(np.asarray(L0_ffn_CB, np.float32)),
        "scb": np.asarray(L0_ffn_SCB, np.float32).reshape(1, Cc),
        "fbias": np.asarray(L0_ffn_bias, np.float32).reshape(1, Cc),
        "ident_in": np.eye(128, dtype=np.float32).astype(bfdt),
    }
    in_maps = []
    half = N // 2
    for c in range(8):
        b, h = c // 2, c % 2
        own = x[b, h * half:(h + 1) * half]
        other = x[b, (1 - h) * half:(2 - h) * half]
        m = dict(common)
        m["xb"] = np.ascontiguousarray(np.concatenate([own, other], axis=0))
        in_maps.append(m)
    return in_maps, (B, N, Cc)


_CACHE = {}


def kernel(**inputs):
    in_maps, (B, N, Cc) = _shard_inputs(**inputs)
    key = (N,)
    if key not in _CACHE:
        _CACHE[key] = build(T_ALL=N, T_OWN=N // 2)
    nc = _CACHE[key]
    res = run_bass_kernel_spmd(nc, in_maps, core_ids=list(range(8)))
    half = N // 2
    outp = np.empty((B, N, Cc), dtype=np.float32)
    for c in range(8):
        b, h = c // 2, c % 2
        outp[b, h * half:(h + 1) * half] = res.results[c]["out"]
    return outp


# version 02
# speedup vs baseline: 1.2388x; 1.2388x over previous
import numpy as np
import ml_dtypes
from contextlib import ExitStack

import concourse.bass as bass
import concourse.mybir as mybir
from concourse.bass_utils import run_bass_kernel_spmd

F32 = mybir.dt.float32
BF16 = mybir.dt.bfloat16
AF = mybir.ActivationFunctionType
ALU = mybir.AluOpType
AX = mybir.AxisListType
bfdt = ml_dtypes.bfloat16

C = 1024
DH = 64
HP = 8
DBG = {}


class Chain:
    def __init__(self, sem, step=1):
        self.sem = sem
        self.n = 0
        self.step = step

    def inc(self, inst, k=None):
        k = self.step if k is None else k
        inst.then_inc(self.sem, k)
        self.n += k
        return self.n

    def reserve(self, k=1):
        self.n += k
        return self.n

    def inc_pre(self, inst, k=1):
        inst.then_inc(self.sem, k)
        return inst


class Floors:
    def __init__(self):
        self.f = {}

    def wait(self, eng, key, chain, n):
        if n <= 0:
            return
        d = self.f.setdefault(key, {})
        if n > d.get(id(chain), 0):
            assert n <= chain.n, f"wait {n} > emitted {chain.n} ({key})"
            eng.wait_ge(chain.sem, n)
            d[id(chain)] = n


def build(T_ALL=8192, T_OWN=4096, nstage=99):
    KC = T_ALL // 128
    NSLAB = T_ALL // 512
    NQB = T_OWN // 512
    NOWN = T_OWN // 128
    NG = KC // 2

    nc = bass.Bass("TRN2", target_bir_lowering=False)

    xb = nc.dram_tensor("xb", [T_ALL, C], F32, kind="ExternalInput")
    qkv_w = nc.dram_tensor("qkv_w", [3 * C, C], F32, kind="ExternalInput")
    proj_w = nc.dram_tensor("proj_w", [C, C], F32, kind="ExternalInput")
    proj_b = nc.dram_tensor("proj_b", [1, C], F32, kind="ExternalInput")
    qnw = nc.dram_tensor("qnw", [1, DH], F32, kind="ExternalInput")
    knw = nc.dram_tensor("knw", [1, DH], F32, kind="ExternalInput")
    cbw = nc.dram_tensor("cbw", [C, C], F32, kind="ExternalInput")
    scb = nc.dram_tensor("scb", [1, C], F32, kind="ExternalInput")
    fbias = nc.dram_tensor("fbias", [1, C], F32, kind="ExternalInput")
    ident_in = nc.dram_tensor("ident_in", [128, 128], BF16, kind="ExternalInput")

    out = nc.dram_tensor("out", [T_OWN, C], F32, kind="ExternalOutput")

    xnT_h = nc.dram_tensor("xnT_h", [8, 128, T_ALL], BF16, kind="Internal")
    g_h = nc.dram_tensor("g_h", [T_OWN, C], F32, kind="Internal")
    oT_h = nc.dram_tensor("oT_h", [HP, 128, T_OWN], BF16, kind="Internal")

    with ExitStack() as ctx:
        sb = lambda n_, s_, d_: ctx.enter_context(nc.sbuf_tensor(n_, s_, d_))
        pt = lambda n_, s_, d_: ctx.enter_context(nc.psum_tensor(n_, s_, d_))
        mksem = lambda n_: ctx.enter_context(nc.semaphore(n_))

        identb = sb("identb", [128, 128], BF16)
        onesf = sb("onesf", [1, 128], F32)
        ones8 = sb("ones8", [1, 128], F32)
        ones127 = sb("ones127", [1, 128], F32)
        epsLN = sb("epsLN", [128, 1], F32)
        epsRMS = sb("epsRMS", [128, 1], F32)
        epsTINY = sb("epsTINY", [1, 1], F32)
        wrow = sb("wrow", [1, 256], F32)
        srow = sb("srow", [1, C], F32)
        frow = sb("frow", [1, C], F32)
        prow = sb("prow", [1, C], F32)
        wmul = sb("wmul", [128, 256], F32)
        scbB = sb("scbB", [128, C], F32)
        fbiasB = sb("fbiasB", [128, C], F32)
        projbB = sb("projbB", [128, C], F32)

        xt = [sb(f"xt{j}", [128, C], F32) for j in range(2)]
        lnsum = sb("lnsum", [128, 1], F32)

        xss = sb("xss", [128, 1], F32)
        lnmean = sb("lnmean", [128, 1], F32)
        lnex2 = sb("lnex2", [128, 1], F32)
        lnm2 = sb("lnm2", [128, 1], F32)
        lnvar = sb("lnvar", [128, 1], F32)
        lnl = sb("lnl", [128, 1], F32)
        lnr = sb("lnr", [128, 1], F32)
        xn_tok = [sb(f"xn_tok{j}", [128, C], BF16) for j in range(2)]
        xnT_st = [sb(f"xnT_st{j}", [128, C], BF16) for j in range(2)]

        wstage = sb("wstage", [128, C], F32)
        wstage_b = sb("wstage_b", [128, C], BF16)
        cbT = sb("cbT", [128, 8 * C], BF16)
        projT = cbT
        wt_buf = sb("wt_buf", [128, 8 * 384], BF16)


        gtile = [sb(f"gtile{j}", [128, C], F32) for j in range(2)]
        gld = [sb(f"gld{j}", [128, C], F32) for j in range(2)]

        xslab = [sb(f"xslab{j}", [128, 8 * 512], BF16) for j in range(2)]
        kT = sb("kT", [128, T_ALL], BF16)
        qT = sb("qT", [128, T_OWN], BF16)
        vbuf = sb("vbuf", [128, 2 * KC * 65], BF16)
        qkvf4 = sb("qkvf4", [128, 4 * 384], F32)
        sqt = sb("sqt", [128, 256], F32)
        rms16 = sb("rms16", [128, 16], F32)
        rstdL = sb("rstdL", [128, 16], F32)
        rstd16 = sb("rstd16", [128, 16], F32)
        tmn = sb("tmn", [128, 256], F32)
        kq_hat = sb("kq_hat", [128, 256], BF16)
        sqm = sb("sqm", [128, 256], BF16)
        m4 = sb("m4", [128, 4], F32)
        m4b = sb("m4b", [128, 4], BF16)
        mx4 = sb("mx4", [4, 1], F32)
        maxacc = sb("maxacc", [4, 1], F32)
        maxaccb = sb("maxaccb", [4, 1], BF16)
        mrow = sb("mrow", [1, 4], F32)
        mab = sb("mab", [1, 2], F32)
        mln = sb("mln", [1, 2], F32)
        negm2 = sb("negm2", [1, 2], F32)
        negMt = sb("negMt", [128, 2], F32)

        ptb = [sb(f"ptb{j}", [128, 1024], BF16) for j in range(4)]
        d01 = sb("d01", [1, 1024], F32)
        ld01 = sb("ld01", [1, 1024], F32)
        r01 = sb("r01", [1, 1024], F32)
        rb0 = sb("rb0", [64, 512], F32)
        rb1 = sb("rb1", [64, 512], F32)
        obuf = [sb(f"obuf{j}", [128, 512], BF16) for j in range(2)]

        ostage = [sb(f"ostage{j}", [128, 8 * 128], BF16) for j in range(2)]
        xld = xt
        s1t = sb("s1t", [128, C], F32)
        s2t = sb("s2t", [128, C], F32)
        outt = [sb(f"outt{j}", [128, C], F32) for j in range(2)]
        xsq = s1t
        fo = s2t
        fo2 = outt[0]

        psA = pt("psA", [128, 1024], F32)
        psB = pt("psB", [128, 1024], F32)
        psO0 = pt("psO0", [65, 512], F32)
        psO1 = pt("psO1", [65, 512], F32)
        psT0 = pt("psT0", [128, 128], BF16)
        psT1 = pt("psT1", [128, 128], BF16)
        psTs = (psT0, psT1)

        c_ldc = Chain(mksem("ldc"), 16)
        c_ldw = Chain(mksem("ldw"), 16)
        c_ldx = [Chain(mksem(f"ldx{j}"), 16) for j in range(2)]
        c_lds = [Chain(mksem(f"lds{j}"), 16) for j in range(2)]
        c_ldg = [Chain(mksem(f"ldg{j}"), 16) for j in range(2)]
        c_ld5 = [Chain(mksem(f"ld5{j}"), 16) for j in range(2)]
        c_stx = [Chain(mksem(f"stx{j}"), 16) for j in range(2)]
        c_stg = [Chain(mksem(f"stg{j}"), 16) for j in range(2)]
        c_sto = [Chain(mksem(f"sto{j}"), 16) for j in range(2)]
        c_stu = [Chain(mksem(f"stu{j}"), 16) for j in range(2)]
        tq = Chain(mksem("tq"))
        vq = Chain(mksem("vq"))
        aq = Chain(mksem("aq"))

        FL = Floors()
        block = ctx.enter_context(nc.Block())
        SYNC, GP, T, V, A = block.sync, block.gpsimd, block.tensor, block.vector, block.scalar

        free = {
            "xt0": (vq, 0), "xt1": (vq, 0),
            "xn0": (tq, 0), "xn1": (tq, 0),
            "xnTst0_dma": (c_stx[0], 0), "xnTst1_dma": (c_stx[1], 0),
            "xnTst0_ffn": (tq, 0), "xnTst1_ffn": (tq, 0),
            "psB_v": (vq, 0),
            "gtile0": (c_stg[0], 0), "gtile1": (c_stg[1], 0),
            "wstage": (vq, 0), "wstage_b": (tq, 0),
            "psT0": (vq, 0), "psT1": (vq, 0),
            "wt_buf": (tq, 0), "wbT": (tq, 0),
            "xslab0": (tq, 0), "xslab1": (tq, 0),
            "psQ": (vq, 0),
            "kT": (tq, 0), "qT": (tq, 0), "vbuf": (tq, 0),
            "negMt": (aq, 0),
            "maxacc": (vq, 0),
            "gld0": (aq, 0), "gld1": (aq, 0),
            "obuf0": (c_sto[0], 0), "obuf1": (c_sto[1], 0),
            "ostage0": (tq, 0), "ostage1": (tq, 0),

            "outt0": (c_stu[0], 0), "outt1": (c_stu[1], 0),
        }

        def FR(name):
            return free[name]

        def setF(name, chain):
            free[name] = (chain, chain.n)

        def wf(eng, key, name):
            ch, n = free[name]
            FL.wait(eng, key, ch, n)

        @SYNC
        def _(sync):
            c_ldc.inc(sync.dma_start(identb[:], ident_in.ap()))
            c_ldc.inc(sync.dma_start(wrow[0:1, 0:64], knw.ap()))
            c_ldc.inc(sync.dma_start(wrow[0:1, 64:128], knw.ap()))
            c_ldc.inc(sync.dma_start(wrow[0:1, 128:192], qnw.ap()))
            c_ldc.inc(sync.dma_start(wrow[0:1, 192:256], qnw.ap()))
            c_ldc.inc(sync.dma_start(srow[:], scb.ap()))
            c_ldc.inc(sync.dma_start(frow[:], fbias.ap()))
            c_ldc.inc(sync.dma_start(prow[:], proj_b.ap()))
        ld_consts = c_ldc.n

        @V
        def _(vector):
            vq.inc(vector.memset(onesf[:], 1.0))
            vq.inc(vector.memset(ones8[:], 0.125))
            vq.inc(vector.memset(ones127[:], 1.0 / 127.0))
            vq.inc(vector.memset(epsLN[:], 1e-5))
            vq.inc(vector.memset(epsRMS[:], 1e-6))
            vq.inc(vector.memset(epsTINY[:], 1e-30))
        v_consts = vq.n

        if DBG.get("s0_bc", True):
            @T
            def _(tensor):
                FL.wait(tensor, "t", c_ldc, ld_consts)
                FL.wait(tensor, "t", vq, v_consts)
                tensor.matmul(psB[:, 0:128], onesf[:], wrow[0:1, 0:128], start=True, stop=True)
                tq.inc(tensor.matmul(psB[:, 128:256], ones8[:], wrow[0:1, 128:256],
                                     start=True, stop=True))
                tensor.matmul(psA[:, 0:512], ones127[:], srow[0:1, 0:512], start=True, stop=True)
                tq.inc(tensor.matmul(psA[:, 512:1024], ones127[:], srow[0:1, 512:1024],
                                     start=True, stop=True))
            t_bc1 = tq.n

            @V
            def _(vector):
                FL.wait(vector, "v", tq, t_bc1)
                vq.inc(vector.tensor_copy(wmul[:], psB[:, 0:256]))
                vq.inc(vector.tensor_copy(scbB[:], psA[:]))
            v_bc1 = vq.n

            @T
            def _(tensor):
                FL.wait(tensor, "t", vq, v_bc1)
                tensor.matmul(psB[:, 0:512], onesf[:], frow[0:1, 0:512], start=True, stop=True)
                tensor.matmul(psB[:, 512:1024], onesf[:], frow[0:1, 512:1024],
                              start=True, stop=True)
                tensor.matmul(psA[:, 0:512], onesf[:], prow[0:1, 0:512], start=True, stop=True)
                tq.inc(tensor.matmul(psA[:, 512:1024], onesf[:], prow[0:1, 512:1024],
                                     start=True, stop=True))
            t_bc2 = tq.n

            @V
            def _(vector):
                FL.wait(vector, "v", tq, t_bc2)
                vq.inc(vector.tensor_copy(fbiasB[:], psB[:]))
                vq.inc(vector.tensor_copy(projbB[:], psA[:]))

        def load_wT(dst_sb, src_dram, rows0, dst_cc_stride, dst_off, dst_free_key=None):
            @SYNC
            def _(sync):
                wf(sync, "s", "wstage")
                c_ldw.inc(sync.dma_start(wstage[:], src_dram.ap()[rows0:rows0 + 128, :]))
            mark_ld = c_ldw.n

            @V
            def _(vector):
                FL.wait(vector, "v", c_ldw, mark_ld)
                wf(vector, "v", "wstage_b")
                vq.inc(vector.tensor_copy(wstage_b[:], wstage[:]))
            setF("wstage", vq)
            mark_cast = vq.n

            for cc in range(8):
                slot = cc % 2

                @T
                def _(tensor, cc=cc, slot=slot):
                    FL.wait(tensor, "t", vq, mark_cast)
                    wf(tensor, "t", f"psT{slot}")
                    if dst_free_key:
                        wf(tensor, "t", dst_free_key)
                    tq.inc(tensor.transpose(psTs[slot][:],
                                            wstage_b[:, cc * 128:(cc + 1) * 128],
                                            identb[:]))
                mark_t = tq.n
                if cc == 7:
                    setF("wstage_b", tq)

                @V
                def _(vector, cc=cc, slot=slot):
                    FL.wait(vector, "v", tq, mark_t)
                    vq.inc(vector.tensor_copy(
                        dst_sb[:, cc * dst_cc_stride + dst_off:
                               cc * dst_cc_stride + dst_off + 128],
                        psTs[slot][:]))
                free[f"psT{slot}"] = (vq, vq.n)
            return vq.n

        cb_ready = 0
        for r in range(8 if DBG.get("s0_cbt", True) else 0):
            cb_ready = load_wT(cbT, cbw, r * 128, C, r * 128)

        for i in range(min(KC if nstage >= 1 else 0, DBG.get("s1_chunks", 10**9))):
            j = i % 2
            own = i < NOWN

            @SYNC
            def _(sync, i=i, j=j):
                wf(sync, "s", f"xt{j}")
                c_ldx[j].inc(sync.dma_start(xt[j][:], xb.ap()[i * 128:(i + 1) * 128, :]))
            mark_x = c_ldx[j].n

            @V
            def _(vector, j=j):
                FL.wait(vector, "v", c_ldx[j], mark_x)
                vq.inc(vector.reduce_sum(lnsum[:], xt[j][:], axis=AX.X))
                vq.inc(vector.tensor_mul(xsq[:], xt[j][:], xt[j][:]))
                FL.wait(vector, "v", vq, vq.n)
                vq.inc(vector.reduce_sum(xss[:], xsq[:], axis=AX.X))
                vq.inc(vector.tensor_scalar_mul(lnmean[:], lnsum[:], 1.0 / C))
                FL.wait(vector, "v", vq, vq.n)
                vq.inc(vector.tensor_scalar_mul(lnex2[:], xss[:], 1.0 / C))
                vq.inc(vector.tensor_mul(lnm2[:], lnmean[:], lnmean[:]))
                FL.wait(vector, "v", vq, vq.n)
                vq.inc(vector.tensor_sub(lnvar[:], lnex2[:], lnm2[:]))
            mark_var = vq.n

            @A
            def _(scalar):
                FL.wait(scalar, "a", vq, mark_var)
                aq.inc(scalar.activation(lnl[:], lnvar[:], AF.Ln, bias=epsLN[:], scale=1.0))
                FL.wait(scalar, "a", aq, aq.n)
                aq.inc(scalar.activation(lnr[:], lnl[:], AF.Exp, scale=-0.5))
            mark_rstd = aq.n

            @V
            def _(vector, j=j):
                FL.wait(vector, "v", aq, mark_rstd)
                wf(vector, "v", f"xn{j}")
                vq.inc(vector.tensor_scalar(xn_tok[j][:], xt[j][:], lnmean[:], lnr[:],
                                            ALU.subtract, ALU.mult))
            mark_xn = vq.n
            free[f"xt{j}"] = (vq, vq.n)

            for cc in range(8 if DBG.get("s1_tp", True) else 0):
                slot = cc % 2

                @T
                def _(tensor, cc=cc, j=j, slot=slot):
                    FL.wait(tensor, "t", vq, mark_xn)
                    wf(tensor, "t", f"psT{slot}")
                    wf(tensor, "t", f"xnTst{j}_dma")
                    wf(tensor, "t", f"xnTst{j}_ffn")
                    tq.inc(tensor.transpose(psTs[slot][:],
                                            xn_tok[j][:, cc * 128:(cc + 1) * 128],
                                            identb[:]))
                mark_t = tq.n
                if cc == 7:
                    free[f"xn{j}"] = (tq, tq.n)

                @V
                def _(vector, cc=cc, j=j, slot=slot):
                    FL.wait(vector, "v", tq, mark_t)
                    vq.inc(vector.tensor_copy(xnT_st[j][:, cc * 128:(cc + 1) * 128],
                                              psTs[slot][:]))
                free[f"psT{slot}"] = (vq, vq.n)
            mark_xnT = vq.n

            @GP
            def _(gp, i=i, j=j):
                if not DBG.get("s1_dma", True):
                    return
                FL.wait(gp, "g", vq, mark_xnT)
                c_stx[j].inc(gp.dma_start(
                    xnT_h.ap()[:, :, i * 128:(i + 1) * 128].rearrange("c p t -> p c t"),
                    xnT_st[j].rearrange("p (c t) -> p c t", t=128)))
            free[f"xnTst{j}_dma"] = (c_stx[j], c_stx[j].n)

            if own and DBG.get("s1_ffn", True):
                @T
                def _(tensor, j=j):
                    FL.wait(tensor, "t", vq, mark_xnT)
                    FL.wait(tensor, "t", vq, cb_ready)
                    wf(tensor, "t", "psB_v")
                    last = None
                    for half in range(2):
                        for cc in range(8):
                            last = tensor.matmul(
                                psB[:, half * 512:(half + 1) * 512],
                                xnT_st[j][:, cc * 128:(cc + 1) * 128],
                                cbT[:, cc * C + half * 512: cc * C + half * 512 + 512],
                                start=(cc == 0), stop=(cc == 7))
                    tq.inc(last)
                mark_ffn = tq.n
                free[f"xnTst{j}_ffn"] = (tq, tq.n)

                @V
                def _(vector, j=j):
                    FL.wait(vector, "v", tq, mark_ffn)
                    vq.inc(vector.tensor_mul(fo[:], psB[:], scbB[:]))
                    FL.wait(vector, "v", vq, vq.n)
                    vq.inc(vector.tensor_add(fo2[:], fo[:], fbiasB[:]))
                    FL.wait(vector, "v", vq, vq.n)
                    wf(vector, "v", f"gtile{j}")
                    vq.inc(vector.tensor_copy(gtile[j][:], fo2[:]))
                free["psB_v"] = (vq, vq.n)
                mark_g = vq.n

                @GP
                def _(gp, i=i, j=j):
                    FL.wait(gp, "g", vq, mark_g)
                    c_stg[j].inc(gp.dma_start(g_h.ap()[i * 128:(i + 1) * 128, :], gtile[j][:]))
                free[f"gtile{j}"] = (c_stg[j], c_stg[j].n)

        stg_totals = [c_stg[0].n, c_stg[1].n]
        stx_totals = [c_stx[0].n, c_stx[1].n]
        free["wbT"] = (tq, tq.n)
        free["outt0"] = (vq, vq.n)
        free["xt0"] = (vq, vq.n)
        free["xt1"] = (vq, vq.n)

        for i in range(NOWN if nstage >= 2 else 0):
            j = i % 2

            @SYNC
            def _(sync, i=i, j=j):
                FL.wait(sync, "s", c_stg[0], stg_totals[0])
                FL.wait(sync, "s", c_stg[1], stg_totals[1])
                wf(sync, "s", f"gld{j}")
                c_ldg[j].inc(sync.dma_start(gld[j][:], g_h.ap()[i * 128:(i + 1) * 128, :]))
            mark_g = c_ldg[j].n

            @A
            def _(scalar, j=j):
                FL.wait(scalar, "a", c_ldg[j], mark_g)
                wf(scalar, "a", f"gtile{j}")
                aq.inc(scalar.activation(gtile[j][:], gld[j][:], AF.Silu, scale=1.0))
            free[f"gld{j}"] = (aq, aq.n)
            mark_s = aq.n

            @GP
            def _(gp, i=i, j=j):
                FL.wait(gp, "g", aq, mark_s)
                c_stg[j].inc(gp.dma_start(g_h.ap()[i * 128:(i + 1) * 128, :], gtile[j][:]))
            free[f"gtile{j}"] = (c_stg[j], c_stg[j].n)

        stg_silu_totals = [c_stg[0].n, c_stg[1].n]

        free["rms16"] = (aq, 0)
        free["qkvf4"] = (vq, 0)
        prev_ofree = (vq, 0)
        prev_bankfree = (vq, 0)
        last_expA = 0
        last_expB = 0
        pv_last = {0: 0, 1: 0, 2: 0, 3: 0}

        for hp in range(HP if nstage >= 3 else 0):
            aq_hpstart = aq.n
            for gidx, rows0 in enumerate((C + hp * 128, hp * 128, 2 * C + hp * 128)):
                load_wT(wt_buf, qkv_w, rows0, 384, gidx * 128, dst_free_key="wt_buf")
            mark_wt = vq.n

            @V
            def _(vector):
                wf(vector, "v", "maxacc")
                vq.inc(vector.memset(maxacc[:], 0.0))
                wf(vector, "v", "vbuf")
                vq.inc(vector.memset(
                    vbuf.rearrange("p (c j) -> p c j", j=65)[:, :, 64:65], 1.0))
            mark_vones = vq.n

            for s in range(NSLAB):
                jsl = s % 2

                @SYNC
                def _(sync, s=s, jsl=jsl):
                    wf(sync, "s", f"xslab{jsl}")
                    FL.wait(sync, "s", c_stx[0], stx_totals[0])
                    FL.wait(sync, "s", c_stx[1], stx_totals[1])
                    for cc in range(8):
                        c_lds[jsl].inc(sync.dma_start(
                            xslab[jsl][:, cc * 512:(cc + 1) * 512],
                            xnT_h.ap()[cc, :, s * 512:(s + 1) * 512]))
                mark_slab = c_lds[jsl].n

                for c4 in range(4):
                    i = s * 4 + c4

                    @T
                    def _(tensor, c4=c4, jsl=jsl):
                        FL.wait(tensor, "t", c_lds[jsl], mark_slab)
                        FL.wait(tensor, "t", vq, mark_wt)
                        wf(tensor, "t", "psQ")
                        FL.wait(tensor, "t", aq, aq_hpstart)
                        last = None
                        for cc in range(8):
                            last = tensor.matmul(
                                psB[:, 0:384],
                                xslab[jsl][:, cc * 512 + c4 * 128: cc * 512 + c4 * 128 + 128],
                                wt_buf[:, cc * 384: cc * 384 + 384],
                                start=(cc == 0), stop=(cc == 7))
                        tq.inc(last)
                    mark_qkv = tq.n
                    if s == NSLAB - 1 and c4 == 3:
                        setF("wt_buf", tq)

                    @V
                    def _(vector, c4=c4):
                        FL.wait(vector, "v", tq, mark_qkv)
                        wf(vector, "v", "qkvf4")
                        vq.inc(vector.tensor_copy(qkvf4[:, c4 * 384:(c4 + 1) * 384],
                                                  psB[:, 0:384]))
                        FL.wait(vector, "v", vq, vq.n)
                        vq.inc(vector.tensor_mul(sqt[:],
                                                 qkvf4[:, c4 * 384:c4 * 384 + 256],
                                                 qkvf4[:, c4 * 384:c4 * 384 + 256]))
                        FL.wait(vector, "v", vq, vq.n)
                        wf(vector, "v", "rms16")
                        vq.inc(vector.reduce_sum(rms16[:, c4 * 4:(c4 + 1) * 4],
                                                 sqt.rearrange("p (g d) -> p g d", d=DH),
                                                 axis=AX.X))
                    free["psQ"] = (vq, vq.n)
                mark_rms = vq.n
                free[f"xslab{jsl}"] = (tq, tq.n)

                @A
                def _(scalar):
                    FL.wait(scalar, "a", vq, mark_rms)
                    aq.inc(scalar.activation(rstdL[:], rms16[:], AF.Ln,
                                             bias=epsRMS[:], scale=1.0 / DH))
                    FL.wait(scalar, "a", aq, aq.n)
                    aq.inc(scalar.activation(rstd16[:], rstdL[:], AF.Exp, scale=-0.5))
                free["rms16"] = (aq, aq.n)
                mark_rstd = aq.n

                for c4 in range(4):
                    i = s * 4 + c4
                    own = i < NOWN

                    @V
                    def _(vector, c4=c4, i=i):
                        FL.wait(vector, "v", aq, mark_rstd)
                        base = c4 * 384
                        rb_ = c4 * 4
                        for g_ in range(4):
                            vq.inc(vector.tensor_scalar(
                                tmn[:, g_ * 64:(g_ + 1) * 64],
                                qkvf4[:, base + g_ * 64: base + (g_ + 1) * 64],
                                rstd16[:, rb_ + g_: rb_ + g_ + 1], None, ALU.mult))
                        FL.wait(vector, "v", vq, vq.n)
                        vq.inc(vector.tensor_mul(kq_hat[:], tmn[:], wmul[:]))
                        wf(vector, "v", "vbuf")
                        dst = vbuf.rearrange("p (h k j) -> p h k j", h=2, j=65)[
                            :, :, i:i + 1, 0:64].rearrange("p h k j -> p (h k) j")
                        srcv = qkvf4[:, base + 256: base + 384].rearrange(
                            "p (h j) -> p h j", h=2)
                        vq.inc(vector.tensor_copy(dst, srcv))
                    mark_kqv = vq.n
                    if c4 == 3:
                        free["qkvf4"] = (vq, vq.n)

                    @T
                    def _(tensor, i=i, own=own):
                        FL.wait(tensor, "t", vq, mark_kqv)
                        wf(tensor, "t", "psT0")
                        wf(tensor, "t", "kT")
                        tq.inc(tensor.transpose(psT0[:], kq_hat[:, 0:128], identb[:]))
                        if own:
                            wf(tensor, "t", "psT1")
                            wf(tensor, "t", "qT")
                            tq.inc(tensor.transpose(psT1[:], kq_hat[:, 128:256],
                                                    identb[:]))
                    mark_tp = tq.n

                    @V
                    def _(vector, i=i, own=own):
                        FL.wait(vector, "v", tq, mark_tp)
                        vq.inc(vector.tensor_copy(kT[:, i * 128:(i + 1) * 128], psT0[:]))
                        free["psT0"] = (vq, vq.n)
                        if own:
                            vq.inc(vector.tensor_copy(qT[:, i * 128:(i + 1) * 128],
                                                      psT1[:]))
                            free["psT1"] = (vq, vq.n)
                        vq.inc(vector.tensor_mul(sqm[:], kq_hat[:], kq_hat[:]))
                        FL.wait(vector, "v", vq, vq.n)
                        vq.inc(vector.reduce_sum(m4[:], sqm.rearrange("p (g d) -> p g d", d=DH),
                                                 axis=AX.X))
                        FL.wait(vector, "v", vq, vq.n)
                        vq.inc(vector.tensor_copy(m4b[:], m4[:]))
                    mark_m4 = vq.n

                    @T
                    def _(tensor, i=i):
                        FL.wait(tensor, "t", vq, mark_m4)
                        wf(tensor, "t", "psT0")
                        tq.inc(tensor.transpose(psT0[0:4, 0:128], m4b[:], identb[:]))
                    mark_mt = tq.n

                    @V
                    def _(vector, i=i, own=own):
                        FL.wait(vector, "v", tq, mark_mt)
                        vq.inc(vector.reduce_max(mx4[:], psT0[0:4, 0:128], axis=AX.X))
                        free["psT0"] = (vq, vq.n)
                        FL.wait(vector, "v", vq, vq.n)
                        wf(vector, "v", "maxacc")
                        if own:
                            vq.inc(vector.tensor_max(maxacc[:], maxacc[:], mx4[:]))
                        else:
                            vq.inc(vector.tensor_max(maxacc[0:2, :], maxacc[0:2, :],
                                                     mx4[0:2, :]))
                        free["maxacc"] = (vq, vq.n)

            vq_all = vq.n

            @V
            def _(vector):
                FL.wait(vector, "v", vq, vq_all)
                vq.inc(vector.tensor_copy(maxaccb[:], maxacc[:]))
            mark_mab = vq.n

            @T
            def _(tensor):
                FL.wait(tensor, "t", vq, mark_mab)
                wf(tensor, "t", "psT0")
                tq.inc(tensor.transpose(psT0[0:1, 0:4], maxaccb[:], identb[0:4, 0:4]))
            mark_mat = tq.n

            @V
            def _(vector):
                FL.wait(vector, "v", tq, mark_mat)
                vq.inc(vector.tensor_copy(mrow[:], psT0[0:1, 0:4]))
                free["psT0"] = (vq, vq.n)
                FL.wait(vector, "v", vq, vq.n)
                vq.inc(vector.tensor_mul(mab[:], mrow[0:1, 0:2], mrow[0:1, 2:4]))
            mark_ab = vq.n

            @A
            def _(scalar):
                FL.wait(scalar, "a", vq, mark_ab)
                aq.inc(scalar.activation(mln[:], mab[:], AF.Ln, bias=epsTINY[:], scale=1.0))
                FL.wait(scalar, "a", aq, aq.n)
                aq.inc(scalar.activation(negm2[:], mln[:], AF.Exp, scale=0.5))
            mark_mh = aq.n

            @T
            def _(tensor):
                FL.wait(tensor, "t", aq, mark_mh)
                wf(tensor, "t", "psQ")
                tensor.matmul(psB[:, 384:385], onesf[:], negm2[0:1, 0:1], start=True, stop=True)
                tq.inc(tensor.matmul(psB[:, 385:386], onesf[:], negm2[0:1, 1:2],
                                     start=True, stop=True))
            mark_mq = tq.n

            @V
            def _(vector):
                FL.wait(vector, "v", tq, mark_mq)
                wf(vector, "v", "negMt")
                vq.inc(vector.tensor_scalar_mul(negMt[:], psB[:, 384:386], -1.0))
            free["psQ"] = (vq, vq.n)
            mark_negM = vq.n

            for qb in range(NQB if nstage >= 4 else 0):
                aq_base = aq.n
                exp_mark_A = {g: aq_base + 2 * g + 1 for g in range(NG)}
                exp_mark_B = {g: aq_base + 2 * g + 2 for g in range(NG)}
                aq.reserve(2 * NG)
                mark_r = aq.reserve(2)

                s_mark_A = {}
                s_mark_B = {}
                pv_mark_A = {}
                pv_mark_B = {}

                @T
                def _(tensor, qb=qb):
                    FL.wait(tensor, "t", vq, mark_negM)
                    FL.wait(tensor, "t", vq, prev_bankfree[1])
                    for g in range(NG + 1):
                        if g < NG:
                            FL.wait(tensor, "t", aq, exp_mark_B.get(g - 1, 0))
                            kc0, kc1 = 2 * g, 2 * g + 1
                            tensor.matmul(psA[:, 0:512],
                                          kT[0:64, kc0 * 128:(kc0 + 1) * 128],
                                          qT[0:64, qb * 512:(qb + 1) * 512],
                                          start=True, stop=True)
                            tq.inc(tensor.matmul(psA[:, 512:1024],
                                                 kT[0:64, kc1 * 128:(kc1 + 1) * 128],
                                                 qT[0:64, qb * 512:(qb + 1) * 512],
                                                 start=True, stop=True))
                            s_mark_A[g] = tq.n
                            tensor.matmul(psB[:, 0:512],
                                          kT[64:128, kc0 * 128:(kc0 + 1) * 128],
                                          qT[64:128, qb * 512:(qb + 1) * 512],
                                          start=True, stop=True, tile_position=(64, 0))
                            tq.inc(tensor.matmul(psB[:, 512:1024],
                                                 kT[64:128, kc1 * 128:(kc1 + 1) * 128],
                                                 qT[64:128, qb * 512:(qb + 1) * 512],
                                                 start=True, stop=True, tile_position=(64, 0)))
                            s_mark_B[g] = tq.n
                        if g >= 1:
                            gp_ = g - 1
                            par = gp_ % 2
                            FL.wait(tensor, "t", aq, exp_mark_B[gp_])
                            FL.wait(tensor, "t", vq, prev_ofree[1])
                            kc0, kc1 = 2 * gp_, 2 * gp_ + 1
                            tensor.matmul(psO0[:],
                                          vbuf[:, kc0 * 65: kc0 * 65 + 65],
                                          ptb[par][:, 0:512],
                                          start=(gp_ == 0), stop=False)
                            tq.inc(tensor.matmul(psO0[:],
                                                 vbuf[:, kc1 * 65: kc1 * 65 + 65],
                                                 ptb[par][:, 512:1024],
                                                 start=False, stop=(gp_ == NG - 1)))
                            pv_mark_A[gp_] = tq.n
                            tensor.matmul(psO1[:],
                                          vbuf[:, (KC + kc0) * 65: (KC + kc0) * 65 + 65],
                                          ptb[2 + par][:, 0:512],
                                          start=(gp_ == 0), stop=False)
                            tq.inc(tensor.matmul(psO1[:],
                                                 vbuf[:, (KC + kc1) * 65: (KC + kc1) * 65 + 65],
                                                 ptb[2 + par][:, 512:1024],
                                                 start=False, stop=(gp_ == NG - 1)))
                            pv_mark_B[gp_] = tq.n
                    FL.wait(tensor, "t", aq, mark_r)
                    tensor.matmul(psA[0:64, 0:512], onesf[0:1, 0:64], r01[0:1, 0:512],
                                  start=True, stop=True)
                    tq.inc(tensor.matmul(psB[0:64, 0:512], onesf[0:1, 0:64],
                                         r01[0:1, 512:1024], start=True, stop=True))
                mark_pv_last = pv_mark_B[NG - 1]
                mark_bc = tq.n

                @V
                def _(vector, qb=qb, hp=hp):
                    FL.wait(vector, "v", tq, mark_pv_last)
                    vq.inc(vector.tensor_copy(d01[0:1, 0:512], psO0[64:65, :]))
                    vq.inc(vector.tensor_copy(d01[0:1, 512:1024], psO1[64:65, :]))
                    mark_d_local = vq.n
                    FL.wait(vector, "v", tq, mark_bc)
                    vq.inc(vector.tensor_copy(rb0[:], psA[0:64, 0:512]))
                    vq.inc(vector.tensor_copy(rb1[:], psB[0:64, 0:512]))
                    FL.wait(vector, "v", vq, vq.n)
                    jo = qb % 2
                    wf(vector, "v", f"obuf{jo}")
                    vq.inc(vector.tensor_mul(obuf[jo][0:64, :], psO0[0:64, :], rb0[:]))
                    vq.inc(vector.tensor_mul(obuf[jo][64:128, :], psO1[0:64, :], rb1[:]))
                mark_d = vq.n - 4
                prev_bankfree = (vq, vq.n)
                prev_ofree = (vq, vq.n)
                mark_ob = vq.n

                @A
                def _(scalar, qb=qb):
                    FL.wait(scalar, "a", vq, mark_negM)
                    for g in range(NG):
                        par = g % 2
                        FL.wait(scalar, "a", tq, s_mark_A[g])
                        FL.wait(scalar, "a", tq, pv_mark_A.get(g - 2, 0))
                        aq.inc_pre(scalar.activation(ptb[par][:], psA[:], AF.Exp,
                                                     bias=negMt[:, 0:1], scale=1.0))
                        FL.wait(scalar, "a", tq, s_mark_B[g])
                        FL.wait(scalar, "a", tq, pv_mark_B.get(g - 2, 0))
                        aq.inc_pre(scalar.activation(ptb[2 + par][:], psB[:], AF.Exp,
                                                     bias=negMt[:, 1:2], scale=1.0))
                    FL.wait(scalar, "a", vq, mark_d)
                    aq.inc_pre(scalar.activation(ld01[:], d01[:], AF.Ln, scale=1.0))
                    FL.wait(scalar, "a", aq, mark_r - 1)
                    aq.inc_pre(scalar.activation(r01[:], ld01[:], AF.Exp, scale=-1.0))
                FL.f.setdefault("a", {})[id(aq)] = max(
                    FL.f.get("a", {}).get(id(aq), 0), mark_r)

                @GP
                def _(gp, qb=qb, hp=hp):
                    FL.wait(gp, "g", vq, mark_ob)
                    jo = qb % 2
                    c_sto[jo].inc(gp.dma_start(oT_h.ap()[hp, :, qb * 512:(qb + 1) * 512],
                                               obuf[jo][:]))
                free[f"obuf{qb % 2}"] = (c_sto[qb % 2], c_sto[qb % 2].n)

            setF("kT", tq)
            setF("qT", tq)
            setF("vbuf", tq)
            free["negMt"] = (aq, aq.n)

        sto_totals = [c_sto[0].n, c_sto[1].n]

        for r in range(8 if DBG.get("s5_prep", True) else 0):
            load_wT(projT, proj_w, r * 128, C, r * 128, dst_free_key="wbT")
        mark_pj = vq.n

        for i in range(NOWN if nstage >= 5 else 0):
            j = i % 2

            @SYNC
            def _(sync, i=i, j=j):
                FL.wait(sync, "s", c_sto[0], sto_totals[0])
                FL.wait(sync, "s", c_sto[1], sto_totals[1])
                FL.wait(sync, "s", c_stg[0], stg_silu_totals[0])
                FL.wait(sync, "s", c_stg[1], stg_silu_totals[1])
                wf(sync, "s", f"ostage{j}")
                for hp in range(HP):
                    c_ld5[j].inc(sync.dma_start(ostage[j][:, hp * 128:(hp + 1) * 128],
                                                oT_h.ap()[hp, :, i * 128:(i + 1) * 128]))
                wf(sync, "s", f"gld{j}")
                c_ld5[j].inc(sync.dma_start(gld[j][:], g_h.ap()[i * 128:(i + 1) * 128, :]))
                wf(sync, "s", f"xt{j}")
                c_ld5[j].inc(sync.dma_start(xld[j][:], xb.ap()[i * 128:(i + 1) * 128, :]))
            mark_ld = c_ld5[j].n

            @T
            def _(tensor, i=i, j=j):
                FL.wait(tensor, "t", c_ld5[j], mark_ld)
                FL.wait(tensor, "t", vq, mark_pj)
                FL.wait(tensor, "t", vq, free["psA_s5"][1] if "psA_s5" in free else 0)
                last = None
                for half in range(2):
                    for cc in range(8):
                        last = tensor.matmul(
                            psA[:, half * 512:(half + 1) * 512],
                            ostage[j][:, cc * 128:(cc + 1) * 128],
                            projT[:, cc * C + half * 512: cc * C + half * 512 + 512],
                            start=(cc == 0), stop=(cc == 7))
                tq.inc(last)
            free[f"ostage{j}"] = (tq, tq.n)
            mark_proj = tq.n

            @V
            def _(vector, i=i, j=j):
                FL.wait(vector, "v", tq, mark_proj)
                vq.inc(vector.tensor_add(s1t[:], psA[:], projbB[:]))
                FL.wait(vector, "v", vq, vq.n)
                vq.inc(vector.tensor_mul(s2t[:], s1t[:], gld[j][:]))
                FL.wait(vector, "v", vq, vq.n)
                wf(vector, "v", f"outt{j}")
                vq.inc(vector.tensor_add(outt[j][:], s2t[:], xld[j][:]))
            free["psA_s5"] = (vq, vq.n)
            free[f"gld{j}"] = (vq, vq.n)
            free[f"xt{j}"] = (vq, vq.n)
            mark_out = vq.n

            @GP
            def _(gp, i=i, j=j):
                FL.wait(gp, "g", vq, mark_out)
                c_stu[j].inc(gp.dma_start(out.ap()[i * 128:(i + 1) * 128, :], outt[j][:]))
            free[f"outt{j}"] = (c_stu[j], c_stu[j].n)

        stu_totals = [c_stu[0].n, c_stu[1].n]

        @GP
        def _(gp):
            gp.wait_ge(c_stu[0].sem, stu_totals[0])
            gp.wait_ge(c_stu[1].sem, stu_totals[1])

        @SYNC
        def _(sync):
            sync.wait_ge(c_stu[0].sem, stu_totals[0])
            sync.wait_ge(c_stu[1].sem, stu_totals[1])

    return nc



def _shard_inputs(L0_attn_x, L0_attn_qkv_w, L0_attn_proj_w, L0_attn_proj_b,
                  L0_attn_q_norm_w, L0_attn_k_norm_w, L0_ffn_CB, L0_ffn_SCB,
                  L0_ffn_bias):
    x = np.ascontiguousarray(np.asarray(L0_attn_x, dtype=np.float32))
    B, N, Cc = x.shape
    common = {
        "qkv_w": np.ascontiguousarray(np.asarray(L0_attn_qkv_w, np.float32)),
        "proj_w": np.ascontiguousarray(np.asarray(L0_attn_proj_w, np.float32)),
        "proj_b": np.asarray(L0_attn_proj_b, np.float32).reshape(1, Cc),
        "qnw": np.asarray(L0_attn_q_norm_w, np.float32).reshape(1, DH),
        "knw": np.asarray(L0_attn_k_norm_w, np.float32).reshape(1, DH),
        "cbw": np.ascontiguousarray(np.asarray(L0_ffn_CB, np.float32)),
        "scb": np.asarray(L0_ffn_SCB, np.float32).reshape(1, Cc),
        "fbias": np.asarray(L0_ffn_bias, np.float32).reshape(1, Cc),
        "ident_in": np.eye(128, dtype=np.float32).astype(bfdt),
    }
    in_maps = []
    half = N // 2
    for c in range(8):
        b, h = c // 2, c % 2
        own = x[b, h * half:(h + 1) * half]
        other = x[b, (1 - h) * half:(2 - h) * half]
        m = dict(common)
        m["xb"] = np.ascontiguousarray(np.concatenate([own, other], axis=0))
        in_maps.append(m)
    return in_maps, (B, N, Cc)


_CACHE = {}


def kernel(**inputs):
    in_maps, (B, N, Cc) = _shard_inputs(**inputs)
    key = (N,)
    if key not in _CACHE:
        _CACHE[key] = build(T_ALL=N, T_OWN=N // 2)
    nc = _CACHE[key]
    res = run_bass_kernel_spmd(nc, in_maps, core_ids=list(range(8)))
    half = N // 2
    outp = np.empty((B, N, Cc), dtype=np.float32)
    for c in range(8):
        b, h = c // 2, c % 2
        outp[b, h * half:(h + 1) * half] = res.results[c]["out"]
    return outp


# version 03
# speedup vs baseline: 1.3759x; 1.1106x over previous
import numpy as np
import ml_dtypes
from contextlib import ExitStack

import concourse.bass as bass
import concourse.mybir as mybir
from concourse.bass_utils import run_bass_kernel_spmd

F32 = mybir.dt.float32
BF16 = mybir.dt.bfloat16
AF = mybir.ActivationFunctionType
ALU = mybir.AluOpType
AX = mybir.AxisListType
bfdt = ml_dtypes.bfloat16

C = 1024
DH = 64
HP = 8
DBG = {}


class Chain:
    def __init__(self, sem, step=1):
        self.sem = sem
        self.n = 0
        self.step = step

    def inc(self, inst, k=None):
        k = self.step if k is None else k
        inst.then_inc(self.sem, k)
        self.n += k
        return self.n

    def reserve(self, k=1):
        self.n += k
        return self.n

    def inc_pre(self, inst, k=1):
        inst.then_inc(self.sem, k)
        return inst


class Floors:
    def __init__(self):
        self.f = {}

    def wait(self, eng, key, chain, n):
        if n <= 0:
            return
        d = self.f.setdefault(key, {})
        if n > d.get(id(chain), 0):
            assert n <= chain.n, f"wait {n} > emitted {chain.n} ({key})"
            eng.wait_ge(chain.sem, n)
            d[id(chain)] = n


def build(T_ALL=8192, T_OWN=4096, nstage=99):
    KC = T_ALL // 128
    NSLAB = T_ALL // 512
    NQB = T_OWN // 512
    NOWN = T_OWN // 128
    NG = KC // 2

    nc = bass.Bass("TRN2", target_bir_lowering=False)

    xb = nc.dram_tensor("xb", [T_ALL, C], F32, kind="ExternalInput")
    qkv_w = nc.dram_tensor("qkv_w", [3 * C, C], F32, kind="ExternalInput")
    proj_w = nc.dram_tensor("proj_w", [C, C], F32, kind="ExternalInput")
    proj_b = nc.dram_tensor("proj_b", [1, C], F32, kind="ExternalInput")
    qnw = nc.dram_tensor("qnw", [1, DH], F32, kind="ExternalInput")
    knw = nc.dram_tensor("knw", [1, DH], F32, kind="ExternalInput")
    cbw = nc.dram_tensor("cbw", [C, C], F32, kind="ExternalInput")
    scb = nc.dram_tensor("scb", [1, C], F32, kind="ExternalInput")
    fbias = nc.dram_tensor("fbias", [1, C], F32, kind="ExternalInput")
    ident_in = nc.dram_tensor("ident_in", [128, 128], BF16, kind="ExternalInput")

    out = nc.dram_tensor("out", [T_OWN, C], F32, kind="ExternalOutput")

    xnT_h = nc.dram_tensor("xnT_h", [8, 128, T_ALL], BF16, kind="Internal")
    g_h = nc.dram_tensor("g_h", [T_OWN, C], F32, kind="Internal")
    oT_h = nc.dram_tensor("oT_h", [HP, 128, T_OWN], BF16, kind="Internal")

    with ExitStack() as ctx:
        sb = lambda n_, s_, d_: ctx.enter_context(nc.sbuf_tensor(n_, s_, d_))
        pt = lambda n_, s_, d_: ctx.enter_context(nc.psum_tensor(n_, s_, d_))
        mksem = lambda n_: ctx.enter_context(nc.semaphore(n_))

        identb = sb("identb", [128, 128], BF16)
        onesf = sb("onesf", [1, 128], F32)
        ones8 = sb("ones8", [1, 128], F32)
        ones127 = sb("ones127", [1, 128], F32)
        epsLN = sb("epsLN", [128, 1], F32)
        epsRMS = sb("epsRMS", [128, 1], F32)
        epsTINY = sb("epsTINY", [1, 1], F32)
        wrow = sb("wrow", [1, 256], F32)
        srow = sb("srow", [1, C], F32)
        frow = sb("frow", [1, C], F32)
        prow = sb("prow", [1, C], F32)
        wmul = sb("wmul", [128, 256], F32)
        scbB = sb("scbB", [128, C], F32)
        fbiasB = sb("fbiasB", [128, C], F32)
        projbB = sb("projbB", [128, C], F32)

        xt = [sb(f"xt{j}", [128, C], F32) for j in range(2)]
        lnsum = sb("lnsum", [128, 1], F32)

        xss = sb("xss", [128, 1], F32)
        lnmean = sb("lnmean", [128, 1], F32)
        lnex2 = sb("lnex2", [128, 1], F32)
        lnm2 = sb("lnm2", [128, 1], F32)
        lnvar = sb("lnvar", [128, 1], F32)
        lnl = sb("lnl", [128, 1], F32)
        lnr = sb("lnr", [128, 1], F32)
        xn_tok = [sb(f"xn_tok{j}", [128, C], BF16) for j in range(2)]
        xnT_st = [sb(f"xnT_st{j}", [128, C], BF16) for j in range(2)]

        wstage = sb("wstage", [128, C], F32)
        wstage_b = sb("wstage_b", [128, C], BF16)
        cbT = sb("cbT", [128, 8 * C], BF16)
        projT = cbT
        wt_buf = sb("wt_buf", [128, 8 * 384], BF16)


        gtile = [sb(f"gtile{j}", [128, C], F32) for j in range(2)]
        gld = [sb(f"gld{j}", [128, C], F32) for j in range(2)]

        xslab = [sb(f"xslab{j}", [128, 8 * 512], BF16) for j in range(2)]
        kT = sb("kT", [128, T_ALL], BF16)
        qT = sb("qT", [128, T_OWN], BF16)
        vbuf = sb("vbuf", [128, 2 * KC * 65], BF16)
        qkvf4 = sb("qkvf4", [128, 4 * 384], F32)
        sqt = sb("sqt", [128, 256], F32)
        rms16 = sb("rms16", [128, 16], F32)
        rstdL = sb("rstdL", [128, 16], F32)
        rstd16 = sb("rstd16", [128, 16], F32)
        tmn = sb("tmn", [128, 256], F32)
        kq_hat = sb("kq_hat", [128, 256], BF16)
        sqm = sb("sqm", [128, 256], BF16)
        m4 = sb("m4", [128, 4], F32)
        m4b = sb("m4b", [128, 4], BF16)
        mx4 = sb("mx4", [4, 1], F32)
        maxacc = sb("maxacc", [4, 1], F32)
        maxaccb = sb("maxaccb", [4, 1], BF16)
        mrow = sb("mrow", [1, 4], F32)
        mab = sb("mab", [1, 2], F32)
        mln = sb("mln", [1, 2], F32)
        negm2 = sb("negm2", [1, 2], F32)
        negMt = sb("negMt", [128, 2], F32)

        ptb = [sb(f"ptb{j}", [128, 1024], BF16) for j in range(4)]
        d01 = sb("d01", [1, 1024], F32)
        ld01 = sb("ld01", [1, 1024], F32)
        r01 = sb("r01", [1, 1024], F32)
        rb0 = sb("rb0", [64, 512], F32)
        rb1 = sb("rb1", [64, 512], F32)
        obuf = [sb(f"obuf{j}", [128, 512], BF16) for j in range(2)]

        ostage = [sb(f"ostage{j}", [128, 8 * 128], BF16) for j in range(2)]
        xld = xt
        s1t = sb("s1t", [128, C], F32)
        s2t = sb("s2t", [128, C], F32)
        outt = [sb(f"outt{j}", [128, C], F32) for j in range(2)]
        xsq = s1t
        fo = s2t
        fo2 = outt[0]

        psA = pt("psA", [128, 1024], F32)
        psB = pt("psB", [128, 1024], F32)
        psO0 = pt("psO0", [65, 512], F32)
        psO1 = pt("psO1", [65, 512], F32)
        psT0 = pt("psT0", [128, 128], BF16)
        psT1 = pt("psT1", [128, 128], BF16)
        psTs = (psT0, psT1)

        c_ldc = Chain(mksem("ldc"), 16)
        c_ldw = Chain(mksem("ldw"), 16)
        c_ldx = [Chain(mksem(f"ldx{j}"), 16) for j in range(2)]
        c_lds = [Chain(mksem(f"lds{j}"), 16) for j in range(2)]
        c_ldg = [Chain(mksem(f"ldg{j}"), 16) for j in range(2)]
        c_ld5 = [Chain(mksem(f"ld5{j}"), 16) for j in range(2)]
        c_stx = [Chain(mksem(f"stx{j}"), 16) for j in range(2)]
        c_stg = [Chain(mksem(f"stg{j}"), 16) for j in range(2)]
        c_sto = [Chain(mksem(f"sto{j}"), 16) for j in range(2)]
        c_stu = [Chain(mksem(f"stu{j}"), 16) for j in range(2)]
        tq = Chain(mksem("tq"))
        vq = Chain(mksem("vq"))
        aq = Chain(mksem("aq"))

        FL = Floors()
        block = ctx.enter_context(nc.Block())
        SYNC, GP, T, V, A = block.sync, block.gpsimd, block.tensor, block.vector, block.scalar

        free = {
            "xt0": (vq, 0), "xt1": (vq, 0),
            "xn0": (tq, 0), "xn1": (tq, 0),
            "xnTst0_dma": (c_stx[0], 0), "xnTst1_dma": (c_stx[1], 0),
            "xnTst0_ffn": (tq, 0), "xnTst1_ffn": (tq, 0),
            "psB_v": (vq, 0),
            "gtile0": (c_stg[0], 0), "gtile1": (c_stg[1], 0),
            "wstage": (vq, 0), "wstage_b": (tq, 0),
            "psT0": (vq, 0), "psT1": (vq, 0),
            "wt_buf": (tq, 0), "wbT": (tq, 0),
            "xslab0": (tq, 0), "xslab1": (tq, 0),
            "psQ": (vq, 0),
            "kT": (tq, 0), "qT": (tq, 0), "vbuf": (tq, 0),
            "negMt": (aq, 0),
            "maxacc": (vq, 0),
            "gld0": (aq, 0), "gld1": (aq, 0),
            "obuf0": (c_sto[0], 0), "obuf1": (c_sto[1], 0),
            "ostage0": (tq, 0), "ostage1": (tq, 0),

            "outt0": (c_stu[0], 0), "outt1": (c_stu[1], 0),
        }

        def FR(name):
            return free[name]

        def setF(name, chain):
            free[name] = (chain, chain.n)

        def wf(eng, key, name):
            ch, n = free[name]
            FL.wait(eng, key, ch, n)

        @SYNC
        def _(sync):
            c_ldc.inc(sync.dma_start(identb[:], ident_in.ap()))
            c_ldc.inc(sync.dma_start(wrow[0:1, 0:64], knw.ap()))
            c_ldc.inc(sync.dma_start(wrow[0:1, 64:128], knw.ap()))
            c_ldc.inc(sync.dma_start(wrow[0:1, 128:192], qnw.ap()))
            c_ldc.inc(sync.dma_start(wrow[0:1, 192:256], qnw.ap()))
            c_ldc.inc(sync.dma_start(srow[:], scb.ap()))
            c_ldc.inc(sync.dma_start(frow[:], fbias.ap()))
            c_ldc.inc(sync.dma_start(prow[:], proj_b.ap()))
        ld_consts = c_ldc.n

        @V
        def _(vector):
            vq.inc(vector.memset(onesf[:], 1.0))
            vq.inc(vector.memset(ones8[:], 0.125))
            vq.inc(vector.memset(ones127[:], 1.0 / 127.0))
            vq.inc(vector.memset(epsLN[:], 1e-5))
            vq.inc(vector.memset(epsRMS[:], 1e-6))
            vq.inc(vector.memset(epsTINY[:], 1e-30))
        v_consts = vq.n

        if DBG.get("s0_bc", True):
            @T
            def _(tensor):
                FL.wait(tensor, "t", c_ldc, ld_consts)
                FL.wait(tensor, "t", vq, v_consts)
                tensor.matmul(psB[:, 0:128], onesf[:], wrow[0:1, 0:128], start=True, stop=True)
                tq.inc(tensor.matmul(psB[:, 128:256], ones8[:], wrow[0:1, 128:256],
                                     start=True, stop=True))
                tensor.matmul(psA[:, 0:512], ones127[:], srow[0:1, 0:512], start=True, stop=True)
                tq.inc(tensor.matmul(psA[:, 512:1024], ones127[:], srow[0:1, 512:1024],
                                     start=True, stop=True))
            t_bc1 = tq.n

            @V
            def _(vector):
                FL.wait(vector, "v", tq, t_bc1)
                vq.inc(vector.tensor_copy(wmul[:], psB[:, 0:256]))
                vq.inc(vector.tensor_copy(scbB[:], psA[:]))
            v_bc1 = vq.n

            @T
            def _(tensor):
                FL.wait(tensor, "t", vq, v_bc1)
                tensor.matmul(psB[:, 0:512], onesf[:], frow[0:1, 0:512], start=True, stop=True)
                tensor.matmul(psB[:, 512:1024], onesf[:], frow[0:1, 512:1024],
                              start=True, stop=True)
                tensor.matmul(psA[:, 0:512], onesf[:], prow[0:1, 0:512], start=True, stop=True)
                tq.inc(tensor.matmul(psA[:, 512:1024], onesf[:], prow[0:1, 512:1024],
                                     start=True, stop=True))
            t_bc2 = tq.n

            @V
            def _(vector):
                FL.wait(vector, "v", tq, t_bc2)
                vq.inc(vector.tensor_copy(fbiasB[:], psB[:]))
                vq.inc(vector.tensor_copy(projbB[:], psA[:]))

        def load_wT(dst_sb, src_dram, rows0, dst_cc_stride, dst_off, dst_free_key=None):
            @SYNC
            def _(sync):
                wf(sync, "s", "wstage")
                c_ldw.inc(sync.dma_start(wstage[:], src_dram.ap()[rows0:rows0 + 128, :]))
            mark_ld = c_ldw.n

            @V
            def _(vector):
                FL.wait(vector, "v", c_ldw, mark_ld)
                wf(vector, "v", "wstage_b")
                vq.inc(vector.tensor_copy(wstage_b[:], wstage[:]))
            setF("wstage", vq)
            mark_cast = vq.n

            for cc in range(8):
                slot = cc % 2

                @T
                def _(tensor, cc=cc, slot=slot):
                    FL.wait(tensor, "t", vq, mark_cast)
                    wf(tensor, "t", f"psT{slot}")
                    if dst_free_key:
                        wf(tensor, "t", dst_free_key)
                    tq.inc(tensor.transpose(psTs[slot][:],
                                            wstage_b[:, cc * 128:(cc + 1) * 128],
                                            identb[:]))
                mark_t = tq.n
                if cc == 7:
                    setF("wstage_b", tq)

                @V
                def _(vector, cc=cc, slot=slot):
                    FL.wait(vector, "v", tq, mark_t)
                    vq.inc(vector.tensor_copy(
                        dst_sb[:, cc * dst_cc_stride + dst_off:
                               cc * dst_cc_stride + dst_off + 128],
                        psTs[slot][:]))
                free[f"psT{slot}"] = (vq, vq.n)
            return vq.n

        cb_ready = 0
        for r in range(8 if DBG.get("s0_cbt", True) else 0):
            cb_ready = load_wT(cbT, cbw, r * 128, C, r * 128)

        for i in range(min(KC if nstage >= 1 else 0, DBG.get("s1_chunks", 10**9))):
            j = i % 2
            own = i < NOWN

            @SYNC
            def _(sync, i=i, j=j):
                wf(sync, "s", f"xt{j}")
                c_ldx[j].inc(sync.dma_start(xt[j][:], xb.ap()[i * 128:(i + 1) * 128, :]))
            mark_x = c_ldx[j].n

            @V
            def _(vector, j=j):
                FL.wait(vector, "v", c_ldx[j], mark_x)
                vq.inc(vector.reduce_sum(lnsum[:], xt[j][:], axis=AX.X))
                vq.inc(vector.tensor_mul(xsq[:], xt[j][:], xt[j][:]))
                FL.wait(vector, "v", vq, vq.n)
                vq.inc(vector.reduce_sum(xss[:], xsq[:], axis=AX.X))
                vq.inc(vector.tensor_scalar_mul(lnmean[:], lnsum[:], 1.0 / C))
                FL.wait(vector, "v", vq, vq.n)
                vq.inc(vector.tensor_scalar_mul(lnex2[:], xss[:], 1.0 / C))
                vq.inc(vector.tensor_mul(lnm2[:], lnmean[:], lnmean[:]))
                FL.wait(vector, "v", vq, vq.n)
                vq.inc(vector.tensor_sub(lnvar[:], lnex2[:], lnm2[:]))
            mark_var = vq.n

            @A
            def _(scalar):
                FL.wait(scalar, "a", vq, mark_var)
                aq.inc(scalar.activation(lnl[:], lnvar[:], AF.Ln, bias=epsLN[:], scale=1.0))
                FL.wait(scalar, "a", aq, aq.n)
                aq.inc(scalar.activation(lnr[:], lnl[:], AF.Exp, scale=-0.5))
            mark_rstd = aq.n

            @V
            def _(vector, j=j):
                FL.wait(vector, "v", aq, mark_rstd)
                wf(vector, "v", f"xn{j}")
                vq.inc(vector.tensor_scalar(xn_tok[j][:], xt[j][:], lnmean[:], lnr[:],
                                            ALU.subtract, ALU.mult))
            mark_xn = vq.n
            free[f"xt{j}"] = (vq, vq.n)

            for cc in range(8 if DBG.get("s1_tp", True) else 0):
                slot = cc % 2

                @T
                def _(tensor, cc=cc, j=j, slot=slot):
                    FL.wait(tensor, "t", vq, mark_xn)
                    wf(tensor, "t", f"psT{slot}")
                    wf(tensor, "t", f"xnTst{j}_dma")
                    wf(tensor, "t", f"xnTst{j}_ffn")
                    tq.inc(tensor.transpose(psTs[slot][:],
                                            xn_tok[j][:, cc * 128:(cc + 1) * 128],
                                            identb[:]))
                mark_t = tq.n
                if cc == 7:
                    free[f"xn{j}"] = (tq, tq.n)

                @V
                def _(vector, cc=cc, j=j, slot=slot):
                    FL.wait(vector, "v", tq, mark_t)
                    vq.inc(vector.tensor_copy(xnT_st[j][:, cc * 128:(cc + 1) * 128],
                                              psTs[slot][:]))
                free[f"psT{slot}"] = (vq, vq.n)
            mark_xnT = vq.n

            @GP
            def _(gp, i=i, j=j):
                if not DBG.get("s1_dma", True):
                    return
                FL.wait(gp, "g", vq, mark_xnT)
                c_stx[j].inc(gp.dma_start(
                    xnT_h.ap()[:, :, i * 128:(i + 1) * 128].rearrange("c p t -> p c t"),
                    xnT_st[j].rearrange("p (c t) -> p c t", t=128)))
            free[f"xnTst{j}_dma"] = (c_stx[j], c_stx[j].n)

            if own and DBG.get("s1_ffn", True):
                @T
                def _(tensor, j=j):
                    FL.wait(tensor, "t", vq, mark_xnT)
                    FL.wait(tensor, "t", vq, cb_ready)
                    wf(tensor, "t", "psB_v")
                    last = None
                    for half in range(2):
                        for cc in range(8):
                            last = tensor.matmul(
                                psB[:, half * 512:(half + 1) * 512],
                                xnT_st[j][:, cc * 128:(cc + 1) * 128],
                                cbT[:, cc * C + half * 512: cc * C + half * 512 + 512],
                                start=(cc == 0), stop=(cc == 7))
                    tq.inc(last)
                mark_ffn = tq.n
                free[f"xnTst{j}_ffn"] = (tq, tq.n)

                @V
                def _(vector, j=j):
                    FL.wait(vector, "v", tq, mark_ffn)
                    vq.inc(vector.tensor_mul(fo[:], psB[:], scbB[:]))
                    FL.wait(vector, "v", vq, vq.n)
                    vq.inc(vector.tensor_add(fo2[:], fo[:], fbiasB[:]))
                    FL.wait(vector, "v", vq, vq.n)
                    wf(vector, "v", f"gtile{j}")
                    vq.inc(vector.tensor_copy(gtile[j][:], fo2[:]))
                free["psB_v"] = (vq, vq.n)
                mark_g = vq.n

                @GP
                def _(gp, i=i, j=j):
                    FL.wait(gp, "g", vq, mark_g)
                    c_stg[j].inc(gp.dma_start(g_h.ap()[i * 128:(i + 1) * 128, :], gtile[j][:]))
                free[f"gtile{j}"] = (c_stg[j], c_stg[j].n)

        stg_totals = [c_stg[0].n, c_stg[1].n]
        stx_totals = [c_stx[0].n, c_stx[1].n]
        free["wbT"] = (tq, tq.n)
        free["outt0"] = (vq, vq.n)
        free["xt0"] = (vq, vq.n)
        free["xt1"] = (vq, vq.n)

        for i in range(NOWN if nstage >= 2 else 0):
            j = i % 2

            @SYNC
            def _(sync, i=i, j=j):
                FL.wait(sync, "s", c_stg[0], stg_totals[0])
                FL.wait(sync, "s", c_stg[1], stg_totals[1])
                wf(sync, "s", f"gld{j}")
                c_ldg[j].inc(sync.dma_start(gld[j][:], g_h.ap()[i * 128:(i + 1) * 128, :]))
            mark_g = c_ldg[j].n

            @A
            def _(scalar, j=j):
                FL.wait(scalar, "a", c_ldg[j], mark_g)
                wf(scalar, "a", f"gtile{j}")
                aq.inc(scalar.activation(gtile[j][:], gld[j][:], AF.Silu, scale=1.0))
            free[f"gld{j}"] = (aq, aq.n)
            mark_s = aq.n

            @GP
            def _(gp, i=i, j=j):
                FL.wait(gp, "g", aq, mark_s)
                c_stg[j].inc(gp.dma_start(g_h.ap()[i * 128:(i + 1) * 128, :], gtile[j][:]))
            free[f"gtile{j}"] = (c_stg[j], c_stg[j].n)

        stg_silu_totals = [c_stg[0].n, c_stg[1].n]

        free["rms16"] = (aq, 0)
        free["qkvf4"] = (vq, 0)
        prev_ofree = (vq, 0)
        prev_bankfree = (vq, 0)
        last_expA = 0
        last_expB = 0
        pv_last = {0: 0, 1: 0, 2: 0, 3: 0}

        for hp in range(HP if nstage >= 3 else 0):
            aq_hpstart = aq.n
            for gidx, rows0 in enumerate((C + hp * 128, hp * 128, 2 * C + hp * 128)):
                load_wT(wt_buf, qkv_w, rows0, 384, gidx * 128, dst_free_key="wt_buf")
            mark_wt = vq.n

            @V
            def _(vector):
                wf(vector, "v", "maxacc")
                vq.inc(vector.memset(maxacc[:], 0.0))
                wf(vector, "v", "vbuf")
                vq.inc(vector.memset(
                    vbuf.rearrange("p (c j) -> p c j", j=65)[:, :, 64:65], 1.0))
            mark_vones = vq.n

            for s in range(NSLAB):
                jsl = s % 2

                @SYNC
                def _(sync, s=s, jsl=jsl):
                    wf(sync, "s", f"xslab{jsl}")
                    FL.wait(sync, "s", c_stx[0], stx_totals[0])
                    FL.wait(sync, "s", c_stx[1], stx_totals[1])
                    for cc in range(8):
                        c_lds[jsl].inc(sync.dma_start(
                            xslab[jsl][:, cc * 512:(cc + 1) * 512],
                            xnT_h.ap()[cc, :, s * 512:(s + 1) * 512]))
                mark_slab = c_lds[jsl].n

                for c4 in range(4):
                    i = s * 4 + c4

                    @T
                    def _(tensor, c4=c4, jsl=jsl):
                        FL.wait(tensor, "t", c_lds[jsl], mark_slab)
                        FL.wait(tensor, "t", vq, mark_wt)
                        wf(tensor, "t", "psQ")
                        FL.wait(tensor, "t", aq, aq_hpstart)
                        last = None
                        for cc in range(8):
                            last = tensor.matmul(
                                psB[:, 0:384],
                                xslab[jsl][:, cc * 512 + c4 * 128: cc * 512 + c4 * 128 + 128],
                                wt_buf[:, cc * 384: cc * 384 + 384],
                                start=(cc == 0), stop=(cc == 7))
                        tq.inc(last)
                    mark_qkv = tq.n
                    if s == NSLAB - 1 and c4 == 3:
                        setF("wt_buf", tq)

                    @V
                    def _(vector, c4=c4):
                        FL.wait(vector, "v", tq, mark_qkv)
                        wf(vector, "v", "qkvf4")
                        vq.inc(vector.tensor_copy(qkvf4[:, c4 * 384:(c4 + 1) * 384],
                                                  psB[:, 0:384]))
                        FL.wait(vector, "v", vq, vq.n)
                        vq.inc(vector.tensor_mul(sqt[:],
                                                 qkvf4[:, c4 * 384:c4 * 384 + 256],
                                                 qkvf4[:, c4 * 384:c4 * 384 + 256]))
                        FL.wait(vector, "v", vq, vq.n)
                        wf(vector, "v", "rms16")
                        vq.inc(vector.reduce_sum(rms16[:, c4 * 4:(c4 + 1) * 4],
                                                 sqt.rearrange("p (g d) -> p g d", d=DH),
                                                 axis=AX.X))
                    free["psQ"] = (vq, vq.n)
                mark_rms = vq.n
                free[f"xslab{jsl}"] = (tq, tq.n)

                @A
                def _(scalar):
                    FL.wait(scalar, "a", vq, mark_rms)
                    aq.inc(scalar.activation(rstdL[:], rms16[:], AF.Ln,
                                             bias=epsRMS[:], scale=1.0 / DH))
                    FL.wait(scalar, "a", aq, aq.n)
                    aq.inc(scalar.activation(rstd16[:], rstdL[:], AF.Exp, scale=-0.5))
                free["rms16"] = (aq, aq.n)
                mark_rstd = aq.n

                for c4 in range(4):
                    i = s * 4 + c4
                    own = i < NOWN

                    @V
                    def _(vector, c4=c4, i=i):
                        FL.wait(vector, "v", aq, mark_rstd)
                        base = c4 * 384
                        rb_ = c4 * 4
                        for g_ in range(4):
                            vq.inc(vector.tensor_scalar(
                                tmn[:, g_ * 64:(g_ + 1) * 64],
                                qkvf4[:, base + g_ * 64: base + (g_ + 1) * 64],
                                rstd16[:, rb_ + g_: rb_ + g_ + 1], None, ALU.mult))
                        FL.wait(vector, "v", vq, vq.n)
                        vq.inc(vector.tensor_mul(kq_hat[:], tmn[:], wmul[:]))
                        wf(vector, "v", "vbuf")
                        dst = vbuf.rearrange("p (h k j) -> p h k j", h=2, j=65)[
                            :, :, i:i + 1, 0:64].rearrange("p h k j -> p (h k) j")
                        srcv = qkvf4[:, base + 256: base + 384].rearrange(
                            "p (h j) -> p h j", h=2)
                        vq.inc(vector.tensor_copy(dst, srcv))
                    mark_kqv = vq.n
                    if c4 == 3:
                        free["qkvf4"] = (vq, vq.n)

                    @T
                    def _(tensor, i=i, own=own):
                        FL.wait(tensor, "t", vq, mark_kqv)
                        wf(tensor, "t", "psT0")
                        wf(tensor, "t", "kT")
                        tq.inc(tensor.transpose(psT0[:], kq_hat[:, 0:128], identb[:]))
                        if own:
                            wf(tensor, "t", "psT1")
                            wf(tensor, "t", "qT")
                            tq.inc(tensor.transpose(psT1[:], kq_hat[:, 128:256],
                                                    identb[:]))
                    mark_tp = tq.n

                    @V
                    def _(vector, i=i, own=own):
                        FL.wait(vector, "v", tq, mark_tp)
                        vq.inc(vector.tensor_copy(kT[:, i * 128:(i + 1) * 128], psT0[:]))
                        free["psT0"] = (vq, vq.n)
                        if own:
                            vq.inc(vector.tensor_copy(qT[:, i * 128:(i + 1) * 128],
                                                      psT1[:]))
                            free["psT1"] = (vq, vq.n)
                        vq.inc(vector.tensor_mul(sqm[:], kq_hat[:], kq_hat[:]))
                        FL.wait(vector, "v", vq, vq.n)
                        vq.inc(vector.reduce_sum(m4[:], sqm.rearrange("p (g d) -> p g d", d=DH),
                                                 axis=AX.X))
                        FL.wait(vector, "v", vq, vq.n)
                        vq.inc(vector.tensor_copy(m4b[:], m4[:]))
                    mark_m4 = vq.n

                    @T
                    def _(tensor, i=i):
                        FL.wait(tensor, "t", vq, mark_m4)
                        wf(tensor, "t", "psT0")
                        tq.inc(tensor.transpose(psT0[0:4, 0:128], m4b[:], identb[:]))
                    mark_mt = tq.n

                    @V
                    def _(vector, i=i, own=own):
                        FL.wait(vector, "v", tq, mark_mt)
                        vq.inc(vector.reduce_max(mx4[:], psT0[0:4, 0:128], axis=AX.X))
                        free["psT0"] = (vq, vq.n)
                        FL.wait(vector, "v", vq, vq.n)
                        wf(vector, "v", "maxacc")
                        if own:
                            vq.inc(vector.tensor_max(maxacc[:], maxacc[:], mx4[:]))
                        else:
                            vq.inc(vector.tensor_max(maxacc[0:2, :], maxacc[0:2, :],
                                                     mx4[0:2, :]))
                        free["maxacc"] = (vq, vq.n)

            vq_all = vq.n

            @V
            def _(vector):
                FL.wait(vector, "v", vq, vq_all)
                vq.inc(vector.tensor_copy(maxaccb[:], maxacc[:]))
            mark_mab = vq.n

            @T
            def _(tensor):
                FL.wait(tensor, "t", vq, mark_mab)
                wf(tensor, "t", "psT0")
                tq.inc(tensor.transpose(psT0[0:1, 0:4], maxaccb[:], identb[0:4, 0:4]))
            mark_mat = tq.n

            @V
            def _(vector):
                FL.wait(vector, "v", tq, mark_mat)
                vq.inc(vector.tensor_copy(mrow[:], psT0[0:1, 0:4]))
                free["psT0"] = (vq, vq.n)
                FL.wait(vector, "v", vq, vq.n)
                vq.inc(vector.tensor_mul(mab[:], mrow[0:1, 0:2], mrow[0:1, 2:4]))
            mark_ab = vq.n

            @V
            def _(vector):
                FL.wait(vector, "v", vq, mark_ab)
                vq.inc(vector.reduce_max(mab[0:1, 0:1], mab[:], axis=AX.X))
            mark_ab2 = vq.n

            @A
            def _(scalar):
                FL.wait(scalar, "a", vq, mark_ab2)
                aq.inc(scalar.activation(mln[0:1, 0:1], mab[0:1, 0:1], AF.Ln,
                                         bias=epsTINY[:], scale=1.0))
                FL.wait(scalar, "a", aq, aq.n)
                aq.inc(scalar.activation(negm2[0:1, 0:1], mln[0:1, 0:1], AF.Exp, scale=0.5))
            mark_mh = aq.n

            @T
            def _(tensor):
                FL.wait(tensor, "t", aq, mark_mh)
                wf(tensor, "t", "psQ")
                tq.inc(tensor.matmul(psB[:, 384:385], onesf[:], negm2[0:1, 0:1],
                                     start=True, stop=True))
            mark_mq = tq.n

            @V
            def _(vector):
                FL.wait(vector, "v", tq, mark_mq)
                wf(vector, "v", "negMt")
                vq.inc(vector.tensor_scalar_mul(negMt[:, 0:1], psB[:, 384:385], -1.0))
            free["psQ"] = (vq, vq.n)
            mark_negM = vq.n

            for qb in range(NQB if nstage >= 4 else 0):
                aq_base = aq.n
                exp_mark = {kc: aq_base + kc + 1 for kc in range(KC)}
                aq.reserve(KC)
                mark_r = aq.reserve(2)

                s_mark = {}
                pv_mark = {}
                psS = (psA, psB)

                @T
                def _(tensor, qb=qb):
                    FL.wait(tensor, "t", vq, mark_negM)
                    FL.wait(tensor, "t", vq, prev_bankfree[1])
                    for kc in range(KC + 1):
                        if kc < KC:
                            par = kc % 2
                            FL.wait(tensor, "t", aq, exp_mark.get(kc - 2, 0))
                            tensor.matmul(psS[par][:, 0:512],
                                          kT[0:64, kc * 128:(kc + 1) * 128],
                                          qT[0:64, qb * 512:(qb + 1) * 512],
                                          start=True, stop=True)
                            tq.inc(tensor.matmul(psS[par][:, 512:1024],
                                                 kT[64:128, kc * 128:(kc + 1) * 128],
                                                 qT[64:128, qb * 512:(qb + 1) * 512],
                                                 start=True, stop=True, tile_position=(64, 0)))
                            s_mark[kc] = tq.n
                        if kc >= 1:
                            kp = kc - 1
                            par = kp % 2
                            FL.wait(tensor, "t", aq, exp_mark[kp])
                            FL.wait(tensor, "t", vq, prev_ofree[1])
                            tensor.matmul(psO0[:],
                                          vbuf[:, kp * 65: kp * 65 + 65],
                                          ptb[par][:, 0:512],
                                          start=(kp == 0), stop=(kp == KC - 1))
                            tq.inc(tensor.matmul(psO1[:],
                                                 vbuf[:, (KC + kp) * 65: (KC + kp) * 65 + 65],
                                                 ptb[par][:, 512:1024],
                                                 start=(kp == 0), stop=(kp == KC - 1)))
                            pv_mark[kp] = tq.n
                    FL.wait(tensor, "t", aq, mark_r)
                    tensor.matmul(psA[0:64, 0:512], onesf[0:1, 0:64], r01[0:1, 0:512],
                                  start=True, stop=True)
                    tq.inc(tensor.matmul(psB[0:64, 0:512], onesf[0:1, 0:64],
                                         r01[0:1, 512:1024], start=True, stop=True))
                mark_pv_last = pv_mark[KC - 1]
                mark_bc = tq.n

                @V
                def _(vector, qb=qb, hp=hp):
                    FL.wait(vector, "v", tq, mark_pv_last)
                    vq.inc(vector.tensor_copy(d01[0:1, 0:512], psO0[64:65, :]))
                    vq.inc(vector.tensor_copy(d01[0:1, 512:1024], psO1[64:65, :]))
                    FL.wait(vector, "v", tq, mark_bc)
                    vq.inc(vector.tensor_copy(rb0[:], psA[0:64, 0:512]))
                    vq.inc(vector.tensor_copy(rb1[:], psB[0:64, 0:512]))
                    FL.wait(vector, "v", vq, vq.n)
                    jo = qb % 2
                    wf(vector, "v", f"obuf{jo}")
                    vq.inc(vector.tensor_mul(obuf[jo][0:64, :], psO0[0:64, :], rb0[:]))
                    vq.inc(vector.tensor_mul(obuf[jo][64:128, :], psO1[0:64, :], rb1[:]))
                mark_d = vq.n - 4
                prev_bankfree = (vq, vq.n)
                prev_ofree = (vq, vq.n)
                mark_ob = vq.n

                @A
                def _(scalar, qb=qb):
                    FL.wait(scalar, "a", vq, mark_negM)
                    for kc in range(KC):
                        par = kc % 2
                        FL.wait(scalar, "a", tq, s_mark[kc])
                        FL.wait(scalar, "a", tq, pv_mark.get(kc - 2, 0))
                        aq.inc_pre(scalar.activation(ptb[par][:], psS[par][:], AF.Exp,
                                                     bias=negMt[:, 0:1], scale=1.0))
                    FL.wait(scalar, "a", vq, mark_d)
                    aq.inc_pre(scalar.activation(ld01[:], d01[:], AF.Ln, scale=1.0))
                    FL.wait(scalar, "a", aq, mark_r - 1)
                    aq.inc_pre(scalar.activation(r01[:], ld01[:], AF.Exp, scale=-1.0))
                FL.f.setdefault("a", {})[id(aq)] = max(
                    FL.f.get("a", {}).get(id(aq), 0), mark_r)

                @GP
                def _(gp, qb=qb, hp=hp):
                    FL.wait(gp, "g", vq, mark_ob)
                    jo = qb % 2
                    c_sto[jo].inc(gp.dma_start(oT_h.ap()[hp, :, qb * 512:(qb + 1) * 512],
                                               obuf[jo][:]))
                free[f"obuf{qb % 2}"] = (c_sto[qb % 2], c_sto[qb % 2].n)

            setF("kT", tq)
            setF("qT", tq)
            setF("vbuf", tq)
            free["negMt"] = (aq, aq.n)

        sto_totals = [c_sto[0].n, c_sto[1].n]

        for r in range(8 if DBG.get("s5_prep", True) else 0):
            load_wT(projT, proj_w, r * 128, C, r * 128, dst_free_key="wbT")
        mark_pj = vq.n

        for i in range(NOWN if nstage >= 5 else 0):
            j = i % 2

            @SYNC
            def _(sync, i=i, j=j):
                FL.wait(sync, "s", c_sto[0], sto_totals[0])
                FL.wait(sync, "s", c_sto[1], sto_totals[1])
                FL.wait(sync, "s", c_stg[0], stg_silu_totals[0])
                FL.wait(sync, "s", c_stg[1], stg_silu_totals[1])
                wf(sync, "s", f"ostage{j}")
                for hp in range(HP):
                    c_ld5[j].inc(sync.dma_start(ostage[j][:, hp * 128:(hp + 1) * 128],
                                                oT_h.ap()[hp, :, i * 128:(i + 1) * 128]))
                wf(sync, "s", f"gld{j}")
                c_ld5[j].inc(sync.dma_start(gld[j][:], g_h.ap()[i * 128:(i + 1) * 128, :]))
                wf(sync, "s", f"xt{j}")
                c_ld5[j].inc(sync.dma_start(xld[j][:], xb.ap()[i * 128:(i + 1) * 128, :]))
            mark_ld = c_ld5[j].n

            @T
            def _(tensor, i=i, j=j):
                FL.wait(tensor, "t", c_ld5[j], mark_ld)
                FL.wait(tensor, "t", vq, mark_pj)
                FL.wait(tensor, "t", vq, free["psA_s5"][1] if "psA_s5" in free else 0)
                last = None
                for half in range(2):
                    for cc in range(8):
                        last = tensor.matmul(
                            psA[:, half * 512:(half + 1) * 512],
                            ostage[j][:, cc * 128:(cc + 1) * 128],
                            projT[:, cc * C + half * 512: cc * C + half * 512 + 512],
                            start=(cc == 0), stop=(cc == 7))
                tq.inc(last)
            free[f"ostage{j}"] = (tq, tq.n)
            mark_proj = tq.n

            @V
            def _(vector, i=i, j=j):
                FL.wait(vector, "v", tq, mark_proj)
                vq.inc(vector.tensor_add(s1t[:], psA[:], projbB[:]))
                FL.wait(vector, "v", vq, vq.n)
                vq.inc(vector.tensor_mul(s2t[:], s1t[:], gld[j][:]))
                FL.wait(vector, "v", vq, vq.n)
                wf(vector, "v", f"outt{j}")
                vq.inc(vector.tensor_add(outt[j][:], s2t[:], xld[j][:]))
            free["psA_s5"] = (vq, vq.n)
            free[f"gld{j}"] = (vq, vq.n)
            free[f"xt{j}"] = (vq, vq.n)
            mark_out = vq.n

            @GP
            def _(gp, i=i, j=j):
                FL.wait(gp, "g", vq, mark_out)
                c_stu[j].inc(gp.dma_start(out.ap()[i * 128:(i + 1) * 128, :], outt[j][:]))
            free[f"outt{j}"] = (c_stu[j], c_stu[j].n)

        stu_totals = [c_stu[0].n, c_stu[1].n]

        @GP
        def _(gp):
            gp.wait_ge(c_stu[0].sem, stu_totals[0])
            gp.wait_ge(c_stu[1].sem, stu_totals[1])

        @SYNC
        def _(sync):
            sync.wait_ge(c_stu[0].sem, stu_totals[0])
            sync.wait_ge(c_stu[1].sem, stu_totals[1])

    return nc



def _shard_inputs(L0_attn_x, L0_attn_qkv_w, L0_attn_proj_w, L0_attn_proj_b,
                  L0_attn_q_norm_w, L0_attn_k_norm_w, L0_ffn_CB, L0_ffn_SCB,
                  L0_ffn_bias):
    x = np.ascontiguousarray(np.asarray(L0_attn_x, dtype=np.float32))
    B, N, Cc = x.shape
    common = {
        "qkv_w": np.ascontiguousarray(np.asarray(L0_attn_qkv_w, np.float32)),
        "proj_w": np.ascontiguousarray(np.asarray(L0_attn_proj_w, np.float32)),
        "proj_b": np.asarray(L0_attn_proj_b, np.float32).reshape(1, Cc),
        "qnw": np.asarray(L0_attn_q_norm_w, np.float32).reshape(1, DH),
        "knw": np.asarray(L0_attn_k_norm_w, np.float32).reshape(1, DH),
        "cbw": np.ascontiguousarray(np.asarray(L0_ffn_CB, np.float32)),
        "scb": np.asarray(L0_ffn_SCB, np.float32).reshape(1, Cc),
        "fbias": np.asarray(L0_ffn_bias, np.float32).reshape(1, Cc),
        "ident_in": np.eye(128, dtype=np.float32).astype(bfdt),
    }
    in_maps = []
    half = N // 2
    for c in range(8):
        b, h = c // 2, c % 2
        own = x[b, h * half:(h + 1) * half]
        other = x[b, (1 - h) * half:(2 - h) * half]
        m = dict(common)
        m["xb"] = np.ascontiguousarray(np.concatenate([own, other], axis=0))
        in_maps.append(m)
    return in_maps, (B, N, Cc)


_CACHE = {}


def kernel(**inputs):
    in_maps, (B, N, Cc) = _shard_inputs(**inputs)
    key = (N,)
    if key not in _CACHE:
        _CACHE[key] = build(T_ALL=N, T_OWN=N // 2)
    nc = _CACHE[key]
    res = run_bass_kernel_spmd(nc, in_maps, core_ids=list(range(8)))
    half = N // 2
    outp = np.empty((B, N, Cc), dtype=np.float32)
    for c in range(8):
        b, h = c // 2, c % 2
        outp[b, h * half:(h + 1) * half] = res.results[c]["out"]
    return outp


# version 04
# speedup vs baseline: 1.4898x; 1.0828x over previous
import numpy as np
import ml_dtypes
from contextlib import ExitStack

import concourse.bass as bass
import concourse.mybir as mybir
from concourse.bass_utils import run_bass_kernel_spmd

F32 = mybir.dt.float32
BF16 = mybir.dt.bfloat16
AF = mybir.ActivationFunctionType
ALU = mybir.AluOpType
AX = mybir.AxisListType
bfdt = ml_dtypes.bfloat16

C = 1024
DH = 64
HP = 8
DBG = {}


class Chain:
    def __init__(self, sem, step=1):
        self.sem = sem
        self.n = 0
        self.step = step

    def inc(self, inst, k=None):
        k = self.step if k is None else k
        inst.then_inc(self.sem, k)
        self.n += k
        return self.n

    def reserve(self, k=1):
        self.n += k
        return self.n

    def inc_pre(self, inst, k=1):
        inst.then_inc(self.sem, k)
        return inst


class Floors:
    def __init__(self):
        self.f = {}

    def wait(self, eng, key, chain, n):
        if n <= 0:
            return
        d = self.f.setdefault(key, {})
        if n > d.get(id(chain), 0):
            assert n <= chain.n, f"wait {n} > emitted {chain.n} ({key})"
            eng.wait_ge(chain.sem, n)
            d[id(chain)] = n


def build(T_ALL=8192, T_OWN=4096, nstage=99):
    KC = T_ALL // 128
    NSLAB = T_ALL // 512
    NQB = T_OWN // 512
    NOWN = T_OWN // 128
    NG = KC // 2

    nc = bass.Bass("TRN2", target_bir_lowering=False)

    xb = nc.dram_tensor("xb", [T_ALL, C], F32, kind="ExternalInput")
    qkv_w = nc.dram_tensor("qkv_w", [3 * C, C], F32, kind="ExternalInput")
    proj_w = nc.dram_tensor("proj_w", [C, C], F32, kind="ExternalInput")
    proj_b = nc.dram_tensor("proj_b", [1, C], F32, kind="ExternalInput")
    qnw = nc.dram_tensor("qnw", [1, DH], F32, kind="ExternalInput")
    knw = nc.dram_tensor("knw", [1, DH], F32, kind="ExternalInput")
    cbw = nc.dram_tensor("cbw", [C, C], F32, kind="ExternalInput")
    scb = nc.dram_tensor("scb", [1, C], F32, kind="ExternalInput")
    fbias = nc.dram_tensor("fbias", [1, C], F32, kind="ExternalInput")
    ident_in = nc.dram_tensor("ident_in", [128, 128], BF16, kind="ExternalInput")

    out = nc.dram_tensor("out", [T_OWN, C], F32, kind="ExternalOutput")

    xnT_h = nc.dram_tensor("xnT_h", [8, 128, T_ALL], BF16, kind="Internal")
    g_h = nc.dram_tensor("g_h", [T_OWN, C], F32, kind="Internal")
    oT_h = nc.dram_tensor("oT_h", [HP, 128, T_OWN], BF16, kind="Internal")

    with ExitStack() as ctx:
        sb = lambda n_, s_, d_: ctx.enter_context(nc.sbuf_tensor(n_, s_, d_))
        pt = lambda n_, s_, d_: ctx.enter_context(nc.psum_tensor(n_, s_, d_))
        mksem = lambda n_: ctx.enter_context(nc.semaphore(n_))

        identb = sb("identb", [128, 128], BF16)
        onesf = sb("onesf", [1, 128], F32)
        ones8 = sb("ones8", [1, 128], F32)
        ones127 = sb("ones127", [1, 128], F32)
        epsLN = sb("epsLN", [128, 1], F32)
        epsRMS = sb("epsRMS", [128, 1], F32)
        epsTINY = sb("epsTINY", [1, 1], F32)
        wrow = sb("wrow", [1, 256], F32)
        srow = sb("srow", [1, C], F32)
        frow = sb("frow", [1, C], F32)
        prow = sb("prow", [1, C], F32)
        wmul = sb("wmul", [128, 256], F32)
        scbB = sb("scbB", [128, C], F32)
        fbiasB = sb("fbiasB", [128, C], F32)
        projbB = sb("projbB", [128, C], F32)

        xt = [sb(f"xt{j}", [128, C], F32) for j in range(2)]
        lnsum = sb("lnsum", [128, 1], F32)

        xss = sb("xss", [128, 1], F32)
        lnmean = sb("lnmean", [128, 2], F32)
        lnex2 = sb("lnex2", [128, 2], F32)
        lnm2 = sb("lnm2", [128, 2], F32)
        lnvar = sb("lnvar", [128, 2], F32)
        lnl = sb("lnl", [128, 2], F32)
        lnr = sb("lnr", [128, 2], F32)
        xn_tok = [sb(f"xn_tok{j}", [128, C], BF16) for j in range(2)]
        xnT_st = [sb(f"xnT_st{j}", [128, C], BF16) for j in range(2)]

        wstage = sb("wstage", [128, C], F32)
        wstage_b = sb("wstage_b", [128, C], BF16)
        cbT = sb("cbT", [128, 8 * C], BF16)
        projT = cbT
        wt_buf = sb("wt_buf", [128, 8 * 384], BF16)


        gtile = [sb(f"gtile{j}", [128, C], F32) for j in range(2)]
        gld = [sb(f"gld{j}", [128, C], F32) for j in range(2)]

        xslab = [sb(f"xslab{j}", [128, 8 * 512], BF16) for j in range(2)]
        kT = sb("kT", [128, T_ALL], BF16)
        qT = sb("qT", [128, T_OWN], BF16)
        vbuf = sb("vbuf", [128, 2 * KC * 65], BF16)
        qkvf4 = sb("qkvf4", [128, 4 * 384], F32)
        sqt = sb("sqt", [128, 512], F32)
        macc128 = sb("macc128", [128, 4], F32)
        rms16 = sb("rms16", [128, 16], F32)
        rstdL = sb("rstdL", [128, 16], F32)
        rstd16 = sb("rstd16", [128, 16], F32)
        tmn = sb("tmn", [128, 256], F32)
        kq_hat = sb("kq_hat", [128, 256], BF16)
        sqm = sb("sqm", [128, 256], BF16)
        m4 = sb("m4", [128, 4], F32)
        m4b = sb("m4b", [128, 4], BF16)
        mx4 = sb("mx4", [4, 1], F32)
        maxacc = sb("maxacc", [4, 1], F32)
        maxaccb = sb("maxaccb", [4, 1], BF16)
        mrow = sb("mrow", [1, 4], F32)
        mab = sb("mab", [1, 2], F32)
        mln = sb("mln", [1, 2], F32)
        negm2 = sb("negm2", [1, 2], F32)
        negMt = sb("negMt", [128, 2], F32)

        ptb = [sb(f"ptb{j}", [128, 1024], BF16) for j in range(4)]
        d01 = sb("d01", [1, 1024], F32)
        ld01 = sb("ld01", [1, 1024], F32)
        r01 = sb("r01", [1, 1024], F32)
        rb0 = sb("rb0", [64, 512], F32)
        rb1 = sb("rb1", [64, 512], F32)
        obuf = [sb(f"obuf{j}", [128, 512], BF16) for j in range(2)]

        ostage = [sb(f"ostage{j}", [128, 8 * 128], BF16) for j in range(2)]
        xld = xt
        s1t = sb("s1t", [128, C], F32)
        s2t = sb("s2t", [128, C], F32)
        outt = [sb(f"outt{j}", [128, C], F32) for j in range(2)]
        xsq = s1t
        fo = s2t
        fo2 = outt[0]

        psA = pt("psA", [128, 1024], F32)
        psB = pt("psB", [128, 1024], F32)
        psO0 = pt("psO0", [65, 512], F32)
        psO1 = pt("psO1", [65, 512], F32)
        psT0 = pt("psT0", [128, 128], BF16)
        psT1 = pt("psT1", [128, 128], BF16)
        psTs = (psT0, psT1)

        c_ldc = Chain(mksem("ldc"), 16)
        c_ldw = Chain(mksem("ldw"), 16)
        c_ldx = [Chain(mksem(f"ldx{j}"), 16) for j in range(2)]
        c_lds = [Chain(mksem(f"lds{j}"), 16) for j in range(2)]
        c_ldg = [Chain(mksem(f"ldg{j}"), 16) for j in range(2)]
        c_ld5 = [Chain(mksem(f"ld5{j}"), 16) for j in range(2)]
        c_stx = [Chain(mksem(f"stx{j}"), 16) for j in range(2)]
        c_stg = [Chain(mksem(f"stg{j}"), 16) for j in range(2)]
        c_sto = [Chain(mksem(f"sto{j}"), 16) for j in range(2)]
        c_stu = [Chain(mksem(f"stu{j}"), 16) for j in range(2)]
        tq = Chain(mksem("tq"))
        vq = Chain(mksem("vq"))
        aq = Chain(mksem("aq"))

        FL = Floors()
        block = ctx.enter_context(nc.Block())
        SYNC, GP, T, V, A = block.sync, block.gpsimd, block.tensor, block.vector, block.scalar

        free = {
            "xt0": (vq, 0), "xt1": (vq, 0),
            "xn0": (tq, 0), "xn1": (tq, 0),
            "xnTst0_dma": (c_stx[0], 0), "xnTst1_dma": (c_stx[1], 0),
            "xnTst0_ffn": (tq, 0), "xnTst1_ffn": (tq, 0),
            "psB_v": (vq, 0),
            "gtile0": (c_stg[0], 0), "gtile1": (c_stg[1], 0),
            "wstage": (vq, 0), "wstage_b": (tq, 0),
            "psT0": (vq, 0), "psT1": (vq, 0),
            "wt_buf": (tq, 0), "wbT": (tq, 0),
            "xslab0": (tq, 0), "xslab1": (tq, 0),
            "psQ": (vq, 0),
            "kT": (tq, 0), "qT": (tq, 0), "vbuf": (tq, 0),
            "negMt": (aq, 0),
            "maxacc": (vq, 0),
            "gld0": (aq, 0), "gld1": (aq, 0),
            "obuf0": (c_sto[0], 0), "obuf1": (c_sto[1], 0),
            "ostage0": (tq, 0), "ostage1": (tq, 0),

            "outt0": (c_stu[0], 0), "outt1": (c_stu[1], 0),
        }

        def FR(name):
            return free[name]

        def setF(name, chain):
            free[name] = (chain, chain.n)

        def wf(eng, key, name):
            ch, n = free[name]
            FL.wait(eng, key, ch, n)

        @SYNC
        def _(sync):
            c_ldc.inc(sync.dma_start(identb[:], ident_in.ap()))
            c_ldc.inc(sync.dma_start(wrow[0:1, 0:64], knw.ap()))
            c_ldc.inc(sync.dma_start(wrow[0:1, 64:128], knw.ap()))
            c_ldc.inc(sync.dma_start(wrow[0:1, 128:192], qnw.ap()))
            c_ldc.inc(sync.dma_start(wrow[0:1, 192:256], qnw.ap()))
            c_ldc.inc(sync.dma_start(srow[:], scb.ap()))
            c_ldc.inc(sync.dma_start(frow[:], fbias.ap()))
            c_ldc.inc(sync.dma_start(prow[:], proj_b.ap()))
        ld_consts = c_ldc.n

        @V
        def _(vector):
            vq.inc(vector.memset(onesf[:], 1.0))
            vq.inc(vector.memset(ones8[:], 0.125))
            vq.inc(vector.memset(ones127[:], 1.0 / 127.0))
            vq.inc(vector.memset(epsLN[:], 1e-5))
            vq.inc(vector.memset(epsRMS[:], 1e-6))
            vq.inc(vector.memset(epsTINY[:], 1e-30))
        v_consts = vq.n

        if DBG.get("s0_bc", True):
            @T
            def _(tensor):
                FL.wait(tensor, "t", c_ldc, ld_consts)
                FL.wait(tensor, "t", vq, v_consts)
                tensor.matmul(psB[:, 0:128], onesf[:], wrow[0:1, 0:128], start=True, stop=True)
                tq.inc(tensor.matmul(psB[:, 128:256], ones8[:], wrow[0:1, 128:256],
                                     start=True, stop=True))
                tensor.matmul(psA[:, 0:512], ones127[:], srow[0:1, 0:512], start=True, stop=True)
                tq.inc(tensor.matmul(psA[:, 512:1024], ones127[:], srow[0:1, 512:1024],
                                     start=True, stop=True))
            t_bc1 = tq.n

            @V
            def _(vector):
                FL.wait(vector, "v", tq, t_bc1)
                vq.inc(vector.tensor_copy(wmul[:], psB[:, 0:256]))
                vq.inc(vector.tensor_copy(scbB[:], psA[:]))
            v_bc1 = vq.n

            @T
            def _(tensor):
                FL.wait(tensor, "t", vq, v_bc1)
                tensor.matmul(psB[:, 0:512], onesf[:], frow[0:1, 0:512], start=True, stop=True)
                tensor.matmul(psB[:, 512:1024], onesf[:], frow[0:1, 512:1024],
                              start=True, stop=True)
                tensor.matmul(psA[:, 0:512], onesf[:], prow[0:1, 0:512], start=True, stop=True)
                tq.inc(tensor.matmul(psA[:, 512:1024], onesf[:], prow[0:1, 512:1024],
                                     start=True, stop=True))
            t_bc2 = tq.n

            @V
            def _(vector):
                FL.wait(vector, "v", tq, t_bc2)
                vq.inc(vector.tensor_copy(fbiasB[:], psB[:]))
                vq.inc(vector.tensor_copy(projbB[:], psA[:]))

        def load_wT(dst_sb, src_dram, rows0, dst_cc_stride, dst_off, dst_free_key=None):
            @SYNC
            def _(sync):
                wf(sync, "s", "wstage")
                c_ldw.inc(sync.dma_start(wstage[:], src_dram.ap()[rows0:rows0 + 128, :]))
            mark_ld = c_ldw.n

            @V
            def _(vector):
                FL.wait(vector, "v", c_ldw, mark_ld)
                wf(vector, "v", "wstage_b")
                vq.inc(vector.tensor_copy(wstage_b[:], wstage[:]))
            setF("wstage", vq)
            mark_cast = vq.n

            for cc in range(8):
                slot = cc % 2

                @T
                def _(tensor, cc=cc, slot=slot):
                    FL.wait(tensor, "t", vq, mark_cast)
                    wf(tensor, "t", f"psT{slot}")
                    if dst_free_key:
                        wf(tensor, "t", dst_free_key)
                    tq.inc(tensor.transpose(psTs[slot][:],
                                            wstage_b[:, cc * 128:(cc + 1) * 128],
                                            identb[:]))
                mark_t = tq.n
                if cc == 7:
                    setF("wstage_b", tq)

                @V
                def _(vector, cc=cc, slot=slot):
                    FL.wait(vector, "v", tq, mark_t)
                    vq.inc(vector.tensor_copy(
                        dst_sb[:, cc * dst_cc_stride + dst_off:
                               cc * dst_cc_stride + dst_off + 128],
                        psTs[slot][:]))
                free[f"psT{slot}"] = (vq, vq.n)
            return vq.n

        cb_ready = 0
        for r in range(8 if DBG.get("s0_cbt", True) else 0):
            cb_ready = load_wT(cbT, cbw, r * 128, C, r * 128)

        for ip in range((KC // 2) if nstage >= 1 else 0):
            marks_x = [0, 0]
            for j in range(2):
                i = ip * 2 + j

                @SYNC
                def _(sync, i=i, j=j):
                    wf(sync, "s", f"xt{j}")
                    c_ldx[j].inc(sync.dma_start(xt[j][:], xb.ap()[i * 128:(i + 1) * 128, :]))
                marks_x[j] = c_ldx[j].n

            @V
            def _(vector):
                for j in range(2):
                    FL.wait(vector, "v", c_ldx[j], marks_x[j])
                    FL.wait(vector, "v", aq, free["lnstat"][1] if "lnstat" in free else 0)
                    vq.inc(vector.reduce_sum(lnsum[:], xt[j][:], axis=AX.X))
                    vq.inc(vector.tensor_mul(xsq[:], xt[j][:], xt[j][:]))
                    FL.wait(vector, "v", vq, vq.n)
                    vq.inc(vector.reduce_sum(xss[:], xsq[:], axis=AX.X))
                    vq.inc(vector.tensor_scalar_mul(lnmean[:, j:j + 1], lnsum[:], 1.0 / C))
                    FL.wait(vector, "v", vq, vq.n)
                    vq.inc(vector.tensor_scalar_mul(lnex2[:, j:j + 1], xss[:], 1.0 / C))
                FL.wait(vector, "v", vq, vq.n)
                vq.inc(vector.tensor_mul(lnm2[:], lnmean[:], lnmean[:]))
                FL.wait(vector, "v", vq, vq.n)
                vq.inc(vector.tensor_sub(lnvar[:], lnex2[:], lnm2[:]))
            mark_var = vq.n

            @A
            def _(scalar):
                FL.wait(scalar, "a", vq, mark_var)
                aq.inc(scalar.activation(lnl[:], lnvar[:], AF.Ln, bias=epsLN[:], scale=1.0))
                FL.wait(scalar, "a", aq, aq.n)
                aq.inc(scalar.activation(lnr[:], lnl[:], AF.Exp, scale=-0.5))
            mark_rstd = aq.n
            free["lnstat"] = (aq, aq.n)

            for j in range(2):
                i = ip * 2 + j
                own = i < NOWN

                @V
                def _(vector, i=i, j=j):
                    FL.wait(vector, "v", aq, mark_rstd)
                    wf(vector, "v", f"xn{j}")
                    vq.inc(vector.tensor_scalar(xn_tok[j][:], xt[j][:],
                                                lnmean[:, j:j + 1], lnr[:, j:j + 1],
                                                ALU.subtract, ALU.mult))
                mark_xn = vq.n
                free[f"xt{j}"] = (vq, vq.n)

                for cc in range(8):
                    slot = cc % 2

                    @T
                    def _(tensor, cc=cc, j=j, slot=slot):
                        FL.wait(tensor, "t", vq, mark_xn)
                        wf(tensor, "t", f"psT{slot}")
                        wf(tensor, "t", f"xnTst{j}_dma")
                        wf(tensor, "t", f"xnTst{j}_ffn")
                        tq.inc(tensor.transpose(psTs[slot][:],
                                                xn_tok[j][:, cc * 128:(cc + 1) * 128],
                                                identb[:]))
                    mark_t = tq.n
                    if cc == 7:
                        free[f"xn{j}"] = (tq, tq.n)

                    @V
                    def _(vector, cc=cc, j=j, slot=slot):
                        FL.wait(vector, "v", tq, mark_t)
                        vq.inc(vector.tensor_copy(xnT_st[j][:, cc * 128:(cc + 1) * 128],
                                                  psTs[slot][:]))
                    free[f"psT{slot}"] = (vq, vq.n)
                mark_xnT = vq.n

                @GP
                def _(gp, i=i, j=j):
                    FL.wait(gp, "g", vq, mark_xnT)
                    c_stx[j].inc(gp.dma_start(
                        xnT_h.ap()[:, :, i * 128:(i + 1) * 128].rearrange("c p t -> p c t"),
                        xnT_st[j].rearrange("p (c t) -> p c t", t=128)))
                free[f"xnTst{j}_dma"] = (c_stx[j], c_stx[j].n)

                if own:
                    @T
                    def _(tensor, j=j):
                        FL.wait(tensor, "t", vq, mark_xnT)
                        FL.wait(tensor, "t", vq, cb_ready)
                        wf(tensor, "t", "psB_v")
                        last = None
                        for half in range(2):
                            for cc in range(8):
                                last = tensor.matmul(
                                    psB[:, half * 512:(half + 1) * 512],
                                    xnT_st[j][:, cc * 128:(cc + 1) * 128],
                                    cbT[:, cc * C + half * 512: cc * C + half * 512 + 512],
                                    start=(cc == 0), stop=(cc == 7))
                        tq.inc(last)
                    mark_ffn = tq.n
                    free[f"xnTst{j}_ffn"] = (tq, tq.n)

                    @V
                    def _(vector, j=j):
                        FL.wait(vector, "v", tq, mark_ffn)
                        vq.inc(vector.tensor_mul(fo[:], psB[:], scbB[:]))
                        FL.wait(vector, "v", vq, vq.n)
                        vq.inc(vector.tensor_add(fo2[:], fo[:], fbiasB[:]))
                        FL.wait(vector, "v", vq, vq.n)
                        wf(vector, "v", f"gtile{j}")
                        vq.inc(vector.tensor_copy(gtile[j][:], fo2[:]))
                    free["psB_v"] = (vq, vq.n)
                    mark_g = vq.n

                    @GP
                    def _(gp, i=i, j=j):
                        FL.wait(gp, "g", vq, mark_g)
                        c_stg[j].inc(gp.dma_start(g_h.ap()[i * 128:(i + 1) * 128, :], gtile[j][:]))
                    free[f"gtile{j}"] = (c_stg[j], c_stg[j].n)

        stg_totals = [c_stg[0].n, c_stg[1].n]
        stx_totals = [c_stx[0].n, c_stx[1].n]
        free["wbT"] = (tq, tq.n)
        free["outt0"] = (vq, vq.n)
        free["xt0"] = (vq, vq.n)
        free["xt1"] = (vq, vq.n)

        for i in range(NOWN if nstage >= 2 else 0):
            j = i % 2

            @SYNC
            def _(sync, i=i, j=j):
                FL.wait(sync, "s", c_stg[0], stg_totals[0])
                FL.wait(sync, "s", c_stg[1], stg_totals[1])
                wf(sync, "s", f"gld{j}")
                c_ldg[j].inc(sync.dma_start(gld[j][:], g_h.ap()[i * 128:(i + 1) * 128, :]))
            mark_g = c_ldg[j].n

            @A
            def _(scalar, j=j):
                FL.wait(scalar, "a", c_ldg[j], mark_g)
                wf(scalar, "a", f"gtile{j}")
                aq.inc(scalar.activation(gtile[j][:], gld[j][:], AF.Silu, scale=1.0))
            free[f"gld{j}"] = (aq, aq.n)
            mark_s = aq.n

            @GP
            def _(gp, i=i, j=j):
                FL.wait(gp, "g", aq, mark_s)
                c_stg[j].inc(gp.dma_start(g_h.ap()[i * 128:(i + 1) * 128, :], gtile[j][:]))
            free[f"gtile{j}"] = (c_stg[j], c_stg[j].n)

        stg_silu_totals = [c_stg[0].n, c_stg[1].n]

        free["rms16"] = (aq, 0)
        free["qkvf4"] = (vq, 0)
        prev_ofree = (vq, 0)
        prev_bankfree = (vq, 0)
        last_expA = 0
        last_expB = 0
        pv_last = {0: 0, 1: 0, 2: 0, 3: 0}

        for hp in range(HP if nstage >= 3 else 0):
            aq_hpstart = aq.n
            for gidx, rows0 in enumerate((C + hp * 128, hp * 128, 2 * C + hp * 128)):
                load_wT(wt_buf, qkv_w, rows0, 384, gidx * 128, dst_free_key="wt_buf")
            mark_wt = vq.n

            @V
            def _(vector):
                wf(vector, "v", "maxacc")
                vq.inc(vector.memset(macc128[:], 0.0))
                wf(vector, "v", "vbuf")
                vq.inc(vector.memset(
                    vbuf.rearrange("p (c j) -> p c j", j=65)[:, :, 64:65], 1.0))
            mark_vones = vq.n

            for s in range(NSLAB):
                jsl = s % 2

                @SYNC
                def _(sync, s=s, jsl=jsl):
                    wf(sync, "s", f"xslab{jsl}")
                    FL.wait(sync, "s", c_stx[0], stx_totals[0])
                    FL.wait(sync, "s", c_stx[1], stx_totals[1])
                    for cc in range(8):
                        c_lds[jsl].inc(sync.dma_start(
                            xslab[jsl][:, cc * 512:(cc + 1) * 512],
                            xnT_h.ap()[cc, :, s * 512:(s + 1) * 512]))
                mark_slab = c_lds[jsl].n

                for half in range(2):
                    c0 = half * 2
                    i0 = s * 4 + c0

                    @T
                    def _(tensor, c0=c0, jsl=jsl):
                        FL.wait(tensor, "t", c_lds[jsl], mark_slab)
                        FL.wait(tensor, "t", vq, mark_wt)
                        wf(tensor, "t", "psQ")
                        FL.wait(tensor, "t", aq, aq_hpstart)
                        last = None
                        for dc in range(2):
                            for cc in range(8):
                                last = tensor.matmul(
                                    psB[:, dc * 512: dc * 512 + 384],
                                    xslab[jsl][:, cc * 512 + (c0 + dc) * 128:
                                               cc * 512 + (c0 + dc) * 128 + 128],
                                    wt_buf[:, cc * 384: cc * 384 + 384],
                                    start=(cc == 0), stop=(cc == 7))
                        tq.inc(last)
                    mark_qkv = tq.n
                    if s == NSLAB - 1 and half == 1:
                        setF("wt_buf", tq)

                    @V
                    def _(vector, c0=c0, i0=i0):
                        FL.wait(vector, "v", tq, mark_qkv)
                        wf(vector, "v", "qkvf4")
                        vq.inc(vector.tensor_copy(
                            qkvf4.rearrange("p (d f) -> p d f", f=384)[:, c0:c0 + 2, :],
                            psB.rearrange("p (d f) -> p d f", f=512)[:, 0:2, 0:384]))
                        FL.wait(vector, "v", vq, vq.n)
                        vq.inc(vector.tensor_mul(
                            sqt.rearrange("p (d f) -> p d f", f=256),
                            qkvf4.rearrange("p (d f) -> p d f", f=384)[:, c0:c0 + 2, 0:256],
                            qkvf4.rearrange("p (d f) -> p d f", f=384)[:, c0:c0 + 2, 0:256]))
                        FL.wait(vector, "v", vq, vq.n)
                        wf(vector, "v", "rms16")
                        vq.inc(vector.reduce_sum(rms16[:, c0 * 4:(c0 + 2) * 4],
                                                 sqt.rearrange("p (g d) -> p g d", d=DH),
                                                 axis=AX.X))
                    free["psQ"] = (vq, vq.n)
                mark_rms = vq.n
                free[f"xslab{jsl}"] = (tq, tq.n)

                @A
                def _(scalar):
                    FL.wait(scalar, "a", vq, mark_rms)
                    aq.inc(scalar.activation(rstdL[:], rms16[:], AF.Ln,
                                             bias=epsRMS[:], scale=1.0 / DH))
                    FL.wait(scalar, "a", aq, aq.n)
                    aq.inc(scalar.activation(rstd16[:], rstdL[:], AF.Exp, scale=-0.5))
                free["rms16"] = (aq, aq.n)
                mark_rstd = aq.n

                for c4 in range(4):
                    i = s * 4 + c4
                    own = i < NOWN

                    @V
                    def _(vector, c4=c4, i=i):
                        FL.wait(vector, "v", aq, mark_rstd)
                        base = c4 * 384
                        rb_ = c4 * 4
                        for g_ in range(4):
                            vq.inc(vector.tensor_scalar(
                                tmn[:, g_ * 64:(g_ + 1) * 64],
                                qkvf4[:, base + g_ * 64: base + (g_ + 1) * 64],
                                rstd16[:, rb_ + g_: rb_ + g_ + 1], None, ALU.mult))
                        FL.wait(vector, "v", vq, vq.n)
                        vq.inc(vector.tensor_mul(kq_hat[:], tmn[:], wmul[:]))
                        wf(vector, "v", "vbuf")
                        dst = vbuf.rearrange("p (h k j) -> p h k j", h=2, j=65)[
                            :, :, i:i + 1, 0:64].rearrange("p h k j -> p (h k) j")
                        srcv = qkvf4[:, base + 256: base + 384].rearrange(
                            "p (h j) -> p h j", h=2)
                        vq.inc(vector.tensor_copy(dst, srcv))
                    mark_kqv = vq.n
                    if c4 == 3:
                        free["qkvf4"] = (vq, vq.n)

                    @T
                    def _(tensor, i=i, own=own):
                        FL.wait(tensor, "t", vq, mark_kqv)
                        wf(tensor, "t", "psT0")
                        wf(tensor, "t", "kT")
                        tq.inc(tensor.transpose(psT0[:], kq_hat[:, 0:128], identb[:]))
                        if own:
                            wf(tensor, "t", "psT1")
                            wf(tensor, "t", "qT")
                            tq.inc(tensor.transpose(psT1[:], kq_hat[:, 128:256],
                                                    identb[:]))
                    mark_tp = tq.n

                    @V
                    def _(vector, i=i, own=own):
                        FL.wait(vector, "v", tq, mark_tp)
                        vq.inc(vector.tensor_copy(kT[:, i * 128:(i + 1) * 128], psT0[:]))
                        free["psT0"] = (vq, vq.n)
                        if own:
                            vq.inc(vector.tensor_copy(qT[:, i * 128:(i + 1) * 128],
                                                      psT1[:]))
                            free["psT1"] = (vq, vq.n)
                        vq.inc(vector.tensor_mul(sqm[:], kq_hat[:], kq_hat[:]))
                        FL.wait(vector, "v", vq, vq.n)
                        vq.inc(vector.reduce_sum(m4[:], sqm.rearrange("p (g d) -> p g d", d=DH),
                                                 axis=AX.X))
                        FL.wait(vector, "v", vq, vq.n)
                        wf(vector, "v", "maxacc")
                        if own:
                            vq.inc(vector.tensor_max(macc128[:], macc128[:], m4[:]))
                        else:
                            vq.inc(vector.tensor_max(macc128[:, 0:2], macc128[:, 0:2],
                                                     m4[:, 0:2]))
                        free["maxacc"] = (vq, vq.n)

            vq_all = vq.n

            @V
            def _(vector):
                FL.wait(vector, "v", vq, vq_all)
                vq.inc(vector.tensor_copy(m4b[:], macc128[:]))
            mark_mab = vq.n

            @T
            def _(tensor):
                FL.wait(tensor, "t", vq, mark_mab)
                wf(tensor, "t", "psT0")
                tq.inc(tensor.transpose(psT0[0:4, 0:128], m4b[:], identb[:]))
            mark_mat = tq.n

            @V
            def _(vector):
                FL.wait(vector, "v", tq, mark_mat)
                vq.inc(vector.reduce_max(mx4[:], psT0[0:4, 0:128], axis=AX.X))
                free["psT0"] = (vq, vq.n)
                FL.wait(vector, "v", vq, vq.n)
                vq.inc(vector.tensor_copy(maxaccb[:], mx4[:]))
                FL.wait(vector, "v", vq, vq.n)
                vq.inc(vector.tensor_copy(maxacc[:], maxaccb[:]))
            mark_mab2 = vq.n

            @T
            def _(tensor):
                FL.wait(tensor, "t", vq, mark_mab2)
                wf(tensor, "t", "psT0")
                tq.inc(tensor.transpose(psT0[0:1, 0:4], maxaccb[:], identb[0:4, 0:4]))
            mark_mat2 = tq.n

            @V
            def _(vector):
                FL.wait(vector, "v", tq, mark_mat2)
                vq.inc(vector.tensor_copy(mrow[:], psT0[0:1, 0:4]))
                free["psT0"] = (vq, vq.n)
                FL.wait(vector, "v", vq, vq.n)
                vq.inc(vector.tensor_mul(mab[:], mrow[0:1, 0:2], mrow[0:1, 2:4]))
            mark_ab = vq.n

            @V
            def _(vector):
                FL.wait(vector, "v", vq, mark_ab)
                vq.inc(vector.reduce_max(mab[0:1, 0:1], mab[:], axis=AX.X))
            mark_ab2 = vq.n

            @A
            def _(scalar):
                FL.wait(scalar, "a", vq, mark_ab2)
                aq.inc(scalar.activation(mln[0:1, 0:1], mab[0:1, 0:1], AF.Ln,
                                         bias=epsTINY[:], scale=1.0))
                FL.wait(scalar, "a", aq, aq.n)
                aq.inc(scalar.activation(negm2[0:1, 0:1], mln[0:1, 0:1], AF.Exp, scale=0.5))
            mark_mh = aq.n

            @T
            def _(tensor):
                FL.wait(tensor, "t", aq, mark_mh)
                wf(tensor, "t", "psQ")
                tq.inc(tensor.matmul(psB[:, 384:385], onesf[:], negm2[0:1, 0:1],
                                     start=True, stop=True))
            mark_mq = tq.n

            @V
            def _(vector):
                FL.wait(vector, "v", tq, mark_mq)
                wf(vector, "v", "negMt")
                vq.inc(vector.tensor_scalar_mul(negMt[:, 0:1], psB[:, 384:385], -1.0))
            free["psQ"] = (vq, vq.n)
            mark_negM = vq.n

            for qb in range(NQB if nstage >= 4 else 0):
                aq_base = aq.n
                exp_mark = {kc: aq_base + kc + 1 for kc in range(KC)}
                aq.reserve(KC)
                mark_r = aq.reserve(2)

                s_mark = {}
                pv_mark = {}
                psS = (psA, psB)

                @T
                def _(tensor, qb=qb):
                    FL.wait(tensor, "t", vq, mark_negM)
                    FL.wait(tensor, "t", vq, prev_bankfree[1])
                    for kc in range(KC + 1):
                        if kc < KC:
                            par = kc % 2
                            FL.wait(tensor, "t", aq, exp_mark.get(kc - 2, 0))
                            tensor.matmul(psS[par][:, 0:512],
                                          kT[0:64, kc * 128:(kc + 1) * 128],
                                          qT[0:64, qb * 512:(qb + 1) * 512],
                                          start=True, stop=True)
                            tq.inc(tensor.matmul(psS[par][:, 512:1024],
                                                 kT[64:128, kc * 128:(kc + 1) * 128],
                                                 qT[64:128, qb * 512:(qb + 1) * 512],
                                                 start=True, stop=True, tile_position=(64, 0)))
                            s_mark[kc] = tq.n
                        if kc >= 1:
                            kp = kc - 1
                            par = kp % 2
                            FL.wait(tensor, "t", aq, exp_mark[kp])
                            FL.wait(tensor, "t", vq, prev_ofree[1])
                            tensor.matmul(psO0[:],
                                          vbuf[:, kp * 65: kp * 65 + 65],
                                          ptb[par][:, 0:512],
                                          start=(kp == 0), stop=(kp == KC - 1))
                            tq.inc(tensor.matmul(psO1[:],
                                                 vbuf[:, (KC + kp) * 65: (KC + kp) * 65 + 65],
                                                 ptb[par][:, 512:1024],
                                                 start=(kp == 0), stop=(kp == KC - 1)))
                            pv_mark[kp] = tq.n
                    FL.wait(tensor, "t", aq, mark_r)
                    tensor.matmul(psA[0:64, 0:512], onesf[0:1, 0:64], r01[0:1, 0:512],
                                  start=True, stop=True)
                    tq.inc(tensor.matmul(psB[0:64, 0:512], onesf[0:1, 0:64],
                                         r01[0:1, 512:1024], start=True, stop=True))
                mark_pv_last = pv_mark[KC - 1]
                mark_bc = tq.n

                @V
                def _(vector, qb=qb, hp=hp):
                    FL.wait(vector, "v", tq, mark_pv_last)
                    vq.inc(vector.tensor_copy(d01[0:1, 0:512], psO0[64:65, :]))
                    vq.inc(vector.tensor_copy(d01[0:1, 512:1024], psO1[64:65, :]))
                    FL.wait(vector, "v", tq, mark_bc)
                    vq.inc(vector.tensor_copy(rb0[:], psA[0:64, 0:512]))
                    vq.inc(vector.tensor_copy(rb1[:], psB[0:64, 0:512]))
                    FL.wait(vector, "v", vq, vq.n)
                    jo = qb % 2
                    wf(vector, "v", f"obuf{jo}")
                    vq.inc(vector.tensor_mul(obuf[jo][0:64, :], psO0[0:64, :], rb0[:]))
                    vq.inc(vector.tensor_mul(obuf[jo][64:128, :], psO1[0:64, :], rb1[:]))
                mark_d = vq.n - 4
                prev_bankfree = (vq, vq.n)
                prev_ofree = (vq, vq.n)
                mark_ob = vq.n

                @A
                def _(scalar, qb=qb):
                    FL.wait(scalar, "a", vq, mark_negM)
                    for kc in range(KC):
                        par = kc % 2
                        FL.wait(scalar, "a", tq, s_mark[kc])
                        FL.wait(scalar, "a", tq, pv_mark.get(kc - 2, 0))
                        aq.inc_pre(scalar.activation(ptb[par][:], psS[par][:], AF.Exp,
                                                     bias=negMt[:, 0:1], scale=1.0))
                    FL.wait(scalar, "a", vq, mark_d)
                    aq.inc_pre(scalar.activation(ld01[:], d01[:], AF.Ln, scale=1.0))
                    FL.wait(scalar, "a", aq, mark_r - 1)
                    aq.inc_pre(scalar.activation(r01[:], ld01[:], AF.Exp, scale=-1.0))
                FL.f.setdefault("a", {})[id(aq)] = max(
                    FL.f.get("a", {}).get(id(aq), 0), mark_r)

                @GP
                def _(gp, qb=qb, hp=hp):
                    FL.wait(gp, "g", vq, mark_ob)
                    jo = qb % 2
                    c_sto[jo].inc(gp.dma_start(oT_h.ap()[hp, :, qb * 512:(qb + 1) * 512],
                                               obuf[jo][:]))
                free[f"obuf{qb % 2}"] = (c_sto[qb % 2], c_sto[qb % 2].n)

            setF("kT", tq)
            setF("qT", tq)
            setF("vbuf", tq)
            free["negMt"] = (aq, aq.n)

        sto_totals = [c_sto[0].n, c_sto[1].n]

        for r in range(8 if DBG.get("s5_prep", True) else 0):
            load_wT(projT, proj_w, r * 128, C, r * 128, dst_free_key="wbT")
        mark_pj = vq.n

        for i in range(NOWN if nstage >= 5 else 0):
            j = i % 2

            @SYNC
            def _(sync, i=i, j=j):
                FL.wait(sync, "s", c_sto[0], sto_totals[0])
                FL.wait(sync, "s", c_sto[1], sto_totals[1])
                FL.wait(sync, "s", c_stg[0], stg_silu_totals[0])
                FL.wait(sync, "s", c_stg[1], stg_silu_totals[1])
                wf(sync, "s", f"ostage{j}")
                for hp in range(HP):
                    c_ld5[j].inc(sync.dma_start(ostage[j][:, hp * 128:(hp + 1) * 128],
                                                oT_h.ap()[hp, :, i * 128:(i + 1) * 128]))
                wf(sync, "s", f"gld{j}")
                c_ld5[j].inc(sync.dma_start(gld[j][:], g_h.ap()[i * 128:(i + 1) * 128, :]))
                wf(sync, "s", f"xt{j}")
                c_ld5[j].inc(sync.dma_start(xld[j][:], xb.ap()[i * 128:(i + 1) * 128, :]))
            mark_ld = c_ld5[j].n

            @T
            def _(tensor, i=i, j=j):
                FL.wait(tensor, "t", c_ld5[j], mark_ld)
                FL.wait(tensor, "t", vq, mark_pj)
                FL.wait(tensor, "t", vq, free["psA_s5"][1] if "psA_s5" in free else 0)
                last = None
                for half in range(2):
                    for cc in range(8):
                        last = tensor.matmul(
                            psA[:, half * 512:(half + 1) * 512],
                            ostage[j][:, cc * 128:(cc + 1) * 128],
                            projT[:, cc * C + half * 512: cc * C + half * 512 + 512],
                            start=(cc == 0), stop=(cc == 7))
                tq.inc(last)
            free[f"ostage{j}"] = (tq, tq.n)
            mark_proj = tq.n

            @V
            def _(vector, i=i, j=j):
                FL.wait(vector, "v", tq, mark_proj)
                vq.inc(vector.tensor_add(s1t[:], psA[:], projbB[:]))
                FL.wait(vector, "v", vq, vq.n)
                vq.inc(vector.tensor_mul(s2t[:], s1t[:], gld[j][:]))
                FL.wait(vector, "v", vq, vq.n)
                wf(vector, "v", f"outt{j}")
                vq.inc(vector.tensor_add(outt[j][:], s2t[:], xld[j][:]))
            free["psA_s5"] = (vq, vq.n)
            free[f"gld{j}"] = (vq, vq.n)
            free[f"xt{j}"] = (vq, vq.n)
            mark_out = vq.n

            @GP
            def _(gp, i=i, j=j):
                FL.wait(gp, "g", vq, mark_out)
                c_stu[j].inc(gp.dma_start(out.ap()[i * 128:(i + 1) * 128, :], outt[j][:]))
            free[f"outt{j}"] = (c_stu[j], c_stu[j].n)

        stu_totals = [c_stu[0].n, c_stu[1].n]

        @GP
        def _(gp):
            gp.wait_ge(c_stu[0].sem, stu_totals[0])
            gp.wait_ge(c_stu[1].sem, stu_totals[1])

        @SYNC
        def _(sync):
            sync.wait_ge(c_stu[0].sem, stu_totals[0])
            sync.wait_ge(c_stu[1].sem, stu_totals[1])

    return nc



def _shard_inputs(L0_attn_x, L0_attn_qkv_w, L0_attn_proj_w, L0_attn_proj_b,
                  L0_attn_q_norm_w, L0_attn_k_norm_w, L0_ffn_CB, L0_ffn_SCB,
                  L0_ffn_bias):
    x = np.ascontiguousarray(np.asarray(L0_attn_x, dtype=np.float32))
    B, N, Cc = x.shape
    common = {
        "qkv_w": np.ascontiguousarray(np.asarray(L0_attn_qkv_w, np.float32)),
        "proj_w": np.ascontiguousarray(np.asarray(L0_attn_proj_w, np.float32)),
        "proj_b": np.asarray(L0_attn_proj_b, np.float32).reshape(1, Cc),
        "qnw": np.asarray(L0_attn_q_norm_w, np.float32).reshape(1, DH),
        "knw": np.asarray(L0_attn_k_norm_w, np.float32).reshape(1, DH),
        "cbw": np.ascontiguousarray(np.asarray(L0_ffn_CB, np.float32)),
        "scb": np.asarray(L0_ffn_SCB, np.float32).reshape(1, Cc),
        "fbias": np.asarray(L0_ffn_bias, np.float32).reshape(1, Cc),
        "ident_in": np.eye(128, dtype=np.float32).astype(bfdt),
    }
    in_maps = []
    half = N // 2
    for c in range(8):
        b, h = c // 2, c % 2
        own = x[b, h * half:(h + 1) * half]
        other = x[b, (1 - h) * half:(2 - h) * half]
        m = dict(common)
        m["xb"] = np.ascontiguousarray(np.concatenate([own, other], axis=0))
        in_maps.append(m)
    return in_maps, (B, N, Cc)


_CACHE = {}


def kernel(**inputs):
    in_maps, (B, N, Cc) = _shard_inputs(**inputs)
    key = (N,)
    if key not in _CACHE:
        _CACHE[key] = build(T_ALL=N, T_OWN=N // 2)
    nc = _CACHE[key]
    res = run_bass_kernel_spmd(nc, in_maps, core_ids=list(range(8)))
    half = N // 2
    outp = np.empty((B, N, Cc), dtype=np.float32)
    for c in range(8):
        b, h = c // 2, c % 2
        outp[b, h * half:(h + 1) * half] = res.results[c]["out"]
    return outp


# version 06
# speedup vs baseline: 1.5559x; 1.0444x over previous
import numpy as np
import ml_dtypes
from contextlib import ExitStack

import concourse.bass as bass
import concourse.mybir as mybir
from concourse.bass_utils import run_bass_kernel_spmd

F32 = mybir.dt.float32
BF16 = mybir.dt.bfloat16
AF = mybir.ActivationFunctionType
ALU = mybir.AluOpType
AX = mybir.AxisListType
bfdt = ml_dtypes.bfloat16

C = 1024
DH = 64
HP = 8
DBG = {}


class Chain:
    def __init__(self, sem, step=1):
        self.sem = sem
        self.n = 0
        self.step = step

    def inc(self, inst, k=None):
        k = self.step if k is None else k
        inst.then_inc(self.sem, k)
        self.n += k
        return self.n

    def reserve(self, k=1):
        self.n += k
        return self.n

    def inc_pre(self, inst, k=1):
        inst.then_inc(self.sem, k)
        return inst


class Floors:
    def __init__(self):
        self.f = {}

    def wait(self, eng, key, chain, n):
        if n <= 0:
            return
        d = self.f.setdefault(key, {})
        if n > d.get(id(chain), 0):
            assert n <= chain.n, f"wait {n} > emitted {chain.n} ({key})"
            eng.wait_ge(chain.sem, n)
            d[id(chain)] = n


def build(T_ALL=8192, T_OWN=4096, nstage=99):
    KC = T_ALL // 128
    NSLAB = T_ALL // 512
    NQB = T_OWN // 512
    NOWN = T_OWN // 128
    NG = KC // 2

    nc = bass.Bass("TRN2", target_bir_lowering=False)

    xb = nc.dram_tensor("xb", [T_ALL, C], F32, kind="ExternalInput")
    qkv_w = nc.dram_tensor("qkv_w", [3 * C, C], F32, kind="ExternalInput")
    proj_w = nc.dram_tensor("proj_w", [C, C], F32, kind="ExternalInput")
    proj_b = nc.dram_tensor("proj_b", [1, C], F32, kind="ExternalInput")
    qnw = nc.dram_tensor("qnw", [1, DH], F32, kind="ExternalInput")
    knw = nc.dram_tensor("knw", [1, DH], F32, kind="ExternalInput")
    cbw = nc.dram_tensor("cbw", [C, C], F32, kind="ExternalInput")
    scb = nc.dram_tensor("scb", [1, C], F32, kind="ExternalInput")
    fbias = nc.dram_tensor("fbias", [1, C], F32, kind="ExternalInput")
    ident_in = nc.dram_tensor("ident_in", [128, 128], BF16, kind="ExternalInput")

    out = nc.dram_tensor("out", [T_OWN, C], F32, kind="ExternalOutput")

    xnT_h = nc.dram_tensor("xnT_h", [8, 128, T_ALL], BF16, kind="Internal")
    g_h = nc.dram_tensor("g_h", [T_OWN, C], F32, kind="Internal")
    oT_h = nc.dram_tensor("oT_h", [HP, 128, T_OWN], BF16, kind="Internal")

    with ExitStack() as ctx:
        sb = lambda n_, s_, d_: ctx.enter_context(nc.sbuf_tensor(n_, s_, d_))
        pt = lambda n_, s_, d_: ctx.enter_context(nc.psum_tensor(n_, s_, d_))
        mksem = lambda n_: ctx.enter_context(nc.semaphore(n_))

        identb = sb("identb", [128, 128], BF16)
        onesf = sb("onesf", [1, 128], F32)
        epsLN = sb("epsLN", [128, 1], F32)
        epsRMS = sb("epsRMS", [128, 1], F32)
        epsTINY = sb("epsTINY", [1, 1], F32)

        brow = sb("brow", [1, C], F32)
        wmul = sb("wmul", [128, 256], F32)
        
        scbB = sb("scbB", [128, C], F32)
        fbiasB = sb("fbiasB", [128, C], F32)
        projbB = sb("projbB", [128, C], F32)

        xt = [sb(f"xt{j}", [128, C], F32) for j in range(2)]
        lnsum = sb("lnsum", [128, 1], F32)

        xss = sb("xss", [128, 1], F32)
        lnmean = sb("lnmean", [128, 2], F32)
        lnex2 = sb("lnex2", [128, 2], F32)
        lnm2 = sb("lnm2", [128, 2], F32)
        lnvar = sb("lnvar", [128, 2], F32)
        lnl = sb("lnl", [128, 2], F32)
        lnr = sb("lnr", [128, 2], F32)
        xn_tok = None

        wstage = sb("wstage", [128, C], F32)
        wstage_b = sb("wstage_b", [128, C], BF16)
        cbT = sb("cbT", [128, 8 * C], BF16)
        projT = cbT
        wt_buf = sb("wt_buf", [128, 8 * 384], BF16)


        gld = [sb(f"gld{j}", [128, C], F32) for j in range(2)]
        wrow = gld[1][0:1, 0:256]
        ones8 = gld[0][0:1, 0:128]
        ones127 = gld[0][0:1, 128:256]

        xslab = [sb(f"xslab{j}", [128, 8 * 512], BF16) for j in range(2)]
        kTp = [sb(f"kT{p}", [128, T_ALL], BF16) for p in range(2)]
        qTp = [sb(f"qT{p}", [128, T_OWN], BF16) for p in range(2)]
        vbufp = [sb(f"vbuf{p}", [128, 2 * KC * 65], BF16) for p in range(2)]
        if 2 * KC * 65 >= 2 * C:
            xn_tok = [vbufp[0][:, 0:C], vbufp[0][:, C:2 * C]]
        else:
            _xn = [sb(f"xn_tok{j}", [128, C], BF16) for j in range(2)]
            xn_tok = [_xn[0][:], _xn[1][:]]
        NLAG = min(4, NSLAB)
        qkvlag = sb("qkvlag", [128, NLAG * 4 * 384], BF16)
        sqt = sb("sqt", [128, 256], F32)
        macc128 = sb("macc128", [128, 4], F32)
        rms16 = sb("rms16", [128, 16], F32)
        rstdL = sb("rstdL", [128, 16], F32)
        rstd16 = sb("rstd16", [128, 16], F32)
        tmn = sb("tmn", [128, 256], F32)
        kq_hat = sb("kq_hat", [128, 256], BF16)
        sqm = sb("sqm", [128, 256], BF16)
        m4 = sb("m4", [128, 4], F32)
        m4b = sb("m4b", [128, 4], BF16)
        mx4 = sb("mx4", [4, 1], F32)
        maxaccb = sb("maxaccb", [4, 1], BF16)
        mrow = sb("mrow", [1, 4], F32)
        mab = sb("mab", [1, 2], F32)
        mln = sb("mln", [1, 2], F32)
        negm2 = sb("negm2", [1, 2], F32)
        negMt = sb("negMt", [128, 2], F32)

        ptb = [sb(f"ptb{j}", [128, 1024], BF16) for j in range(2)]
        xnT_st = ptb

        rb0 = sb("rb0", [64, 512], F32)
        rb1 = sb("rb1", [64, 512], F32)
        obuf = [sb(f"obuf{j}", [128, 512], BF16) for j in range(2)]

        ostage = [sb(f"ostage{j}", [128, 8 * 128], BF16) for j in range(2)]
        xld = xt
        s1t = sb("s1t", [128, C], F32)
        s2t = sb("s2t", [128, C], F32)
        outt = [sb(f"outt{j}", [128, C], F32) for j in range(2)]
        xsq = s1t
        fo = s2t
        fo2 = outt[0]

        psA = pt("psA", [128, 1024], F32)
        psB = pt("psB", [128, 1024], F32)
        psO0 = pt("psO0", [65, 512], F32)
        psO1 = pt("psO1", [65, 512], F32)
        psT0 = pt("psT0", [128, 128], BF16)
        psQK = pt("psQK", [128, 512], F32)
        psT1 = psT0
        psTs = (psT0, psT0)

        c_ldc = Chain(mksem("ldc"), 16)
        c_ldw = Chain(mksem("ldw"), 16)
        c_ldx = [Chain(mksem(f"ldx{j}"), 16) for j in range(2)]
        c_lds = [Chain(mksem(f"lds{j}"), 16) for j in range(2)]
        c_ldg = [Chain(mksem(f"ldg{j}"), 16) for j in range(2)]
        c_ld5 = [Chain(mksem(f"ld5{j}"), 16) for j in range(2)]
        c_stx = [Chain(mksem(f"stx{j}"), 16) for j in range(2)]
        c_stg = [Chain(mksem(f"stg{j}"), 16) for j in range(2)]
        c_sto = [Chain(mksem(f"sto{j}"), 16) for j in range(2)]
        c_stu = [Chain(mksem(f"stu{j}"), 16) for j in range(2)]
        tq = Chain(mksem("tq"))
        vq = Chain(mksem("vq"))
        aq = Chain(mksem("aq"))

        FL = Floors()
        block = ctx.enter_context(nc.Block())
        SYNC, GP, T, V, A = block.sync, block.gpsimd, block.tensor, block.vector, block.scalar

        free = {
            "xt0": (vq, 0), "xt1": (vq, 0),
            "xn0": (tq, 0), "xn1": (tq, 0),
            "xnTst0_dma": (c_stx[0], 0), "xnTst1_dma": (c_stx[1], 0),
            "xnTst0_ffn": (tq, 0), "xnTst1_ffn": (tq, 0),
            "psB_v": (vq, 0),
            "fo2_st": (c_stg[0], 0),
            "wstage": (vq, 0), "wstage_b": (tq, 0),
            "psT0": (vq, 0),
            "wt_buf": (tq, 0), "wbT": (tq, 0),
            "xslab0": (tq, 0), "xslab1": (tq, 0),
            "psQ": (vq, 0),
            "kT": (tq, 0), "qT": (tq, 0), "vbuf": (tq, 0),
            "negMt": (aq, 0),
            "maxacc": (vq, 0),
            "gld0": (aq, 0), "gld1": (aq, 0),
            "obuf0": (c_sto[0], 0), "obuf1": (c_sto[1], 0),
            "ostage0": (tq, 0), "ostage1": (tq, 0),

            "outt0": (c_stu[0], 0), "outt1": (c_stu[1], 0),
        }

        def FR(name):
            return free[name]

        def setF(name, chain):
            free[name] = (chain, chain.n)

        def wf(eng, key, name):
            if name == "psT1":
                name = "psT0"
            if name == "kqhat1":
                name = "kqhat0"
            ch, n = free[name]
            FL.wait(eng, key, ch, n)

        @SYNC
        def _(sync):
            c_ldc.inc(sync.dma_start(identb[:], ident_in.ap()))
            c_ldc.inc(sync.dma_start(wrow[0:1, 0:64], knw.ap()))
            c_ldc.inc(sync.dma_start(wrow[0:1, 64:128], knw.ap()))
            c_ldc.inc(sync.dma_start(wrow[0:1, 128:192], qnw.ap()))
            c_ldc.inc(sync.dma_start(wrow[0:1, 192:256], qnw.ap()))
            c_ldc.inc(sync.dma_start(brow[:], scb.ap()))
        ld_consts = c_ldc.n

        @V
        def _(vector):
            vq.inc(vector.memset(onesf[:], 1.0))
            vq.inc(vector.memset(ones8, 0.125))
            vq.inc(vector.memset(ones127, 1.0 / 127.0))
            vq.inc(vector.memset(epsLN[:], 1e-5))
            vq.inc(vector.memset(epsRMS[:], 1e-6))
            vq.inc(vector.memset(epsTINY[:], 1e-30))
        v_consts = vq.n

        if DBG.get("s0_bc", True):
            @T
            def _(tensor):
                FL.wait(tensor, "t", c_ldc, ld_consts)
                FL.wait(tensor, "t", vq, v_consts)
                tensor.matmul(psB[:, 0:128], onesf[:], wrow[0:1, 0:128], start=True, stop=True)
                tq.inc(tensor.matmul(psB[:, 128:256], ones8, wrow[0:1, 128:256],
                                     start=True, stop=True))
                tensor.matmul(psA[:, 0:512], ones127, brow[0:1, 0:512], start=True, stop=True)
                tq.inc(tensor.matmul(psA[:, 512:1024], ones127, brow[0:1, 512:1024],
                                     start=True, stop=True))
            t_bc1 = tq.n

            @V
            def _(vector):
                FL.wait(vector, "v", tq, t_bc1)
                vq.inc(vector.tensor_copy(wmul[:], psB[:, 0:256]))
                vq.inc(vector.tensor_copy(scbB[:], psA[:]))
            v_bc1 = vq.n

            @SYNC
            def _(sync):
                FL.wait(sync, "s", tq, t_bc1)
                c_ldc.inc(sync.dma_start(brow[:], fbias.ap()))
            ld_f = c_ldc.n

            @T
            def _(tensor):
                FL.wait(tensor, "t", c_ldc, ld_f)
                FL.wait(tensor, "t", vq, v_bc1)
                tensor.matmul(psB[:, 0:512], onesf[:], brow[0:1, 0:512], start=True, stop=True)
                tq.inc(tensor.matmul(psB[:, 512:1024], onesf[:], brow[0:1, 512:1024],
                                     start=True, stop=True))
            t_bc2 = tq.n

            @SYNC
            def _(sync):
                FL.wait(sync, "s", tq, t_bc2)
                c_ldc.inc(sync.dma_start(brow[:], proj_b.ap()))
            ld_p = c_ldc.n

            @T
            def _(tensor):
                FL.wait(tensor, "t", c_ldc, ld_p)
                tensor.matmul(psA[:, 0:512], onesf[:], brow[0:1, 0:512], start=True, stop=True)
                tq.inc(tensor.matmul(psA[:, 512:1024], onesf[:], brow[0:1, 512:1024],
                                     start=True, stop=True))
            t_bc3 = tq.n

            @V
            def _(vector):
                FL.wait(vector, "v", tq, t_bc2)
                vq.inc(vector.tensor_copy(fbiasB[:], psB[:]))
                FL.wait(vector, "v", tq, t_bc3)
                vq.inc(vector.tensor_copy(projbB[:], psA[:]))

        def load_wT(dst_sb, src_dram, rows0, dst_cc_stride, dst_off, dst_free_key=None):
            @SYNC
            def _(sync):
                wf(sync, "s", "wstage")
                c_ldw.inc(sync.dma_start(wstage[:], src_dram.ap()[rows0:rows0 + 128, :]))
            mark_ld = c_ldw.n

            @V
            def _(vector):
                FL.wait(vector, "v", c_ldw, mark_ld)
                wf(vector, "v", "wstage_b")
                vq.inc(vector.tensor_copy(wstage_b[:], wstage[:]))
            setF("wstage", vq)
            mark_cast = vq.n

            for cc in range(8):
                slot = cc % 2

                @T
                def _(tensor, cc=cc, slot=slot):
                    FL.wait(tensor, "t", vq, mark_cast)
                    wf(tensor, "t", f"psT{slot}")
                    if dst_free_key:
                        wf(tensor, "t", dst_free_key)
                    tq.inc(tensor.transpose(psTs[slot][:],
                                            wstage_b[:, cc * 128:(cc + 1) * 128],
                                            identb[:]))
                mark_t = tq.n
                if cc == 7:
                    setF("wstage_b", tq)

                @V
                def _(vector, cc=cc, slot=slot):
                    FL.wait(vector, "v", tq, mark_t)
                    vq.inc(vector.tensor_copy(
                        dst_sb[:, cc * dst_cc_stride + dst_off:
                               cc * dst_cc_stride + dst_off + 128],
                        psTs[slot][:]))
                free["psT0"] = (vq, vq.n)
            return vq.n

        cb_ready = 0
        for r in range(8 if DBG.get("s0_cbt", True) else 0):
            cb_ready = load_wT(cbT, cbw, r * 128, C, r * 128)

        for ip in range((KC // 2) if nstage >= 1 else 0):
            marks_x = [0, 0]
            for j in range(2):
                i = ip * 2 + j

                @SYNC
                def _(sync, i=i, j=j):
                    wf(sync, "s", f"xt{j}")
                    c_ldx[j].inc(sync.dma_start(xt[j][:], xb.ap()[i * 128:(i + 1) * 128, :]))
                marks_x[j] = c_ldx[j].n

            @V
            def _(vector):
                for j in range(2):
                    FL.wait(vector, "v", c_ldx[j], marks_x[j])
                    FL.wait(vector, "v", aq, free["lnstat"][1] if "lnstat" in free else 0)
                    vq.inc(vector.reduce_sum(lnsum[:], xt[j][:], axis=AX.X))
                    vq.inc(vector.tensor_mul(xsq[:], xt[j][:], xt[j][:]))
                    FL.wait(vector, "v", vq, vq.n)
                    vq.inc(vector.reduce_sum(xss[:], xsq[:], axis=AX.X))
                    vq.inc(vector.tensor_scalar_mul(lnmean[:, j:j + 1], lnsum[:], 1.0 / C))
                    FL.wait(vector, "v", vq, vq.n)
                    vq.inc(vector.tensor_scalar_mul(lnex2[:, j:j + 1], xss[:], 1.0 / C))
                FL.wait(vector, "v", vq, vq.n)
                vq.inc(vector.tensor_mul(lnm2[:], lnmean[:], lnmean[:]))
                FL.wait(vector, "v", vq, vq.n)
                vq.inc(vector.tensor_sub(lnvar[:], lnex2[:], lnm2[:]))
            mark_var = vq.n

            @A
            def _(scalar):
                FL.wait(scalar, "a", vq, mark_var)
                aq.inc(scalar.activation(lnl[:], lnvar[:], AF.Ln, bias=epsLN[:], scale=1.0))
                FL.wait(scalar, "a", aq, aq.n)
                aq.inc(scalar.activation(lnr[:], lnl[:], AF.Exp, scale=-0.5))
            mark_rstd = aq.n
            free["lnstat"] = (aq, aq.n)

            for j in range(2):
                i = ip * 2 + j
                own = i < NOWN

                @V
                def _(vector, i=i, j=j):
                    FL.wait(vector, "v", aq, mark_rstd)
                    wf(vector, "v", f"xn{j}")
                    vq.inc(vector.tensor_scalar(xn_tok[j], xt[j][:],
                                                lnmean[:, j:j + 1], lnr[:, j:j + 1],
                                                ALU.subtract, ALU.mult))
                mark_xn = vq.n
                free[f"xt{j}"] = (vq, vq.n)

                for cc in range(8):
                    slot = cc % 2

                    @T
                    def _(tensor, cc=cc, j=j, slot=slot):
                        FL.wait(tensor, "t", vq, mark_xn)
                        wf(tensor, "t", f"psT{slot}")
                        wf(tensor, "t", f"xnTst{j}_dma")
                        wf(tensor, "t", f"xnTst{j}_ffn")
                        tq.inc(tensor.transpose(psTs[slot][:],
                                                xn_tok[j][:, cc * 128:(cc + 1) * 128],
                                                identb[:]))
                    mark_t = tq.n
                    if cc == 7:
                        free[f"xn{j}"] = (tq, tq.n)

                    @V
                    def _(vector, cc=cc, j=j, slot=slot):
                        FL.wait(vector, "v", tq, mark_t)
                        vq.inc(vector.tensor_copy(xnT_st[j][:, cc * 128:(cc + 1) * 128],
                                                  psTs[slot][:]))
                    free["psT0"] = (vq, vq.n)
                mark_xnT = vq.n

                @GP
                def _(gp, i=i, j=j):
                    FL.wait(gp, "g", vq, mark_xnT)
                    c_stx[j].inc(gp.dma_start(
                        xnT_h.ap()[:, :, i * 128:(i + 1) * 128].rearrange("c p t -> p c t"),
                        xnT_st[j].rearrange("p (c t) -> p c t", t=128)))
                free[f"xnTst{j}_dma"] = (c_stx[j], c_stx[j].n)

                if own:
                    @T
                    def _(tensor, j=j):
                        FL.wait(tensor, "t", vq, mark_xnT)
                        FL.wait(tensor, "t", vq, cb_ready)
                        wf(tensor, "t", "psB_v")
                        last = None
                        for half in range(2):
                            for cc in range(8):
                                last = tensor.matmul(
                                    psB[:, half * 512:(half + 1) * 512],
                                    xnT_st[j][:, cc * 128:(cc + 1) * 128],
                                    cbT[:, cc * C + half * 512: cc * C + half * 512 + 512],
                                    start=(cc == 0), stop=(cc == 7))
                        tq.inc(last)
                    mark_ffn = tq.n
                    free[f"xnTst{j}_ffn"] = (tq, tq.n)

                    @V
                    def _(vector, j=j):
                        FL.wait(vector, "v", tq, mark_ffn)
                        vq.inc(vector.tensor_mul(fo[:], psB[:], scbB[:]))
                        FL.wait(vector, "v", vq, vq.n)
                        wf(vector, "v", "fo2_st")
                        vq.inc(vector.tensor_add(fo2[:], fo[:], fbiasB[:]))
                    free["psB_v"] = (vq, vq.n)
                    mark_g = vq.n

                    @GP
                    def _(gp, i=i, j=j):
                        FL.wait(gp, "g", vq, mark_g)
                        c_stg[j].inc(gp.dma_start(g_h.ap()[i * 128:(i + 1) * 128, :], fo2[:]))
                    free["fo2_st"] = (c_stg[j], c_stg[j].n)

        stg_totals = [c_stg[0].n, c_stg[1].n]
        stx_totals = [c_stx[0].n, c_stx[1].n]
        free["wbT"] = (tq, tq.n)
        free["vbuf0"] = (tq, tq.n)
        free["outt0"] = (vq, vq.n)
        free["xt0"] = (vq, vq.n)
        free["xt1"] = (vq, vq.n)

        for i in range(NOWN if nstage >= 2 else 0):
            j = i % 2

            @SYNC
            def _(sync, i=i, j=j):
                FL.wait(sync, "s", c_stg[0], stg_totals[0])
                FL.wait(sync, "s", c_stg[1], stg_totals[1])
                wf(sync, "s", f"gld{j}")
                c_ldg[j].inc(sync.dma_start(gld[j][:], g_h.ap()[i * 128:(i + 1) * 128, :]))
            mark_g = c_ldg[j].n

            @A
            def _(scalar, j=j):
                FL.wait(scalar, "a", c_ldg[j], mark_g)
                wf(scalar, "a", f"outt{j}")
                aq.inc(scalar.activation(outt[j][:], gld[j][:], AF.Silu, scale=1.0))
            free[f"gld{j}"] = (aq, aq.n)
            mark_s = aq.n

            @GP
            def _(gp, i=i, j=j):
                FL.wait(gp, "g", aq, mark_s)
                c_stg[j].inc(gp.dma_start(g_h.ap()[i * 128:(i + 1) * 128, :], outt[j][:]))
            free[f"outt{j}"] = (c_stg[j], c_stg[j].n)

        stg_silu_totals = [c_stg[0].n, c_stg[1].n]

        free.update({
            "kT0": (tq, 0), "kT1": (tq, 0),
            "qT0": (tq, 0), "qT1": (tq, 0),
            "vbuf0": (tq, 0), "vbuf1": (tq, 0),
            "negMt0": (aq, 0), "negMt1": (aq, 0),
            "rms16": (aq, 0), "psQK": (vq, 0),
        })
        for sl in range(NLAG):
            free[f"qkvlag{sl}"] = (vq, 0)
        for sl in range(NSLAB):
            free[f"rms_sl{sl % 4}"] = (aq, 0)
            free[f"rstd_sl{sl % 2}"] = (vq, 0)
        free["kqhat0"] = (tq, 0)
        free["kqhat1"] = (tq, 0)
        kq_hat2 = [kq_hat, kq_hat]
        rms64 = sb("rms64", [128, 64], F32)
        rstd32 = sb("rstd32", [128, 32], F32)
        mark_negM_p = [0, 0]
        d01 = gld[0][0:1, 0:1024]
        r01 = gld[1][0:1, 0:1024]
        lnscr = s1t[0:1, 0:1024]
        sh = {}

        def wtprep_steps(hp):
            steps = []
            for gidx, rows0 in enumerate((C + hp * 128, hp * 128, 2 * C + hp * 128)):
                def dma_fn(rows0=rows0, gidx=gidx):
                    @SYNC
                    def _(sync):
                        wf(sync, "s", "wstage")
                        c_ldw.inc(sync.dma_start(wstage[:],
                                                 qkv_w.ap()[rows0:rows0 + 128, :]))
                    sh[f"wld{gidx}"] = c_ldw.n
                steps.append(("S", dma_fn))

                def cast_fn(gidx=gidx):
                    @V
                    def _(vector):
                        FL.wait(vector, "v", c_ldw, sh[f"wld{gidx}"])
                        wf(vector, "v", "wstage_b")
                        vq.inc(vector.tensor_copy(wstage_b[:], wstage[:]))
                    setF("wstage", vq)
                    sh[f"wcast{gidx}"] = vq.n
                steps.append(("V", cast_fn))

                for cc in range(8):
                    def tp_fn(gidx=gidx, cc=cc):
                        @T
                        def _(tensor):
                            FL.wait(tensor, "t", vq, sh[f"wcast{gidx}"])
                            wf(tensor, "t", "psT0")
                            wf(tensor, "t", "wt_buf")
                            tq.inc(tensor.transpose(psT0[:],
                                                    wstage_b[:, cc * 128:(cc + 1) * 128],
                                                    identb[:]))
                        sh["wtp"] = tq.n
                        if cc == 7:
                            setF("wstage_b", tq)
                    steps.append(("T", tp_fn))

                    def ev_fn(gidx=gidx, cc=cc):
                        @V
                        def _(vector):
                            FL.wait(vector, "v", tq, sh["wtp"])
                            vq.inc(vector.tensor_copy(
                                wt_buf[:, cc * 384 + gidx * 128:
                                       cc * 384 + gidx * 128 + 128],
                                psT0[:]))
                        free["psT0"] = (vq, vq.n)
                        sh["wt_ready"] = vq.n
                    steps.append(("V", ev_fn))
            return steps

        def s3_streams(hp):
            p = hp % 2
            kT, qT, vbuf = kTp[p], qTp[p], vbufp[p]
            st1 = [[] for _ in range(NSLAB)]
            st2 = [[] for _ in range(NSLAB)]
            tail = []

            def init_fn():
                @V
                def _(vector):
                    wf(vector, "v", "maxacc")
                    vq.inc(vector.memset(macc128[:], 0.0))
                    wf(vector, "v", f"vbuf{p}")
                    vq.inc(vector.memset(
                        vbuf.rearrange("p (c j) -> p c j", j=65)[:, :, 64:65], 1.0))
                sh[f"vones{hp}"] = vq.n
            st1[0].append(("V", init_fn))

            for s in range(NSLAB):
                jsl = s % 2

                def dma_fn(s=s, jsl=jsl):
                    @SYNC
                    def _(sync):
                        wf(sync, "s", f"xslab{jsl}")
                        FL.wait(sync, "s", c_stx[0], stx_totals[0])
                        FL.wait(sync, "s", c_stx[1], stx_totals[1])
                        for cc in range(8):
                            c_lds[jsl].inc(sync.dma_start(
                                xslab[jsl][:, cc * 512:(cc + 1) * 512],
                                xnT_h.ap()[cc, :, s * 512:(s + 1) * 512]))
                    sh[f"slab{s}"] = c_lds[jsl].n
                st1[s].append(("S", dma_fn))

                for c4 in range(4):
                    i = s * 4 + c4
                    slot = (s % NLAG) * 4 + c4

                    def qkv_fn(s=s, c4=c4, i=i, jsl=jsl):
                        @T
                        def _(tensor):
                            FL.wait(tensor, "t", c_lds[jsl], sh[f"slab{s}"])
                            FL.wait(tensor, "t", vq, sh["wt_ready"])
                            FL.wait(tensor, "t", vq, sh[f"vones{hp}"])
                            wf(tensor, "t", "psQK")
                            last = None
                            for cc in range(8):
                                last = tensor.matmul(
                                    psQK[:, 0:384],
                                    xslab[jsl][:, cc * 512 + c4 * 128:
                                               cc * 512 + c4 * 128 + 128],
                                    wt_buf[:, cc * 384: cc * 384 + 384],
                                    start=(cc == 0), stop=(cc == 7))
                            tq.inc(last)
                        sh[f"qkv{i}"] = tq.n
                        if c4 == 3:
                            free[f"xslab{jsl}"] = (tq, tq.n)
                        if s == NSLAB - 1 and c4 == 3:
                            setF("wt_buf", tq)
                    st1[s].append(("T", qkv_fn))

                    def evac_fn(s=s, c4=c4, i=i, slot=slot):
                        @V
                        def _(vector):
                            FL.wait(vector, "v", tq, sh[f"qkv{i}"])
                            wf(vector, "v", f"qkvlag{slot // 4}")
                            vq.inc(vector.tensor_copy(
                                qkvlag[:, slot * 384:(slot + 1) * 384],
                                psQK[:, 0:384]))
                            FL.wait(vector, "v", vq, vq.n)
                            vq.inc(vector.tensor_mul(
                                sqt[:],
                                qkvlag[:, slot * 384: slot * 384 + 256],
                                qkvlag[:, slot * 384: slot * 384 + 256]))
                            FL.wait(vector, "v", vq, vq.n)
                            wf(vector, "v", f"rms_sl{s % 4}")
                            vq.inc(vector.reduce_sum(
                                rms64[:, (s % 4) * 16 + c4 * 4:
                                      (s % 4) * 16 + (c4 + 1) * 4],
                                sqt.rearrange("p (g d) -> p g d", d=DH),
                                axis=AX.X))
                        free["psQK"] = (vq, vq.n)
                        sh[f"rms{i}"] = vq.n
                    st1[s].append(("V", evac_fn))

                def rstd_fn(s=s):
                    @A
                    def _(scalar):
                        FL.wait(scalar, "a", aq, aq.n)
                        FL.wait(scalar, "a", vq, sh[f"rms{s * 4 + 3}"])
                        wf(scalar, "a", f"rstd_sl{s % 2}")
                        aq.inc(scalar.activation(
                            rstdL[:], rms64[:, (s % 4) * 16:(s % 4) * 16 + 16],
                            AF.Ln, bias=epsRMS[:], scale=1.0 / DH))
                        FL.wait(scalar, "a", aq, aq.n)
                        aq.inc(scalar.activation(
                            rstd32[:, (s % 2) * 16:(s % 2) * 16 + 16],
                            rstdL[:], AF.Exp, scale=-0.5))
                    free[f"rms_sl{s % 4}"] = (aq, aq.n)
                    sh[f"rstd{s}"] = aq.n
                st2[s].append(("A", rstd_fn))

                for c4 in range(4):
                    i = s * 4 + c4
                    slot = (s % NLAG) * 4 + c4
                    own = i < NOWN
                    kpar = i % 2

                    def norm_fn(s=s, c4=c4, i=i, slot=slot, kpar=kpar, p=p):
                        @V
                        def _(vector):
                            FL.wait(vector, "v", aq, sh[f"rstd{s}"])
                            wf(vector, "v", f"kqhat{kpar}")
                            base = slot * 384
                            rb_ = (s % 2) * 16 + c4 * 4
                            for g_ in range(4):
                                vq.inc(vector.tensor_scalar(
                                    tmn[:, g_ * 64:(g_ + 1) * 64],
                                    qkvlag[:, base + g_ * 64: base + (g_ + 1) * 64],
                                    rstd32[:, rb_ + g_: rb_ + g_ + 1], None, ALU.mult))
                            FL.wait(vector, "v", vq, vq.n)
                            vq.inc(vector.tensor_mul(kq_hat2[kpar][:], tmn[:], wmul[:]))
                            dst = vbufp[p].rearrange("p (h k j) -> p h k j", h=2, j=65)[
                                :, :, i:i + 1, 0:64].rearrange("p h k j -> p (h k) j")
                            srcv = qkvlag[:, base + 256: base + 384].rearrange(
                                "p (h j) -> p h j", h=2)
                            vq.inc(vector.tensor_copy(dst, srcv))
                        if c4 == 3:
                            free[f"qkvlag{slot // 4}"] = (vq, vq.n)
                        if s == NSLAB - 1 and c4 == 3:
                            free[f"rstd_sl{s % 2}"] = (vq, vq.n)
                        else:
                            free[f"rstd_sl{s % 2}"] = (vq, vq.n)
                        sh[f"norm{i}"] = vq.n
                    st2[s].append(("V", norm_fn))

                    def tpk_fn(i=i, kpar=kpar, p=p):
                        @T
                        def _(tensor):
                            FL.wait(tensor, "t", vq, sh[f"norm{i}"])
                            wf(tensor, "t", "psT0")
                            wf(tensor, "t", f"kT{p}")
                            tq.inc(tensor.transpose(psT0[:], kq_hat2[kpar][:, 0:128],
                                                    identb[:]))
                        sh[f"tpk{i}"] = tq.n
                    st2[s].append(("T", tpk_fn))

                    def evk_fn(i=i, p=p, own=own, kpar=kpar):
                        @V
                        def _(vector):
                            FL.wait(vector, "v", tq, sh[f"tpk{i}"])
                            vq.inc(vector.tensor_copy(
                                kTp[p][:, i * 128:(i + 1) * 128], psT0[:]))
                            free["psT0"] = (vq, vq.n)
                            if not own:
                                vq.inc(vector.tensor_mul(sqm[:], kq_hat2[kpar][:],
                                                         kq_hat2[kpar][:]))
                                FL.wait(vector, "v", vq, vq.n)
                                vq.inc(vector.reduce_sum(
                                    m4[:], sqm.rearrange("p (g d) -> p g d", d=DH),
                                    axis=AX.X))
                                FL.wait(vector, "v", vq, vq.n)
                                wf(vector, "v", "maxacc")
                                vq.inc(vector.tensor_max(macc128[:, 0:2],
                                                         macc128[:, 0:2], m4[:, 0:2]))
                                free["maxacc"] = (vq, vq.n)
                        if not own:
                            free["kqhat0"] = (tq, sh[f"tpk{i}"])
                        sh[f"evk{i}"] = vq.n
                    st2[s].append(("V", evk_fn))

                    if own:
                        def tpq_fn(i=i, kpar=kpar, p=p):
                            @T
                            def _(tensor):
                                FL.wait(tensor, "t", vq, sh[f"evk{i}"])
                                wf(tensor, "t", "psT0")
                                wf(tensor, "t", f"qT{p}")
                                tq.inc(tensor.transpose(psT0[:],
                                                        kq_hat2[kpar][:, 128:256],
                                                        identb[:]))
                            sh[f"tpq{i}"] = tq.n
                        st2[s].append(("T", tpq_fn))

                        def evq_fn(i=i, p=p, kpar=kpar):
                            @V
                            def _(vector):
                                FL.wait(vector, "v", tq, sh[f"tpq{i}"])
                                vq.inc(vector.tensor_copy(
                                    qTp[p][:, i * 128:(i + 1) * 128], psT0[:]))
                                free["psT0"] = (vq, vq.n)
                                vq.inc(vector.tensor_mul(sqm[:], kq_hat2[kpar][:],
                                                         kq_hat2[kpar][:]))
                                FL.wait(vector, "v", vq, vq.n)
                                vq.inc(vector.reduce_sum(
                                    m4[:], sqm.rearrange("p (g d) -> p g d", d=DH),
                                    axis=AX.X))
                                FL.wait(vector, "v", vq, vq.n)
                                wf(vector, "v", "maxacc")
                                vq.inc(vector.tensor_max(macc128[:], macc128[:], m4[:]))
                                free["maxacc"] = (vq, vq.n)
                            free["kqhat0"] = (tq, sh[f"tpq{i}"])
                        st2[s].append(("V", evq_fn))

            def tail1():
                @V
                def _(vector):
                    FL.wait(vector, "v", vq, vq.n)
                    vq.inc(vector.tensor_copy(m4b[:], macc128[:]))
                sh["mab"] = vq.n
            tail.append(("V", tail1))

            def tail2():
                @T
                def _(tensor):
                    FL.wait(tensor, "t", vq, sh["mab"])
                    wf(tensor, "t", "psT0")
                    tq.inc(tensor.transpose(psT0[0:4, 0:128], m4b[:], identb[:]))
                sh["mat"] = tq.n
            tail.append(("T", tail2))

            def tail3():
                @V
                def _(vector):
                    FL.wait(vector, "v", tq, sh["mat"])
                    vq.inc(vector.reduce_max(mx4[:], psT0[0:4, 0:128], axis=AX.X))
                    free["psT0"] = (vq, vq.n)
                    FL.wait(vector, "v", vq, vq.n)
                    vq.inc(vector.tensor_copy(maxaccb[:], mx4[:]))
                sh["mab2"] = vq.n
            tail.append(("V", tail3))

            def tail4():
                @T
                def _(tensor):
                    FL.wait(tensor, "t", vq, sh["mab2"])
                    wf(tensor, "t", "psT0")
                    tq.inc(tensor.transpose(psT0[0:1, 0:4], maxaccb[:], identb[0:4, 0:4]))
                sh["mat2"] = tq.n
            tail.append(("T", tail4))

            def tail5():
                @V
                def _(vector):
                    FL.wait(vector, "v", tq, sh["mat2"])
                    vq.inc(vector.tensor_copy(mrow[:], psT0[0:1, 0:4]))
                    free["psT0"] = (vq, vq.n)
                    FL.wait(vector, "v", vq, vq.n)
                    vq.inc(vector.tensor_mul(mab[:], mrow[0:1, 0:2], mrow[0:1, 2:4]))
                    FL.wait(vector, "v", vq, vq.n)
                    vq.inc(vector.reduce_max(mab[0:1, 0:1], mab[:], axis=AX.X))
                sh["mabf"] = vq.n
            tail.append(("V", tail5))

            def tail6():
                @A
                def _(scalar):
                    FL.wait(scalar, "a", vq, sh["mabf"])
                    aq.inc(scalar.activation(mln[0:1, 0:1], mab[0:1, 0:1], AF.Ln,
                                             bias=epsTINY[:], scale=1.0))
                    FL.wait(scalar, "a", aq, aq.n)
                    aq.inc(scalar.activation(negm2[0:1, 0:1], mln[0:1, 0:1],
                                             AF.Exp, scale=0.5))
                sh["mh"] = aq.n
            tail.append(("A", tail6))

            def tail7():
                @T
                def _(tensor):
                    FL.wait(tensor, "t", aq, sh["mh"])
                    wf(tensor, "t", "psQK")
                    tq.inc(tensor.matmul(psQK[:, 384:385], onesf[:], negm2[0:1, 0:1],
                                         start=True, stop=True))
                sh["mq"] = tq.n
            tail.append(("T", tail7))

            def tail8(p=p):
                @V
                def _(vector):
                    FL.wait(vector, "v", tq, sh["mq"])
                    wf(vector, "v", f"negMt{p}")
                    vq.inc(vector.tensor_scalar_mul(negMt[:, p:p + 1],
                                                    psQK[:, 384:385], -1.0))
                free["psQK"] = (vq, vq.n)
                mark_negM_p[p] = vq.n
            tail.append(("V", tail8))
            return st1, st2, tail

        def run_steps(steps):
            for eng_, fn in steps:
                fn()

        prev_ofree = (vq, 0)
        prev_bankfree = (vq, 0)

        if nstage >= 3:
            run_steps(wtprep_steps(0))
            st1_, st2_, tail_ = s3_streams(0)
            for s in range(NSLAB):
                run_steps(st1_[s])
                run_steps(st2_[s])
            run_steps(tail_)

        for w in range(HP if nstage >= 4 else 0):
            p_r = w % 2
            kT, qT, vbuf = kTp[p_r], qTp[p_r], vbufp[p_r]
            if w + 1 < HP and nstage >= 3:
                nxt_pre = wtprep_steps(w + 1)
                st1_, st2_, tail_ = s3_streams(w + 1)
                qb_assign = {i: [] for i in range(NQB)}
                qb_assign[0].extend(nxt_pre)
                leftover = []
                for s in range(NSLAB):
                    qb_assign[min(s // 2, NQB - 1)].extend(st1_[s])
                for s in range(NSLAB):
                    dst_qb = s // 2 + 1
                    if dst_qb < NQB:
                        qb_assign[dst_qb].extend(st2_[s])
                    else:
                        leftover.extend(st2_[s])
                tail_steps = leftover + tail_
            else:
                qb_assign = {i: [] for i in range(NQB)}
                leftover = []
                tail_steps = []

            for qb in range(NQB):
                aq_base = aq.n
                exp_mark = {kc: 0 for kc in range(KC)}
                pending = qb_assign[qb]
                a_steps = [fn for eng_, fn in pending if eng_ == "A"]
                other = [(eng_, fn) for eng_, fn in pending if eng_ != "A"]
                for fn in a_steps:
                    fn()

                aq_base = aq.n
                for kc in range(KC):
                    exp_mark[kc] = aq_base + kc + 1
                aq.reserve(KC)
                mark_r = aq.reserve(2)

                s_mark = {}
                pv_mark = {}
                psS = (psA, psB)
                noct = 8
                per = (len(other) + noct - 1) // noct if other else 0
                oc_steps = [other[o * per:(o + 1) * per] for o in range(noct)]

                for oc in range(noct):
                    kc_lo, kc_hi = oc * (KC // noct), (oc + 1) * (KC // noct)
                    if oc == noct - 1:
                        kc_hi = KC + 1
                    kc_hi = min(kc_hi, KC + 1)

                    @T
                    def _(tensor, qb=qb, kc_lo=kc_lo, kc_hi=kc_hi, p_r=p_r):
                        FL.wait(tensor, "t", vq, mark_negM_p[p_r])
                        FL.wait(tensor, "t", vq, prev_bankfree[1])
                        for kc in range(kc_lo, kc_hi):
                            if kc < KC:
                                par = kc % 2
                                FL.wait(tensor, "t", aq, exp_mark.get(kc - 2, 0))
                                tensor.matmul(psS[par][:, 0:512],
                                              kT[0:64, kc * 128:(kc + 1) * 128],
                                              qT[0:64, qb * 512:(qb + 1) * 512],
                                              start=True, stop=True)
                                tq.inc(tensor.matmul(psS[par][:, 512:1024],
                                                     kT[64:128, kc * 128:(kc + 1) * 128],
                                                     qT[64:128, qb * 512:(qb + 1) * 512],
                                                     start=True, stop=True,
                                                     tile_position=(64, 0)))
                                s_mark[kc] = tq.n
                            if kc >= 1:
                                kp = kc - 1
                                par = kp % 2
                                FL.wait(tensor, "t", aq, exp_mark[kp])
                                FL.wait(tensor, "t", vq, prev_ofree[1])
                                tensor.matmul(psO0[:],
                                              vbuf[:, kp * 65: kp * 65 + 65],
                                              ptb[par][:, 0:512],
                                              start=(kp == 0), stop=(kp == KC - 1))
                                tq.inc(tensor.matmul(
                                    psO1[:],
                                    vbuf[:, (KC + kp) * 65: (KC + kp) * 65 + 65],
                                    ptb[par][:, 512:1024],
                                    start=(kp == 0), stop=(kp == KC - 1)))
                                pv_mark[kp] = tq.n
                    for eng_, fn in oc_steps[oc]:
                        fn()
                mark_pv_last = pv_mark[KC - 1]

                @T
                def _(tensor, qb=qb):
                    FL.wait(tensor, "t", aq, mark_r)
                    tensor.matmul(psA[0:64, 0:512], onesf[0:1, 0:64],
                                  r01[0:1, 0:512], start=True, stop=True)
                    tq.inc(tensor.matmul(psB[0:64, 0:512], onesf[0:1, 0:64],
                                         r01[0:1, 512:1024],
                                         start=True, stop=True))
                mark_bc = tq.n

                @V
                def _(vector, qb=qb, hp=w):
                    FL.wait(vector, "v", tq, mark_pv_last)
                    vq.inc(vector.tensor_copy(d01[0:1, 0:512], psO0[64:65, :]))
                    vq.inc(vector.tensor_copy(d01[0:1, 512:1024], psO1[64:65, :]))
                    FL.wait(vector, "v", tq, mark_bc)
                    vq.inc(vector.tensor_copy(rb0[:], psA[0:64, 0:512]))
                    vq.inc(vector.tensor_copy(rb1[:], psB[0:64, 0:512]))
                    FL.wait(vector, "v", vq, vq.n)
                    jo = qb % 2
                    wf(vector, "v", f"obuf{jo}")
                    vq.inc(vector.tensor_mul(obuf[jo][0:64, :], psO0[0:64, :], rb0[:]))
                    vq.inc(vector.tensor_mul(obuf[jo][64:128, :], psO1[0:64, :], rb1[:]))
                mark_d = vq.n - 4
                prev_bankfree = (vq, vq.n)
                prev_ofree = (vq, vq.n)
                mark_ob = vq.n

                @A
                def _(scalar, qb=qb, p_r=p_r):
                    FL.wait(scalar, "a", vq, mark_negM_p[p_r])
                    for kc in range(KC):
                        par = kc % 2
                        FL.wait(scalar, "a", tq, s_mark[kc])
                        FL.wait(scalar, "a", tq, pv_mark.get(kc - 2, 0))
                        aq.inc_pre(scalar.activation(ptb[par][:], psS[par][:], AF.Exp,
                                                     bias=negMt[:, p_r:p_r + 1],
                                                     scale=1.0))
                    FL.wait(scalar, "a", vq, mark_d)
                    aq.inc_pre(scalar.activation(lnscr, d01, AF.Ln, scale=1.0))
                    FL.wait(scalar, "a", aq, mark_r - 1)
                    aq.inc_pre(scalar.activation(r01, lnscr, AF.Exp, scale=-1.0))
                FL.f.setdefault("a", {})[id(aq)] = max(
                    FL.f.get("a", {}).get(id(aq), 0), mark_r)

                @GP
                def _(gp, qb=qb, hp=w):
                    FL.wait(gp, "g", vq, mark_ob)
                    jo = qb % 2
                    c_sto[jo].inc(gp.dma_start(oT_h.ap()[hp, :, qb * 512:(qb + 1) * 512],
                                               obuf[jo][:]))
                free[f"obuf{qb % 2}"] = (c_sto[qb % 2], c_sto[qb % 2].n)

            run_steps(tail_steps)
            free[f"kT{p_r}"] = (tq, tq.n)
            free[f"qT{p_r}"] = (tq, tq.n)
            free[f"vbuf{p_r}"] = (tq, tq.n)
            free[f"negMt{p_r}"] = (aq, aq.n)

        sto_totals = [c_sto[0].n, c_sto[1].n]

        for r in range(8 if DBG.get("s5_prep", True) else 0):
            load_wT(projT, proj_w, r * 128, C, r * 128, dst_free_key="wbT")
        mark_pj = vq.n

        for i in range(NOWN if nstage >= 5 else 0):
            j = i % 2

            @SYNC
            def _(sync, i=i, j=j):
                FL.wait(sync, "s", c_sto[0], sto_totals[0])
                FL.wait(sync, "s", c_sto[1], sto_totals[1])
                FL.wait(sync, "s", c_stg[0], stg_silu_totals[0])
                FL.wait(sync, "s", c_stg[1], stg_silu_totals[1])
                wf(sync, "s", f"ostage{j}")
                for hp in range(HP):
                    c_ld5[j].inc(sync.dma_start(ostage[j][:, hp * 128:(hp + 1) * 128],
                                                oT_h.ap()[hp, :, i * 128:(i + 1) * 128]))
                wf(sync, "s", f"gld{j}")
                c_ld5[j].inc(sync.dma_start(gld[j][:], g_h.ap()[i * 128:(i + 1) * 128, :]))
                wf(sync, "s", f"xt{j}")
                c_ld5[j].inc(sync.dma_start(xld[j][:], xb.ap()[i * 128:(i + 1) * 128, :]))
            mark_ld = c_ld5[j].n

            @T
            def _(tensor, i=i, j=j):
                FL.wait(tensor, "t", c_ld5[j], mark_ld)
                FL.wait(tensor, "t", vq, mark_pj)
                FL.wait(tensor, "t", vq, free["psA_s5"][1] if "psA_s5" in free else 0)
                last = None
                for half in range(2):
                    for cc in range(8):
                        last = tensor.matmul(
                            psA[:, half * 512:(half + 1) * 512],
                            ostage[j][:, cc * 128:(cc + 1) * 128],
                            projT[:, cc * C + half * 512: cc * C + half * 512 + 512],
                            start=(cc == 0), stop=(cc == 7))
                tq.inc(last)
            free[f"ostage{j}"] = (tq, tq.n)
            mark_proj = tq.n

            @V
            def _(vector, i=i, j=j):
                FL.wait(vector, "v", tq, mark_proj)
                vq.inc(vector.tensor_add(s1t[:], psA[:], projbB[:]))
                FL.wait(vector, "v", vq, vq.n)
                vq.inc(vector.tensor_mul(s2t[:], s1t[:], gld[j][:]))
                FL.wait(vector, "v", vq, vq.n)
                wf(vector, "v", f"outt{j}")
                vq.inc(vector.tensor_add(outt[j][:], s2t[:], xld[j][:]))
            free["psA_s5"] = (vq, vq.n)
            free[f"gld{j}"] = (vq, vq.n)
            free[f"xt{j}"] = (vq, vq.n)
            mark_out = vq.n

            @GP
            def _(gp, i=i, j=j):
                FL.wait(gp, "g", vq, mark_out)
                c_stu[j].inc(gp.dma_start(out.ap()[i * 128:(i + 1) * 128, :], outt[j][:]))
            free[f"outt{j}"] = (c_stu[j], c_stu[j].n)

        stu_totals = [c_stu[0].n, c_stu[1].n]

        @GP
        def _(gp):
            gp.wait_ge(c_stu[0].sem, stu_totals[0])
            gp.wait_ge(c_stu[1].sem, stu_totals[1])

        @SYNC
        def _(sync):
            sync.wait_ge(c_stu[0].sem, stu_totals[0])
            sync.wait_ge(c_stu[1].sem, stu_totals[1])

    return nc



def _shard_inputs(L0_attn_x, L0_attn_qkv_w, L0_attn_proj_w, L0_attn_proj_b,
                  L0_attn_q_norm_w, L0_attn_k_norm_w, L0_ffn_CB, L0_ffn_SCB,
                  L0_ffn_bias):
    x = np.ascontiguousarray(np.asarray(L0_attn_x, dtype=np.float32))
    B, N, Cc = x.shape
    common = {
        "qkv_w": np.ascontiguousarray(np.asarray(L0_attn_qkv_w, np.float32)),
        "proj_w": np.ascontiguousarray(np.asarray(L0_attn_proj_w, np.float32)),
        "proj_b": np.asarray(L0_attn_proj_b, np.float32).reshape(1, Cc),
        "qnw": np.asarray(L0_attn_q_norm_w, np.float32).reshape(1, DH),
        "knw": np.asarray(L0_attn_k_norm_w, np.float32).reshape(1, DH),
        "cbw": np.ascontiguousarray(np.asarray(L0_ffn_CB, np.float32)),
        "scb": np.asarray(L0_ffn_SCB, np.float32).reshape(1, Cc),
        "fbias": np.asarray(L0_ffn_bias, np.float32).reshape(1, Cc),
        "ident_in": np.eye(128, dtype=np.float32).astype(bfdt),
    }
    in_maps = []
    half = N // 2
    for c in range(8):
        b, h = c // 2, c % 2
        own = x[b, h * half:(h + 1) * half]
        other = x[b, (1 - h) * half:(2 - h) * half]
        m = dict(common)
        m["xb"] = np.ascontiguousarray(np.concatenate([own, other], axis=0))
        in_maps.append(m)
    return in_maps, (B, N, Cc)


_CACHE = {}


def kernel(**inputs):
    in_maps, (B, N, Cc) = _shard_inputs(**inputs)
    key = (N,)
    if key not in _CACHE:
        _CACHE[key] = build(T_ALL=N, T_OWN=N // 2)
    nc = _CACHE[key]
    res = run_bass_kernel_spmd(nc, in_maps, core_ids=list(range(8)))
    half = N // 2
    outp = np.empty((B, N, Cc), dtype=np.float32)
    for c in range(8):
        b, h = c // 2, c % 2
        outp[b, h * half:(h + 1) * half] = res.results[c]["out"]
    return outp


# version 09
# speedup vs baseline: 1.5973x; 1.0266x over previous
import numpy as np
import ml_dtypes
from contextlib import ExitStack

import concourse.bass as bass
import concourse.mybir as mybir
from concourse.bass_utils import run_bass_kernel_spmd

F32 = mybir.dt.float32
BF16 = mybir.dt.bfloat16
AF = mybir.ActivationFunctionType
ALU = mybir.AluOpType
AX = mybir.AxisListType
bfdt = ml_dtypes.bfloat16

C = 1024
DH = 64
HP = 8
DBG = {}


class Chain:
    def __init__(self, sem, step=1):
        self.sem = sem
        self.n = 0
        self.step = step

    def inc(self, inst, k=None):
        k = self.step if k is None else k
        inst.then_inc(self.sem, k)
        self.n += k
        return self.n

    def reserve(self, k=1):
        self.n += k
        return self.n

    def inc_pre(self, inst, k=1):
        inst.then_inc(self.sem, k)
        return inst


class Floors:
    def __init__(self):
        self.f = {}

    def wait(self, eng, key, chain, n):
        if n <= 0:
            return
        d = self.f.setdefault(key, {})
        if n > d.get(id(chain), 0):
            assert n <= chain.n, f"wait {n} > emitted {chain.n} ({key})"
            eng.wait_ge(chain.sem, n)
            d[id(chain)] = n


def build(T_ALL=8192, T_OWN=4096, nstage=99):
    KC = T_ALL // 128
    NSLAB = T_ALL // 512
    NQB = T_OWN // 512
    NOWN = T_OWN // 128
    NG = KC // 2

    nc = bass.Bass("TRN2", target_bir_lowering=False)

    xb = nc.dram_tensor("xb", [T_ALL, C], F32, kind="ExternalInput")
    qkv_w = nc.dram_tensor("qkv_w", [3 * C, C], F32, kind="ExternalInput")
    proj_w = nc.dram_tensor("proj_w", [C, C], F32, kind="ExternalInput")
    proj_b = nc.dram_tensor("proj_b", [1, C], F32, kind="ExternalInput")
    qnw = nc.dram_tensor("qnw", [1, DH], F32, kind="ExternalInput")
    knw = nc.dram_tensor("knw", [1, DH], F32, kind="ExternalInput")
    cbw = nc.dram_tensor("cbw", [C, C], F32, kind="ExternalInput")
    scb = nc.dram_tensor("scb", [1, C], F32, kind="ExternalInput")
    fbias = nc.dram_tensor("fbias", [1, C], F32, kind="ExternalInput")
    ident_in = nc.dram_tensor("ident_in", [128, 128], BF16, kind="ExternalInput")

    out = nc.dram_tensor("out", [T_OWN, C], F32, kind="ExternalOutput")

    xnT_h = nc.dram_tensor("xnT_h", [8, 128, T_ALL], BF16, kind="Internal")
    g_h = nc.dram_tensor("g_h", [T_OWN, C], F32, kind="Internal")
    oT_h = nc.dram_tensor("oT_h", [HP, 128, T_OWN], BF16, kind="Internal")

    with ExitStack() as ctx:
        sb = lambda n_, s_, d_: ctx.enter_context(nc.sbuf_tensor(n_, s_, d_))
        pt = lambda n_, s_, d_: ctx.enter_context(nc.psum_tensor(n_, s_, d_))
        mksem = lambda n_: ctx.enter_context(nc.semaphore(n_))

        identb = sb("identb", [128, 128], BF16)
        onesf = sb("onesf", [1, 128], F32)
        epsLN = sb("epsLN", [128, 1], F32)
        epsRMS = sb("epsRMS", [128, 1], F32)
        epsTINY = sb("epsTINY", [1, 1], F32)

        brow = sb("brow", [1, C], F32)
        wmul = sb("wmul", [128, 256], F32)
        
        scbB = sb("scbB", [128, C], F32)
        fbiasB = sb("fbiasB", [128, C], F32)
        projbB = sb("projbB", [128, C], F32)

        xt = [sb(f"xt{j}", [128, C], F32) for j in range(2)]
        lnsum = sb("lnsum", [128, 1], F32)

        xss = sb("xss", [128, 1], F32)
        lnmean = sb("lnmean", [128, 2], F32)
        lnex2 = sb("lnex2", [128, 2], F32)
        lnm2 = sb("lnm2", [128, 2], F32)
        lnvar = sb("lnvar", [128, 2], F32)
        lnl = sb("lnl", [128, 2], F32)
        lnr = sb("lnr", [128, 2], F32)
        xn_tok = None

        wstage = sb("wstage", [128, C], F32)
        wstage_b = sb("wstage_b", [128, C], BF16)
        cbT = sb("cbT", [128, 8 * C], BF16)
        projT = cbT
        wt_buf = sb("wt_buf", [128, 8 * 384], BF16)


        gld = [sb(f"gld{j}", [128, C], F32) for j in range(2)]
        wrow = gld[1][0:1, 0:256]
        ones8 = gld[0][0:1, 0:128]
        ones127 = gld[0][0:1, 128:256]

        xslab = [sb(f"xslab{j}", [128, 8 * 512], BF16) for j in range(2)]
        kTp = [sb(f"kT{p}", [128, T_ALL], BF16) for p in range(2)]
        qTp = [sb(f"qT{p}", [128, T_OWN], BF16) for p in range(2)]
        vbufp = [sb(f"vbuf{p}", [128, 2 * KC * 65], BF16) for p in range(2)]
        if 2 * KC * 65 >= 2 * C:
            xn_tok = [vbufp[0][:, 0:C], vbufp[0][:, C:2 * C]]
        else:
            _xn = [sb(f"xn_tok{j}", [128, C], BF16) for j in range(2)]
            xn_tok = [_xn[0][:], _xn[1][:]]
        NLAG = min(4, NSLAB)
        qkvlag = sb("qkvlag", [128, NLAG * 4 * 384], BF16)
        sqt = sb("sqt", [128, 256], F32)
        macc128 = sb("macc128", [128, 4], F32)
        rms16 = sb("rms16", [128, 16], F32)
        rstdL = sb("rstdL", [128, 16], F32)
        rstd16 = sb("rstd16", [128, 16], F32)
        tmn = sb("tmn", [128, 256], F32)
        kq_hat = sb("kq_hat", [128, 256], BF16)
        sqm = sb("sqm", [128, 256], BF16)
        m4 = sb("m4", [128, 4], F32)
        m4b = sb("m4b", [128, 4], BF16)
        mx4 = sb("mx4", [4, 1], F32)
        maxaccb = sb("maxaccb", [4, 1], BF16)
        mrow = sb("mrow", [1, 4], F32)
        mab = sb("mab", [1, 2], F32)
        mln = sb("mln", [1, 2], F32)
        negm2 = sb("negm2", [1, 2], F32)
        negMt = sb("negMt", [128, 2], F32)

        ptb = [sb(f"ptb{j}", [128, 1024], BF16) for j in range(2)]
        xnT_st = ptb

        rb0 = sb("rb0", [64, 512], F32)
        rb1 = sb("rb1", [64, 512], F32)
        obuf = [sb(f"obuf{j}", [128, 512], BF16) for j in range(2)]

        ostage = [sb(f"ostage{j}", [128, 8 * 128], BF16) for j in range(2)]
        xld = xt
        s1t = sb("s1t", [128, C], F32)
        s2t = sb("s2t", [128, C], F32)
        outt = [sb(f"outt{j}", [128, C], F32) for j in range(2)]
        xsq = s1t
        fo = s2t
        fo2 = outt[0]

        psA = pt("psA", [128, 1024], F32)
        psB = pt("psB", [128, 1024], F32)
        psO0 = pt("psO0", [65, 512], F32)
        psO1 = pt("psO1", [65, 512], F32)
        psT0 = pt("psT0", [128, 128], BF16)
        psQK = pt("psQK", [128, 512], F32)
        psT1 = psT0
        psTs = (psT0, psT0)

        c_ldc = Chain(mksem("ldc"), 16)
        c_ldw = Chain(mksem("ldw"), 16)
        c_ldx = [Chain(mksem(f"ldx{j}"), 16) for j in range(2)]
        c_lds = [Chain(mksem(f"lds{j}"), 16) for j in range(2)]
        c_ldg = [Chain(mksem(f"ldg{j}"), 16) for j in range(2)]
        c_ld5 = [Chain(mksem(f"ld5{j}"), 16) for j in range(2)]
        c_stx = [Chain(mksem(f"stx{j}"), 16) for j in range(2)]
        c_stg = [Chain(mksem(f"stg{j}"), 16) for j in range(2)]
        c_sto = [Chain(mksem(f"sto{j}"), 16) for j in range(2)]
        c_stu = [Chain(mksem(f"stu{j}"), 16) for j in range(2)]
        tq = Chain(mksem("tq"))
        vq = Chain(mksem("vq"))
        aq = Chain(mksem("aq"))

        FL = Floors()
        block = ctx.enter_context(nc.Block())
        SYNC, GP, T, V, A = block.sync, block.gpsimd, block.tensor, block.vector, block.scalar

        free = {
            "xt0": (vq, 0), "xt1": (vq, 0),
            "xn0": (tq, 0), "xn1": (tq, 0),
            "xnTst0_dma": (c_stx[0], 0), "xnTst1_dma": (c_stx[1], 0),
            "xnTst0_ffn": (tq, 0), "xnTst1_ffn": (tq, 0),
            "psB_v": (vq, 0),
            "fo2_st": (c_stg[0], 0),
            "wstage": (vq, 0), "wstage_b": (tq, 0),
            "psT0": (vq, 0),
            "wt_buf": (tq, 0), "wbT": (tq, 0),
            "xslab0": (tq, 0), "xslab1": (tq, 0),
            "psQ": (vq, 0),
            "kT": (tq, 0), "qT": (tq, 0), "vbuf": (tq, 0),
            "negMt": (aq, 0),
            "maxacc": (vq, 0),
            "gld0": (aq, 0), "gld1": (aq, 0),
            "obuf0": (c_sto[0], 0), "obuf1": (c_sto[1], 0),
            "ostage0": (tq, 0), "ostage1": (tq, 0),

            "outt0": (c_stu[0], 0), "outt1": (c_stu[1], 0),
        }

        def FR(name):
            return free[name]

        def setF(name, chain):
            free[name] = (chain, chain.n)

        def wf(eng, key, name):
            if name == "psT1":
                name = "psT0"
            if name == "kqhat1":
                name = "kqhat0"
            ch, n = free[name]
            FL.wait(eng, key, ch, n)

        @SYNC
        def _(sync):
            c_ldc.inc(sync.dma_start(identb[:], ident_in.ap()))
            c_ldc.inc(sync.dma_start(wrow[0:1, 0:64], knw.ap()))
            c_ldc.inc(sync.dma_start(wrow[0:1, 64:128], knw.ap()))
            c_ldc.inc(sync.dma_start(wrow[0:1, 128:192], qnw.ap()))
            c_ldc.inc(sync.dma_start(wrow[0:1, 192:256], qnw.ap()))
            c_ldc.inc(sync.dma_start(brow[:], scb.ap()))
        ld_consts = c_ldc.n

        @V
        def _(vector):
            vq.inc(vector.memset(onesf[:], 1.0))
            vq.inc(vector.memset(ones8, 0.125))
            vq.inc(vector.memset(ones127, 1.0 / 127.0))
            vq.inc(vector.memset(epsLN[:], 1e-5))
            vq.inc(vector.memset(epsRMS[:], 1e-6))
            vq.inc(vector.memset(epsTINY[:], 1e-30))
        v_consts = vq.n

        if DBG.get("s0_bc", True):
            @T
            def _(tensor):
                FL.wait(tensor, "t", c_ldc, ld_consts)
                FL.wait(tensor, "t", vq, v_consts)
                tensor.matmul(psB[:, 0:128], onesf[:], wrow[0:1, 0:128], start=True, stop=True)
                tq.inc(tensor.matmul(psB[:, 128:256], ones8, wrow[0:1, 128:256],
                                     start=True, stop=True))
                tensor.matmul(psA[:, 0:512], ones127, brow[0:1, 0:512], start=True, stop=True)
                tq.inc(tensor.matmul(psA[:, 512:1024], ones127, brow[0:1, 512:1024],
                                     start=True, stop=True))
            t_bc1 = tq.n

            @V
            def _(vector):
                FL.wait(vector, "v", tq, t_bc1)
                vq.inc(vector.tensor_copy(wmul[:], psB[:, 0:256]))
                vq.inc(vector.tensor_copy(scbB[:], psA[:]))
            v_bc1 = vq.n

            @SYNC
            def _(sync):
                FL.wait(sync, "s", tq, t_bc1)
                c_ldc.inc(sync.dma_start(brow[:], fbias.ap()))
            ld_f = c_ldc.n

            @T
            def _(tensor):
                FL.wait(tensor, "t", c_ldc, ld_f)
                FL.wait(tensor, "t", vq, v_bc1)
                tensor.matmul(psB[:, 0:512], onesf[:], brow[0:1, 0:512], start=True, stop=True)
                tq.inc(tensor.matmul(psB[:, 512:1024], onesf[:], brow[0:1, 512:1024],
                                     start=True, stop=True))
            t_bc2 = tq.n

            @SYNC
            def _(sync):
                FL.wait(sync, "s", tq, t_bc2)
                c_ldc.inc(sync.dma_start(brow[:], proj_b.ap()))
            ld_p = c_ldc.n

            @T
            def _(tensor):
                FL.wait(tensor, "t", c_ldc, ld_p)
                tensor.matmul(psA[:, 0:512], onesf[:], brow[0:1, 0:512], start=True, stop=True)
                tq.inc(tensor.matmul(psA[:, 512:1024], onesf[:], brow[0:1, 512:1024],
                                     start=True, stop=True))
            t_bc3 = tq.n

            @V
            def _(vector):
                FL.wait(vector, "v", tq, t_bc2)
                vq.inc(vector.tensor_copy(fbiasB[:], psB[:]))
                FL.wait(vector, "v", tq, t_bc3)
                vq.inc(vector.tensor_copy(projbB[:], psA[:]))

        def load_wT(dst_sb, src_dram, rows0, dst_cc_stride, dst_off, dst_free_key=None):
            @SYNC
            def _(sync):
                wf(sync, "s", "wstage")
                c_ldw.inc(sync.dma_start(wstage[:], src_dram.ap()[rows0:rows0 + 128, :]))
            mark_ld = c_ldw.n

            @V
            def _(vector):
                FL.wait(vector, "v", c_ldw, mark_ld)
                wf(vector, "v", "wstage_b")
                vq.inc(vector.tensor_copy(wstage_b[:], wstage[:]))
            setF("wstage", vq)
            mark_cast = vq.n

            for cc in range(8):
                slot = cc % 2

                @T
                def _(tensor, cc=cc, slot=slot):
                    FL.wait(tensor, "t", vq, mark_cast)
                    wf(tensor, "t", f"psT{slot}")
                    if dst_free_key:
                        wf(tensor, "t", dst_free_key)
                    tq.inc(tensor.transpose(psTs[slot][:],
                                            wstage_b[:, cc * 128:(cc + 1) * 128],
                                            identb[:]))
                mark_t = tq.n
                if cc == 7:
                    setF("wstage_b", tq)

                @V
                def _(vector, cc=cc, slot=slot):
                    FL.wait(vector, "v", tq, mark_t)
                    vq.inc(vector.tensor_copy(
                        dst_sb[:, cc * dst_cc_stride + dst_off:
                               cc * dst_cc_stride + dst_off + 128],
                        psTs[slot][:]))
                free["psT0"] = (vq, vq.n)
            return vq.n

        cb_ready = 0
        for r in range(8 if DBG.get("s0_cbt", True) else 0):
            cb_ready = load_wT(cbT, cbw, r * 128, C, r * 128)

        for ip in range((KC // 2) if nstage >= 1 else 0):
            marks_x = [0, 0]
            for j in range(2):
                i = ip * 2 + j

                @SYNC
                def _(sync, i=i, j=j):
                    wf(sync, "s", f"xt{j}")
                    c_ldx[j].inc(sync.dma_start(xt[j][:], xb.ap()[i * 128:(i + 1) * 128, :]))
                marks_x[j] = c_ldx[j].n

            @V
            def _(vector):
                for j in range(2):
                    FL.wait(vector, "v", c_ldx[j], marks_x[j])
                    FL.wait(vector, "v", aq, free["lnstat"][1] if "lnstat" in free else 0)
                    vq.inc(vector.reduce_sum(lnsum[:], xt[j][:], axis=AX.X))
                    vq.inc(vector.tensor_mul(xsq[:], xt[j][:], xt[j][:]))
                    FL.wait(vector, "v", vq, vq.n)
                    vq.inc(vector.reduce_sum(xss[:], xsq[:], axis=AX.X))
                    vq.inc(vector.tensor_scalar_mul(lnmean[:, j:j + 1], lnsum[:], 1.0 / C))
                    FL.wait(vector, "v", vq, vq.n)
                    vq.inc(vector.tensor_scalar_mul(lnex2[:, j:j + 1], xss[:], 1.0 / C))
                FL.wait(vector, "v", vq, vq.n)
                vq.inc(vector.tensor_mul(lnm2[:], lnmean[:], lnmean[:]))
                FL.wait(vector, "v", vq, vq.n)
                vq.inc(vector.tensor_sub(lnvar[:], lnex2[:], lnm2[:]))
            mark_var = vq.n

            @A
            def _(scalar):
                FL.wait(scalar, "a", vq, mark_var)
                aq.inc(scalar.activation(lnl[:], lnvar[:], AF.Ln, bias=epsLN[:], scale=1.0))
                FL.wait(scalar, "a", aq, aq.n)
                aq.inc(scalar.activation(lnr[:], lnl[:], AF.Exp, scale=-0.5))
            mark_rstd = aq.n
            free["lnstat"] = (aq, aq.n)

            for j in range(2):
                i = ip * 2 + j
                own = i < NOWN

                @V
                def _(vector, i=i, j=j):
                    FL.wait(vector, "v", aq, mark_rstd)
                    wf(vector, "v", f"xn{j}")
                    vq.inc(vector.tensor_scalar(xn_tok[j], xt[j][:],
                                                lnmean[:, j:j + 1], lnr[:, j:j + 1],
                                                ALU.subtract, ALU.mult))
                mark_xn = vq.n
                free[f"xt{j}"] = (vq, vq.n)

                for cc in range(8):
                    slot = cc % 2

                    @T
                    def _(tensor, cc=cc, j=j, slot=slot):
                        FL.wait(tensor, "t", vq, mark_xn)
                        wf(tensor, "t", f"psT{slot}")
                        wf(tensor, "t", f"xnTst{j}_dma")
                        wf(tensor, "t", f"xnTst{j}_ffn")
                        tq.inc(tensor.transpose(psTs[slot][:],
                                                xn_tok[j][:, cc * 128:(cc + 1) * 128],
                                                identb[:]))
                    mark_t = tq.n
                    if cc == 7:
                        free[f"xn{j}"] = (tq, tq.n)

                    @V
                    def _(vector, cc=cc, j=j, slot=slot):
                        FL.wait(vector, "v", tq, mark_t)
                        vq.inc(vector.tensor_copy(xnT_st[j][:, cc * 128:(cc + 1) * 128],
                                                  psTs[slot][:]))
                    free["psT0"] = (vq, vq.n)
                mark_xnT = vq.n

                @GP
                def _(gp, i=i, j=j):
                    FL.wait(gp, "g", vq, mark_xnT)
                    c_stx[j].inc(gp.dma_start(
                        xnT_h.ap()[:, :, i * 128:(i + 1) * 128].rearrange("c p t -> p c t"),
                        xnT_st[j].rearrange("p (c t) -> p c t", t=128)))
                free[f"xnTst{j}_dma"] = (c_stx[j], c_stx[j].n)

                if own:
                    @T
                    def _(tensor, j=j):
                        FL.wait(tensor, "t", vq, mark_xnT)
                        FL.wait(tensor, "t", vq, cb_ready)
                        wf(tensor, "t", "psB_v")
                        last = None
                        for half in range(2):
                            for cc in range(8):
                                last = tensor.matmul(
                                    psB[:, half * 512:(half + 1) * 512],
                                    xnT_st[j][:, cc * 128:(cc + 1) * 128],
                                    cbT[:, cc * C + half * 512: cc * C + half * 512 + 512],
                                    start=(cc == 0), stop=(cc == 7))
                        tq.inc(last)
                    mark_ffn = tq.n
                    free[f"xnTst{j}_ffn"] = (tq, tq.n)

                    @V
                    def _(vector, j=j):
                        FL.wait(vector, "v", tq, mark_ffn)
                        vq.inc(vector.tensor_mul(fo[:], psB[:], scbB[:]))
                        FL.wait(vector, "v", vq, vq.n)
                        wf(vector, "v", "fo2_st")
                        vq.inc(vector.tensor_add(fo2[:], fo[:], fbiasB[:]))
                    free["psB_v"] = (vq, vq.n)
                    mark_g = vq.n

                    @GP
                    def _(gp, i=i, j=j):
                        FL.wait(gp, "g", vq, mark_g)
                        c_stg[j].inc(gp.dma_start(g_h.ap()[i * 128:(i + 1) * 128, :], fo2[:]))
                    free["fo2_st"] = (c_stg[j], c_stg[j].n)

        stg_totals = [c_stg[0].n, c_stg[1].n]
        stx_totals = [c_stx[0].n, c_stx[1].n]
        free["wbT"] = (tq, tq.n)
        free["vbuf0"] = (tq, tq.n)
        free["outt0"] = (vq, vq.n)
        free["xt0"] = (vq, vq.n)
        free["xt1"] = (vq, vq.n)

        for i in range(NOWN if nstage >= 2 else 0):
            j = i % 2

            @SYNC
            def _(sync, i=i, j=j):
                FL.wait(sync, "s", c_stg[0], stg_totals[0])
                FL.wait(sync, "s", c_stg[1], stg_totals[1])
                wf(sync, "s", f"gld{j}")
                c_ldg[j].inc(sync.dma_start(gld[j][:], g_h.ap()[i * 128:(i + 1) * 128, :]))
            mark_g = c_ldg[j].n

            @A
            def _(scalar, j=j):
                FL.wait(scalar, "a", c_ldg[j], mark_g)
                wf(scalar, "a", f"outt{j}")
                aq.inc(scalar.activation(outt[j][:], gld[j][:], AF.Silu, scale=1.0))
            free[f"gld{j}"] = (aq, aq.n)
            mark_s = aq.n

            @GP
            def _(gp, i=i, j=j):
                FL.wait(gp, "g", aq, mark_s)
                c_stg[j].inc(gp.dma_start(g_h.ap()[i * 128:(i + 1) * 128, :], outt[j][:]))
            free[f"outt{j}"] = (c_stg[j], c_stg[j].n)

        stg_silu_totals = [c_stg[0].n, c_stg[1].n]

        free.update({
            "kT0": (tq, 0), "kT1": (tq, 0),
            "qT0": (tq, 0), "qT1": (tq, 0),
            "vbuf0": (tq, 0), "vbuf1": (tq, 0),
            "negMt0": (aq, 0), "negMt1": (aq, 0),
            "rms16": (aq, 0), "psQK": (vq, 0),
        })
        for sl in range(NLAG):
            free[f"qkvlag{sl}"] = (vq, 0)
        for sl in range(NSLAB):
            free[f"rms_sl{sl % 4}"] = (aq, 0)
            free[f"rstd_sl{sl % 2}"] = (vq, 0)
        free["kqhat0"] = (tq, 0)
        free["kqhat1"] = (tq, 0)
        kq_hat2 = [kq_hat, kq_hat]
        rms64 = sb("rms64", [128, 64], F32)
        rstd32 = sb("rstd32", [128, 32], F32)
        mark_negM_p = [0, 0]
        d01 = gld[0][0:1, 0:1024]
        r01 = gld[1][0:1, 0:1024]
        lnscr = s1t[0:1, 0:1024]
        sh = {}

        def wtprep_steps(hp):
            steps = []
            for gidx, rows0 in enumerate((C + hp * 128, hp * 128, 2 * C + hp * 128)):
                def dma_fn(rows0=rows0, gidx=gidx):
                    @SYNC
                    def _(sync):
                        wf(sync, "s", "wstage")
                        c_ldw.inc(sync.dma_start(wstage[:],
                                                 qkv_w.ap()[rows0:rows0 + 128, :]))
                    sh[f"wld{gidx}"] = c_ldw.n
                steps.append(("S", dma_fn))

                def cast_fn(gidx=gidx):
                    @V
                    def _(vector):
                        FL.wait(vector, "v", c_ldw, sh[f"wld{gidx}"])
                        wf(vector, "v", "wstage_b")
                        vq.inc(vector.tensor_copy(wstage_b[:], wstage[:]))
                    setF("wstage", vq)
                    sh[f"wcast{gidx}"] = vq.n
                steps.append(("V", cast_fn))

                for cc in range(8):
                    def tp_fn(gidx=gidx, cc=cc):
                        @T
                        def _(tensor):
                            FL.wait(tensor, "t", vq, sh[f"wcast{gidx}"])
                            wf(tensor, "t", "psT0")
                            wf(tensor, "t", "wt_buf")
                            tq.inc(tensor.transpose(psT0[:],
                                                    wstage_b[:, cc * 128:(cc + 1) * 128],
                                                    identb[:]))
                        sh["wtp"] = tq.n
                        if cc == 7:
                            setF("wstage_b", tq)
                    steps.append(("T", tp_fn))

                    def ev_fn(gidx=gidx, cc=cc):
                        @V
                        def _(vector):
                            FL.wait(vector, "v", tq, sh["wtp"])
                            vq.inc(vector.tensor_copy(
                                wt_buf[:, cc * 384 + gidx * 128:
                                       cc * 384 + gidx * 128 + 128],
                                psT0[:]))
                        free["psT0"] = (vq, vq.n)
                        sh["wt_ready"] = vq.n
                    steps.append(("V", ev_fn))
            return steps

        def s3_streams(hp):
            p = hp % 2
            kT, qT, vbuf = kTp[p], qTp[p], vbufp[p]
            st1 = [[] for _ in range(NSLAB)]
            st2 = [[] for _ in range(NSLAB)]
            tail = []

            def init_fn():
                @V
                def _(vector):
                    wf(vector, "v", "maxacc")
                    vq.inc(vector.memset(macc128[:], 0.0))
                    wf(vector, "v", f"vbuf{p}")
                    vq.inc(vector.memset(
                        vbuf.rearrange("p (c j) -> p c j", j=65)[:, :, 64:65], 1.0))
                sh[f"vones{hp}"] = vq.n
            st1[0].append(("V", init_fn))

            for s in range(NSLAB):
                jsl = s % 2

                def dma_fn(s=s, jsl=jsl):
                    @SYNC
                    def _(sync):
                        wf(sync, "s", f"xslab{jsl}")
                        FL.wait(sync, "s", c_stx[0], stx_totals[0])
                        FL.wait(sync, "s", c_stx[1], stx_totals[1])
                        for cc in range(8):
                            c_lds[jsl].inc(sync.dma_start(
                                xslab[jsl][:, cc * 512:(cc + 1) * 512],
                                xnT_h.ap()[cc, :, s * 512:(s + 1) * 512]))
                    sh[f"slab{s}"] = c_lds[jsl].n
                st1[s].append(("S", dma_fn))

                for c4 in range(4):
                    i = s * 4 + c4
                    slot = (s % NLAG) * 4 + c4

                    def qkv_fn(s=s, c4=c4, i=i, jsl=jsl):
                        @T
                        def _(tensor):
                            FL.wait(tensor, "t", c_lds[jsl], sh[f"slab{s}"])
                            FL.wait(tensor, "t", vq, sh["wt_ready"])
                            FL.wait(tensor, "t", vq, sh[f"vones{hp}"])
                            wf(tensor, "t", "psQK")
                            last = None
                            for cc in range(8):
                                last = tensor.matmul(
                                    psQK[:, 0:384],
                                    xslab[jsl][:, cc * 512 + c4 * 128:
                                               cc * 512 + c4 * 128 + 128],
                                    wt_buf[:, cc * 384: cc * 384 + 384],
                                    start=(cc == 0), stop=(cc == 7))
                            tq.inc(last)
                        sh[f"qkv{i}"] = tq.n
                        if c4 == 3:
                            free[f"xslab{jsl}"] = (tq, tq.n)
                        if s == NSLAB - 1 and c4 == 3:
                            setF("wt_buf", tq)
                    st1[s].append(("T", qkv_fn))

                    def evac_fn(s=s, c4=c4, i=i, slot=slot):
                        @V
                        def _(vector):
                            FL.wait(vector, "v", tq, sh[f"qkv{i}"])
                            wf(vector, "v", f"qkvlag{slot // 4}")
                            vq.inc(vector.tensor_copy(
                                qkvlag[:, slot * 384:(slot + 1) * 384],
                                psQK[:, 0:384]))
                            FL.wait(vector, "v", vq, vq.n)
                            vq.inc(vector.tensor_mul(
                                sqt[:],
                                qkvlag[:, slot * 384: slot * 384 + 256],
                                qkvlag[:, slot * 384: slot * 384 + 256]))
                            FL.wait(vector, "v", vq, vq.n)
                            wf(vector, "v", f"rms_sl{s % 4}")
                            vq.inc(vector.reduce_sum(
                                rms64[:, (s % 4) * 16 + c4 * 4:
                                      (s % 4) * 16 + (c4 + 1) * 4],
                                sqt.rearrange("p (g d) -> p g d", d=DH),
                                axis=AX.X))
                        free["psQK"] = (vq, vq.n)
                        sh[f"rms{i}"] = vq.n
                    st1[s].append(("V", evac_fn))

                def rstd_fn(s=s):
                    @A
                    def _(scalar):
                        FL.wait(scalar, "a", aq, aq.n)
                        FL.wait(scalar, "a", vq, sh[f"rms{s * 4 + 3}"])
                        wf(scalar, "a", f"rstd_sl{s % 2}")
                        aq.inc(scalar.activation(
                            rstdL[:], rms64[:, (s % 4) * 16:(s % 4) * 16 + 16],
                            AF.Ln, bias=epsRMS[:], scale=1.0 / DH))
                        FL.wait(scalar, "a", aq, aq.n)
                        aq.inc(scalar.activation(
                            rstd32[:, (s % 2) * 16:(s % 2) * 16 + 16],
                            rstdL[:], AF.Exp, scale=-0.5))
                    free[f"rms_sl{s % 4}"] = (aq, aq.n)
                    sh[f"rstd{s}"] = aq.n
                st2[s].append(("A", rstd_fn))

                for c4 in range(4):
                    i = s * 4 + c4
                    slot = (s % NLAG) * 4 + c4
                    own = i < NOWN
                    kpar = i % 2

                    def norm_fn(s=s, c4=c4, i=i, slot=slot, kpar=kpar, p=p):
                        @V
                        def _(vector):
                            FL.wait(vector, "v", aq, sh[f"rstd{s}"])
                            wf(vector, "v", f"kqhat{kpar}")
                            base = slot * 384
                            rb_ = (s % 2) * 16 + c4 * 4
                            for g_ in range(4):
                                vq.inc(vector.tensor_scalar(
                                    tmn[:, g_ * 64:(g_ + 1) * 64],
                                    qkvlag[:, base + g_ * 64: base + (g_ + 1) * 64],
                                    rstd32[:, rb_ + g_: rb_ + g_ + 1], None, ALU.mult))
                            FL.wait(vector, "v", vq, vq.n)
                            vq.inc(vector.tensor_mul(kq_hat2[kpar][:], tmn[:], wmul[:]))
                            dst = vbufp[p].rearrange("p (h k j) -> p h k j", h=2, j=65)[
                                :, :, i:i + 1, 0:64].rearrange("p h k j -> p (h k) j")
                            srcv = qkvlag[:, base + 256: base + 384].rearrange(
                                "p (h j) -> p h j", h=2)
                            vq.inc(vector.tensor_copy(dst, srcv))
                        if c4 == 3:
                            free[f"qkvlag{slot // 4}"] = (vq, vq.n)
                        if s == NSLAB - 1 and c4 == 3:
                            free[f"rstd_sl{s % 2}"] = (vq, vq.n)
                        else:
                            free[f"rstd_sl{s % 2}"] = (vq, vq.n)
                        sh[f"norm{i}"] = vq.n
                    st2[s].append(("V", norm_fn))

                    def tpk_fn(i=i, kpar=kpar, p=p):
                        @T
                        def _(tensor):
                            FL.wait(tensor, "t", vq, sh[f"norm{i}"])
                            wf(tensor, "t", "psT0")
                            wf(tensor, "t", f"kT{p}")
                            tq.inc(tensor.transpose(psT0[:], kq_hat2[kpar][:, 0:128],
                                                    identb[:]))
                        sh[f"tpk{i}"] = tq.n
                    st2[s].append(("T", tpk_fn))

                    def evk_fn(i=i, p=p, own=own, kpar=kpar):
                        @V
                        def _(vector):
                            FL.wait(vector, "v", tq, sh[f"tpk{i}"])
                            vq.inc(vector.tensor_copy(
                                kTp[p][:, i * 128:(i + 1) * 128], psT0[:]))
                            free["psT0"] = (vq, vq.n)
                            if not own:
                                vq.inc(vector.tensor_mul(sqm[:], kq_hat2[kpar][:],
                                                         kq_hat2[kpar][:]))
                                FL.wait(vector, "v", vq, vq.n)
                                vq.inc(vector.reduce_sum(
                                    m4[:], sqm.rearrange("p (g d) -> p g d", d=DH),
                                    axis=AX.X))
                                FL.wait(vector, "v", vq, vq.n)
                                wf(vector, "v", "maxacc")
                                vq.inc(vector.tensor_max(macc128[:, 0:2],
                                                         macc128[:, 0:2], m4[:, 0:2]))
                                free["maxacc"] = (vq, vq.n)
                        if not own:
                            free["kqhat0"] = (tq, sh[f"tpk{i}"])
                        sh[f"evk{i}"] = vq.n
                    st2[s].append(("V", evk_fn))

                    if own:
                        def tpq_fn(i=i, kpar=kpar, p=p):
                            @T
                            def _(tensor):
                                FL.wait(tensor, "t", vq, sh[f"evk{i}"])
                                wf(tensor, "t", "psT0")
                                wf(tensor, "t", f"qT{p}")
                                tq.inc(tensor.transpose(psT0[:],
                                                        kq_hat2[kpar][:, 128:256],
                                                        identb[:]))
                            sh[f"tpq{i}"] = tq.n
                        st2[s].append(("T", tpq_fn))

                        def evq_fn(i=i, p=p, kpar=kpar):
                            @V
                            def _(vector):
                                FL.wait(vector, "v", tq, sh[f"tpq{i}"])
                                vq.inc(vector.tensor_copy(
                                    qTp[p][:, i * 128:(i + 1) * 128], psT0[:]))
                                free["psT0"] = (vq, vq.n)
                                vq.inc(vector.tensor_mul(sqm[:], kq_hat2[kpar][:],
                                                         kq_hat2[kpar][:]))
                                FL.wait(vector, "v", vq, vq.n)
                                vq.inc(vector.reduce_sum(
                                    m4[:], sqm.rearrange("p (g d) -> p g d", d=DH),
                                    axis=AX.X))
                                FL.wait(vector, "v", vq, vq.n)
                                wf(vector, "v", "maxacc")
                                vq.inc(vector.tensor_max(macc128[:], macc128[:], m4[:]))
                                free["maxacc"] = (vq, vq.n)
                            free["kqhat0"] = (tq, sh[f"tpq{i}"])
                        st2[s].append(("V", evq_fn))

            def tail1(p=p):
                sh[f"kq_ready{p}"] = vq.n

                @V
                def _(vector):
                    FL.wait(vector, "v", vq, vq.n)
                    vq.inc(vector.tensor_copy(m4b[:], macc128[:]))
                sh["mab"] = vq.n
            tail.append(("V", tail1))

            def tail2():
                @T
                def _(tensor):
                    FL.wait(tensor, "t", vq, sh["mab"])
                    wf(tensor, "t", "psT0")
                    tq.inc(tensor.transpose(psT0[0:4, 0:128], m4b[:], identb[:]))
                sh["mat"] = tq.n
            tail.append(("T", tail2))

            def tail3():
                @V
                def _(vector):
                    FL.wait(vector, "v", tq, sh["mat"])
                    vq.inc(vector.reduce_max(mx4[:], psT0[0:4, 0:128], axis=AX.X))
                    free["psT0"] = (vq, vq.n)
                    FL.wait(vector, "v", vq, vq.n)
                    vq.inc(vector.tensor_copy(maxaccb[:], mx4[:]))
                sh["mab2"] = vq.n
            tail.append(("V", tail3))

            def tail4():
                @T
                def _(tensor):
                    FL.wait(tensor, "t", vq, sh["mab2"])
                    wf(tensor, "t", "psT0")
                    tq.inc(tensor.transpose(psT0[0:1, 0:4], maxaccb[:], identb[0:4, 0:4]))
                sh["mat2"] = tq.n
            tail.append(("T", tail4))

            def tail5():
                @V
                def _(vector):
                    FL.wait(vector, "v", tq, sh["mat2"])
                    vq.inc(vector.tensor_copy(mrow[:], psT0[0:1, 0:4]))
                    free["psT0"] = (vq, vq.n)
                    FL.wait(vector, "v", vq, vq.n)
                    vq.inc(vector.tensor_mul(mab[:], mrow[0:1, 0:2], mrow[0:1, 2:4]))
                    FL.wait(vector, "v", vq, vq.n)
                    vq.inc(vector.reduce_max(mab[0:1, 0:1], mab[:], axis=AX.X))
                sh["mabf"] = vq.n
            tail.append(("V", tail5))

            def tail6():
                @A
                def _(scalar):
                    FL.wait(scalar, "a", vq, sh["mabf"])
                    aq.inc(scalar.activation(mln[0:1, 0:1], mab[0:1, 0:1], AF.Ln,
                                             bias=epsTINY[:], scale=1.0))
                    FL.wait(scalar, "a", aq, aq.n)
                    aq.inc(scalar.activation(negm2[0:1, 0:1], mln[0:1, 0:1],
                                             AF.Exp, scale=0.5))
                sh["mh"] = aq.n
            tail.append(("A", tail6))

            def tail7():
                @T
                def _(tensor):
                    FL.wait(tensor, "t", aq, sh["mh"])
                    wf(tensor, "t", "psQK")
                    tq.inc(tensor.matmul(psQK[:, 384:385], onesf[:], negm2[0:1, 0:1],
                                         start=True, stop=True))
                sh["mq"] = tq.n
            tail.append(("T", tail7))

            def tail8(p=p):
                @V
                def _(vector):
                    FL.wait(vector, "v", tq, sh["mq"])
                    wf(vector, "v", f"negMt{p}")
                    vq.inc(vector.tensor_scalar_mul(negMt[:, p:p + 1],
                                                    psQK[:, 384:385], -1.0))
                free["psQK"] = (vq, vq.n)
                mark_negM_p[p] = vq.n
            tail.append(("V", tail8))
            return st1, st2, tail

        def run_steps(steps):
            for eng_, fn in steps:
                fn()

        prev_ofree = (vq, 0)
        prev_bankfree = (vq, 0)
        prev_exp_end = 0

        if nstage >= 3:
            run_steps(wtprep_steps(0))
            st1_, st2_, tail_ = s3_streams(0)
            for s in range(NSLAB):
                run_steps(st1_[s])
                run_steps(st2_[s])
            run_steps(tail_)

        for w in range(HP if nstage >= 4 else 0):
            p_r = w % 2
            kT, qT, vbuf = kTp[p_r], qTp[p_r], vbufp[p_r]
            if w + 1 < HP and nstage >= 3:
                nxt_pre = wtprep_steps(w + 1)
                st1_, st2_, tail_ = s3_streams(w + 1)
                qb_assign = {i: [] for i in range(NQB)}
                qb_assign[0].extend(nxt_pre)
                leftover = []
                for s in range(NSLAB):
                    qb_assign[min(s // 2, NQB - 1)].extend(st1_[s])
                for s in range(NSLAB):
                    dst_qb = s // 2 + 1
                    if dst_qb < NQB:
                        qb_assign[dst_qb].extend(st2_[s])
                    else:
                        leftover.extend(st2_[s])
                tail_steps = leftover + tail_
            else:
                qb_assign = {i: [] for i in range(NQB)}
                leftover = []
                tail_steps = []

            for qb in range(NQB):
                aq_base = aq.n
                exp_mark = {kc: 0 for kc in range(KC)}
                pending = qb_assign[qb]
                a_steps = [fn for eng_, fn in pending if eng_ == "A"]
                other = [(eng_, fn) for eng_, fn in pending if eng_ != "A"]
                for fn in a_steps:
                    fn()

                aq_base = aq.n
                for kc in range(KC):
                    exp_mark[kc] = aq_base + kc + 1
                aq.reserve(KC)
                mark_r = aq.reserve(2)

                s_mark = {}
                pv_mark = {}
                psS = (psA, psB)
                noct = 8
                per = (len(other) + noct - 1) // noct if other else 0
                oc_steps = [other[o * per:(o + 1) * per] for o in range(noct)]

                for oc in range(noct):
                    kc_lo, kc_hi = oc * (KC // noct), (oc + 1) * (KC // noct)
                    if oc == noct - 1:
                        kc_hi = KC + 1
                    kc_hi = min(kc_hi, KC + 1)

                    @T
                    def _(tensor, qb=qb, kc_lo=kc_lo, kc_hi=kc_hi, p_r=p_r):
                        FL.wait(tensor, "t", vq, sh.get(f"kq_ready{p_r}", 0) or mark_negM_p[p_r])
                        for kc in range(kc_lo, kc_hi):
                            if kc < KC:
                                par = kc % 2
                                FL.wait(tensor, "t", aq,
                                        exp_mark.get(kc - 2, 0) if kc >= 2 else prev_exp_end)
                                tensor.matmul(psS[par][:, 0:512],
                                              kT[0:64, kc * 128:(kc + 1) * 128],
                                              qT[0:64, qb * 512:(qb + 1) * 512],
                                              start=True, stop=True)
                                tq.inc(tensor.matmul(psS[par][:, 512:1024],
                                                     kT[64:128, kc * 128:(kc + 1) * 128],
                                                     qT[64:128, qb * 512:(qb + 1) * 512],
                                                     start=True, stop=True,
                                                     tile_position=(64, 0)))
                                s_mark[kc] = tq.n
                            if kc >= 1:
                                kp = kc - 1
                                par = kp % 2
                                FL.wait(tensor, "t", aq, exp_mark[kp])
                                FL.wait(tensor, "t", vq, prev_ofree[1])
                                tensor.matmul(psO0[:],
                                              vbuf[:, kp * 65: kp * 65 + 65],
                                              ptb[par][:, 0:512],
                                              start=(kp == 0), stop=(kp == KC - 1))
                                tq.inc(tensor.matmul(
                                    psO1[:],
                                    vbuf[:, (KC + kp) * 65: (KC + kp) * 65 + 65],
                                    ptb[par][:, 512:1024],
                                    start=(kp == 0), stop=(kp == KC - 1)))
                                pv_mark[kp] = tq.n
                    for eng_, fn in oc_steps[oc]:
                        fn()
                mark_pv_last = pv_mark[KC - 1]

                @V
                def _(vector, qb=qb, hp=w):
                    FL.wait(vector, "v", tq, mark_pv_last)
                    vq.inc(vector.tensor_copy(d01[0:1, 0:512], psO0[64:65, :]))
                    vq.inc(vector.tensor_copy(d01[0:1, 512:1024], psO1[64:65, :]))
                mark_d = vq.n

                @T
                def _(tensor, qb=qb):
                    FL.wait(tensor, "t", aq, mark_r)
                    wf(tensor, "t", "psQK")
                    tensor.matmul(psQK[0:64, 0:512], onesf[0:1, 0:64],
                                  r01[0:1, 0:512], start=True, stop=True)
                    tq.inc(tensor.matmul(psQK[64:128, 0:512], onesf[0:1, 0:64],
                                         r01[0:1, 512:1024],
                                         start=True, stop=True, tile_position=(0, 64)))
                mark_bc = tq.n

                @V
                def _(vector, qb=qb, hp=w):
                    FL.wait(vector, "v", tq, mark_bc)
                    vq.inc(vector.tensor_copy(rb0[:], psQK[0:64, 0:512]))
                    vq.inc(vector.tensor_copy(rb1[:], psQK[64:128, 0:512]))
                    FL.wait(vector, "v", vq, vq.n)
                    jo = qb % 2
                    wf(vector, "v", f"obuf{jo}")
                    vq.inc(vector.tensor_mul(obuf[jo][0:64, :], psO0[0:64, :], rb0[:]))
                    vq.inc(vector.tensor_mul(obuf[jo][64:128, :], psO1[0:64, :], rb1[:]))
                free["psQK"] = (vq, vq.n)
                prev_ofree = (vq, vq.n)
                mark_ob = vq.n

                @A
                def _(scalar, qb=qb, p_r=p_r):
                    FL.wait(scalar, "a", vq, mark_negM_p[p_r])
                    for kc in range(KC):
                        par = kc % 2
                        FL.wait(scalar, "a", tq, s_mark[kc])
                        FL.wait(scalar, "a", tq, pv_mark.get(kc - 2, 0))
                        aq.inc_pre(scalar.activation(ptb[par][:], psS[par][:], AF.Exp,
                                                     bias=negMt[:, p_r:p_r + 1],
                                                     scale=1.0))
                    FL.wait(scalar, "a", vq, mark_d)
                    aq.inc_pre(scalar.activation(lnscr, d01, AF.Ln, scale=1.0))
                    FL.wait(scalar, "a", aq, mark_r - 1)
                    aq.inc_pre(scalar.activation(r01, lnscr, AF.Exp, scale=-1.0))
                FL.f.setdefault("a", {})[id(aq)] = max(
                    FL.f.get("a", {}).get(id(aq), 0), mark_r)
                prev_exp_end = mark_r

                @GP
                def _(gp, qb=qb, hp=w):
                    FL.wait(gp, "g", vq, mark_ob)
                    jo = qb % 2
                    c_sto[jo].inc(gp.dma_start(oT_h.ap()[hp, :, qb * 512:(qb + 1) * 512],
                                               obuf[jo][:]))
                free[f"obuf{qb % 2}"] = (c_sto[qb % 2], c_sto[qb % 2].n)

            run_steps(tail_steps)
            free[f"kT{p_r}"] = (tq, tq.n)
            free[f"qT{p_r}"] = (tq, tq.n)
            free[f"vbuf{p_r}"] = (tq, tq.n)
            free[f"negMt{p_r}"] = (aq, aq.n)

        sto_totals = [c_sto[0].n, c_sto[1].n]

        for r in range(8 if DBG.get("s5_prep", True) else 0):
            load_wT(projT, proj_w, r * 128, C, r * 128, dst_free_key="wbT")
        mark_pj = vq.n

        for i in range(NOWN if nstage >= 5 else 0):
            j = i % 2

            @SYNC
            def _(sync, i=i, j=j):
                FL.wait(sync, "s", c_sto[0], sto_totals[0])
                FL.wait(sync, "s", c_sto[1], sto_totals[1])
                FL.wait(sync, "s", c_stg[0], stg_silu_totals[0])
                FL.wait(sync, "s", c_stg[1], stg_silu_totals[1])
                wf(sync, "s", f"ostage{j}")
                for hp in range(HP):
                    c_ld5[j].inc(sync.dma_start(ostage[j][:, hp * 128:(hp + 1) * 128],
                                                oT_h.ap()[hp, :, i * 128:(i + 1) * 128]))
                wf(sync, "s", f"gld{j}")
                c_ld5[j].inc(sync.dma_start(gld[j][:], g_h.ap()[i * 128:(i + 1) * 128, :]))
                wf(sync, "s", f"xt{j}")
                c_ld5[j].inc(sync.dma_start(xld[j][:], xb.ap()[i * 128:(i + 1) * 128, :]))
            mark_ld = c_ld5[j].n

            @T
            def _(tensor, i=i, j=j):
                FL.wait(tensor, "t", c_ld5[j], mark_ld)
                FL.wait(tensor, "t", vq, mark_pj)
                FL.wait(tensor, "t", vq, free["psA_s5"][1] if "psA_s5" in free else 0)
                last = None
                for half in range(2):
                    for cc in range(8):
                        last = tensor.matmul(
                            psA[:, half * 512:(half + 1) * 512],
                            ostage[j][:, cc * 128:(cc + 1) * 128],
                            projT[:, cc * C + half * 512: cc * C + half * 512 + 512],
                            start=(cc == 0), stop=(cc == 7))
                tq.inc(last)
            free[f"ostage{j}"] = (tq, tq.n)
            mark_proj = tq.n

            @V
            def _(vector, i=i, j=j):
                FL.wait(vector, "v", tq, mark_proj)
                vq.inc(vector.tensor_add(s1t[:], psA[:], projbB[:]))
                FL.wait(vector, "v", vq, vq.n)
                vq.inc(vector.tensor_mul(s2t[:], s1t[:], gld[j][:]))
                FL.wait(vector, "v", vq, vq.n)
                wf(vector, "v", f"outt{j}")
                vq.inc(vector.tensor_add(outt[j][:], s2t[:], xld[j][:]))
            free["psA_s5"] = (vq, vq.n)
            free[f"gld{j}"] = (vq, vq.n)
            free[f"xt{j}"] = (vq, vq.n)
            mark_out = vq.n

            @GP
            def _(gp, i=i, j=j):
                FL.wait(gp, "g", vq, mark_out)
                c_stu[j].inc(gp.dma_start(out.ap()[i * 128:(i + 1) * 128, :], outt[j][:]))
            free[f"outt{j}"] = (c_stu[j], c_stu[j].n)

        stu_totals = [c_stu[0].n, c_stu[1].n]

        @GP
        def _(gp):
            gp.wait_ge(c_stu[0].sem, stu_totals[0])
            gp.wait_ge(c_stu[1].sem, stu_totals[1])

        @SYNC
        def _(sync):
            sync.wait_ge(c_stu[0].sem, stu_totals[0])
            sync.wait_ge(c_stu[1].sem, stu_totals[1])

    return nc



def _shard_inputs(L0_attn_x, L0_attn_qkv_w, L0_attn_proj_w, L0_attn_proj_b,
                  L0_attn_q_norm_w, L0_attn_k_norm_w, L0_ffn_CB, L0_ffn_SCB,
                  L0_ffn_bias):
    x = np.ascontiguousarray(np.asarray(L0_attn_x, dtype=np.float32))
    B, N, Cc = x.shape
    common = {
        "qkv_w": np.ascontiguousarray(np.asarray(L0_attn_qkv_w, np.float32)),
        "proj_w": np.ascontiguousarray(np.asarray(L0_attn_proj_w, np.float32)),
        "proj_b": np.asarray(L0_attn_proj_b, np.float32).reshape(1, Cc),
        "qnw": np.asarray(L0_attn_q_norm_w, np.float32).reshape(1, DH),
        "knw": np.asarray(L0_attn_k_norm_w, np.float32).reshape(1, DH),
        "cbw": np.ascontiguousarray(np.asarray(L0_ffn_CB, np.float32)),
        "scb": np.asarray(L0_ffn_SCB, np.float32).reshape(1, Cc),
        "fbias": np.asarray(L0_ffn_bias, np.float32).reshape(1, Cc),
        "ident_in": np.eye(128, dtype=np.float32).astype(bfdt),
    }
    in_maps = []
    half = N // 2
    for c in range(8):
        b, h = c // 2, c % 2
        own = x[b, h * half:(h + 1) * half]
        other = x[b, (1 - h) * half:(2 - h) * half]
        m = dict(common)
        m["xb"] = np.ascontiguousarray(np.concatenate([own, other], axis=0))
        in_maps.append(m)
    return in_maps, (B, N, Cc)


_CACHE = {}


def kernel(**inputs):
    in_maps, (B, N, Cc) = _shard_inputs(**inputs)
    key = (N,)
    if key not in _CACHE:
        _CACHE[key] = build(T_ALL=N, T_OWN=N // 2)
    nc = _CACHE[key]
    res = run_bass_kernel_spmd(nc, in_maps, core_ids=list(range(8)))
    half = N // 2
    outp = np.empty((B, N, Cc), dtype=np.float32)
    for c in range(8):
        b, h = c // 2, c % 2
        outp[b, h * half:(h + 1) * half] = res.results[c]["out"]
    return outp


# version 18
# speedup vs baseline: 1.9871x; 1.2440x over previous
import itertools

import numpy as np
import ml_dtypes
from contextlib import ExitStack

import concourse.bass as bass
import concourse.mybir as mybir
from concourse.bass_utils import run_bass_kernel_spmd

F32 = mybir.dt.float32
BF16 = mybir.dt.bfloat16
AF = mybir.ActivationFunctionType
ALU = mybir.AluOpType
AX = mybir.AxisListType
bfdt = ml_dtypes.bfloat16

C = 1024
DH = 64
HP = 8
DBG = {}


class Chain:
    def __init__(self, sem, step=1):
        self.sem = sem
        self.n = 0
        self.step = step

    def inc(self, inst, k=None):
        k = self.step if k is None else k
        inst.then_inc(self.sem, k)
        self.n += k
        return self.n

    def reserve(self, k=1):
        self.n += k
        return self.n

    def inc_pre(self, inst, k=1):
        inst.then_inc(self.sem, k)
        return inst


class Floors:
    def __init__(self):
        self.f = {}

    def wait(self, eng, key, chain, n):
        if n <= 0:
            return
        d = self.f.setdefault(key, {})
        if n > d.get(id(chain), 0):
            assert n <= chain.n, f"wait {n} > emitted {chain.n} ({key})"
            eng.wait_ge(chain.sem, n)
            d[id(chain)] = n


def build(T_ALL=8192, T_OWN=4096, nstage=99):
    KC = T_ALL // 128
    NSLAB = T_ALL // 512
    NQB = T_OWN // 512
    NOWN = T_OWN // 128
    NG = KC // 2

    nc = bass.Bass("TRN2", target_bir_lowering=False)

    xb = nc.dram_tensor("xb", [T_ALL, C], F32, kind="ExternalInput")
    qkv_t = nc.dram_tensor("qkv_t", [HP, 128, 3 * 8 * 128], F32, kind="ExternalInput")
    proj_t = nc.dram_tensor("proj_t", [128, 8 * C], F32, kind="ExternalInput")
    proj_b = nc.dram_tensor("proj_b", [1, C], F32, kind="ExternalInput")
    qnw = nc.dram_tensor("qnw", [1, DH], F32, kind="ExternalInput")
    knw = nc.dram_tensor("knw", [1, DH], F32, kind="ExternalInput")
    cb_t = nc.dram_tensor("cb_t", [128, 8 * C], F32, kind="ExternalInput")
    scb = nc.dram_tensor("scb", [1, C], F32, kind="ExternalInput")
    fbias = nc.dram_tensor("fbias", [1, C], F32, kind="ExternalInput")
    ident_in = nc.dram_tensor("ident_in", [128, 128], BF16, kind="ExternalInput")
    identf_in = nc.dram_tensor("identf_in", [128, 128], F32, kind="ExternalInput")

    out = nc.dram_tensor("out", [T_OWN, C], F32, kind="ExternalOutput")

    xnT_h = nc.dram_tensor("xnT_h", [8, 128, T_ALL], BF16, kind="Internal")
    g_h = nc.dram_tensor("g_h", [T_OWN, C], F32, kind="Internal")
    oT_h = nc.dram_tensor("oT_h", [HP, 128, T_OWN], BF16, kind="Internal")

    with ExitStack() as ctx:
        sb = lambda n_, s_, d_: ctx.enter_context(nc.sbuf_tensor(n_, s_, d_))
        pt = lambda n_, s_, d_: ctx.enter_context(nc.psum_tensor(n_, s_, d_))
        mksem = lambda n_: ctx.enter_context(nc.semaphore(n_))

        identb = sb("identb", [128, 128], BF16)
        identf = sb("identf", [128, 128], F32)
        onesf = sb("onesf", [1, 128], F32)
        epsLN = sb("epsLN", [128, 1], F32)
        epsRMS = sb("epsRMS", [128, 1], F32)
        epsTINY = sb("epsTINY", [1, 1], F32)

        brow = sb("brow", [1, C], F32)
        wmul = sb("wmul", [128, 256], F32)
        
        scbB = sb("scbB", [128, C], F32)
        fbiasB = sb("fbiasB", [128, C], F32)
        projbB = sb("projbB", [128, C], F32)

        xt = [sb(f"xt{j}", [128, C], F32) for j in range(2)]
        lnsum = sb("lnsum", [128, 1], F32)

        xss = sb("xss", [128, 1], F32)
        lnmean = sb("lnmean", [128, 2], F32)
        lnex2 = sb("lnex2", [128, 2], F32)
        lnm2 = sb("lnm2", [128, 2], F32)
        lnvar = sb("lnvar", [128, 2], F32)
        lnl = sb("lnl", [128, 2], F32)
        lnr = sb("lnr", [128, 2], F32)
        xn_tok = None

        wst3 = sb("wst3", [128, 2 * C], F32)
        cbT = sb("cbT", [128, 8 * C], BF16)
        projT = cbT
        wt_buf = sb("wt_buf", [128, 8 * 384], BF16)


        gld = [sb(f"gld{j}", [128, C], F32) for j in range(2)]
        wrow = gld[1][0:1, 0:256]
        ones8 = gld[0][0:1, 0:128]
        ones127 = gld[0][0:1, 128:256]

        xslab = [sb(f"xslab{j}", [128, 8 * 512], BF16) for j in range(2)]
        kTp = [sb(f"kT{p}", [128, T_ALL], BF16) for p in range(2)]
        qTp = [sb(f"qT{p}", [128, T_OWN], BF16) for p in range(2)]
        vbufp = [sb(f"vbuf{p}", [128, 2 * KC * 65], BF16) for p in range(2)]
        if 2 * KC * 65 >= 2 * C:
            xn_tok = [vbufp[0][:, 0:C], vbufp[0][:, C:2 * C]]
        else:
            _xn = [sb(f"xn_tok{j}", [128, C], BF16) for j in range(2)]
            xn_tok = [_xn[0][:], _xn[1][:]]
        NLAG = min(4, NSLAB)
        qkvlag = sb("qkvlag", [128, NLAG * 4 * 384], BF16)
        sqt = sb("sqt", [128, 256], F32)
        macc128 = sb("macc128", [128, 4], F32)
        rms16 = sb("rms16", [128, 16], F32)
        rstdL = sb("rstdL", [128, 16], F32)
        rstd16 = sb("rstd16", [128, 16], F32)
        tmn = sb("tmn", [128, 256], F32)
        kq_hat = sb("kq_hat", [128, 256], BF16)
        sqm = sb("sqm", [128, 256], BF16)
        m4 = sb("m4", [128, 4], F32)
        m4b = sb("m4b", [128, 4], BF16)
        mx4 = sb("mx4", [4, 1], F32)
        maxaccb = sb("maxaccb", [4, 1], BF16)
        mrow = sb("mrow", [1, 4], F32)
        mab = sb("mab", [1, 2], F32)
        mln = sb("mln", [1, 2], F32)
        negm2 = sb("negm2", [1, 2], F32)
        negMt = sb("negMt", [128, 2], F32)

        ptb = [sb(f"ptb{j}", [128, 1024], BF16) for j in range(2)]
        xnT_st = ptb

        rb0 = sb("rb0", [64, 512], F32)
        rb1 = sb("rb1", [64, 512], F32)
        obuf = [sb(f"obuf{j}", [128, 512], BF16) for j in range(2)]

        ostage = [sb(f"ostage{j}", [128, 8 * 128], BF16) for j in range(2)]
        xld = xt
        s1t = sb("s1t", [128, C], F32)
        s2t = sb("s2t", [128, C], F32)
        outt = [sb(f"outt{j}", [128, C], F32) for j in range(2)]
        xsq = s1t
        fo = s2t
        fo2 = outt[0]

        psA = pt("psA", [128, 1024], F32)
        psB = pt("psB", [128, 1024], F32)
        psO0 = pt("psO0", [65, 512], F32)
        psO1 = pt("psO1", [65, 512], F32)
        psT2 = pt("psT2", [128, 128], BF16)
        psQK = pt("psQK", [128, 512], F32)
        psT0 = psT2
        psT1 = psT2
        psTs = (psT2[:], psT2[:])
        psTs1 = (psA[:, 0:128], psA[:, 512:640])

        c_ldc = Chain(mksem("ldc"), 16)
        c_ldw = Chain(mksem("ldw"), 16)
        c_ldx = [Chain(mksem(f"ldx{j}"), 16) for j in range(2)]
        c_lds = [Chain(mksem(f"lds{j}"), 16) for j in range(2)]
        c_ldg = [Chain(mksem(f"ldg{j}"), 16) for j in range(2)]
        c_ld5 = [Chain(mksem(f"ld5{j}"), 16) for j in range(2)]
        c_stx = [Chain(mksem(f"stx{j}"), 16) for j in range(2)]
        c_stg = [Chain(mksem(f"stg{j}"), 16) for j in range(2)]
        c_sto = [Chain(mksem(f"sto{j}"), 16) for j in range(2)]
        c_stu = [Chain(mksem(f"stu{j}"), 16) for j in range(2)]
        tq = Chain(mksem("tq"))
        vq = Chain(mksem("vq"))
        aq = Chain(mksem("aq"))

        FL = Floors()
        block = ctx.enter_context(nc.Block())
        SYNC, GP, T, V, A = block.sync, block.gpsimd, block.tensor, block.vector, block.scalar

        free = {
            "xt0": (vq, 0), "xt1": (vq, 0),
            "xn0": (tq, 0), "xn1": (tq, 0),
            "xnTst0_dma": (c_stx[0], 0), "xnTst1_dma": (c_stx[1], 0),
            "xnTst0_ffn": (tq, 0), "xnTst1_ffn": (tq, 0),
            "psB_v": (vq, 0),
            "fo2_st": (c_stg[0], 0),
            "wst0": (vq, 0), "wst1": (vq, 0),
            "psT0": (vq, 0), "psT1": (vq, 0),
            "wt_buf": (tq, 0), "wbT": (tq, 0),
            "xslab0": (tq, 0), "xslab1": (tq, 0),
            "psQ": (vq, 0),
            "kT": (tq, 0), "qT": (tq, 0), "vbuf": (tq, 0),
            "negMt": (aq, 0),
            "maxacc": (vq, 0),
            "gld0": (aq, 0), "gld1": (aq, 0),
            "obuf0": (c_sto[0], 0), "obuf1": (c_sto[1], 0),
            "ostage0": (tq, 0), "ostage1": (tq, 0),

            "outt0": (c_stu[0], 0), "outt1": (c_stu[1], 0),
        }

        def FR(name):
            return free[name]

        def setF(name, chain):
            free[name] = (chain, chain.n)

        def wf(eng, key, name):
            if name == "kqhat1":
                name = "kqhat0"
            ch, n = free[name]
            FL.wait(eng, key, ch, n)

        @SYNC
        def _(sync):
            c_ldc.inc(sync.dma_start(identb[:], ident_in.ap()))
            c_ldc.inc(sync.dma_start(wrow[0:1, 0:64], knw.ap()))
            c_ldc.inc(sync.dma_start(wrow[0:1, 64:128], knw.ap()))
            c_ldc.inc(sync.dma_start(wrow[0:1, 128:192], qnw.ap()))
            c_ldc.inc(sync.dma_start(wrow[0:1, 192:256], qnw.ap()))
            c_ldc.inc(sync.dma_start(brow[:], scb.ap()))
            c_ldc.inc(sync.dma_start(identf[:], identf_in.ap()))
        ld_consts = c_ldc.n

        @V
        def _(vector):
            vq.inc(vector.memset(onesf[:], 1.0))
            vq.inc(vector.memset(ones8, 0.125))
            vq.inc(vector.memset(ones127, 1.0 / 127.0))
            vq.inc(vector.memset(epsLN[:], 1e-5))
            vq.inc(vector.memset(epsRMS[:], 1e-6))
            vq.inc(vector.memset(epsTINY[:], 1e-30))
        v_consts = vq.n

        if DBG.get("s0_bc", True):
            @T
            def _(tensor):
                FL.wait(tensor, "t", c_ldc, ld_consts)
                FL.wait(tensor, "t", vq, v_consts)
                tensor.matmul(psB[:, 0:128], onesf[:], wrow[0:1, 0:128], start=True, stop=True)
                tq.inc(tensor.matmul(psB[:, 128:256], ones8, wrow[0:1, 128:256],
                                     start=True, stop=True))
                tensor.matmul(psA[:, 0:512], ones127, brow[0:1, 0:512], start=True, stop=True)
                tq.inc(tensor.matmul(psA[:, 512:1024], ones127, brow[0:1, 512:1024],
                                     start=True, stop=True))
            t_bc1 = tq.n

            @V
            def _(vector):
                FL.wait(vector, "v", tq, t_bc1)
                vq.inc(vector.tensor_copy(wmul[:], psB[:, 0:256]))
                vq.inc(vector.tensor_copy(scbB[:], psA[:]))
            v_bc1 = vq.n

            @SYNC
            def _(sync):
                FL.wait(sync, "s", tq, t_bc1)
                c_ldc.inc(sync.dma_start(brow[:], fbias.ap()))
            ld_f = c_ldc.n

            @T
            def _(tensor):
                FL.wait(tensor, "t", c_ldc, ld_f)
                FL.wait(tensor, "t", vq, v_bc1)
                tensor.matmul(psB[:, 0:512], onesf[:], brow[0:1, 0:512], start=True, stop=True)
                tq.inc(tensor.matmul(psB[:, 512:1024], onesf[:], brow[0:1, 512:1024],
                                     start=True, stop=True))
            t_bc2 = tq.n

            @SYNC
            def _(sync):
                FL.wait(sync, "s", tq, t_bc2)
                c_ldc.inc(sync.dma_start(brow[:], proj_b.ap()))
            ld_p = c_ldc.n

            @T
            def _(tensor):
                FL.wait(tensor, "t", c_ldc, ld_p)
                tensor.matmul(psA[:, 0:512], onesf[:], brow[0:1, 0:512], start=True, stop=True)
                tq.inc(tensor.matmul(psA[:, 512:1024], onesf[:], brow[0:1, 512:1024],
                                     start=True, stop=True))
            t_bc3 = tq.n

            @V
            def _(vector):
                FL.wait(vector, "v", tq, t_bc2)
                vq.inc(vector.tensor_copy(fbiasB[:], psB[:]))
                FL.wait(vector, "v", tq, t_bc3)
                vq.inc(vector.tensor_copy(projbB[:], psA[:]))
            free["ps1T0"] = (vq, vq.n)
            free["ps1T1"] = (vq, vq.n)
            free["xn0"] = (tq, t_bc3)
            free["xn1"] = (tq, t_bc3)

        shw = {}

        def load_w_steps(dst_sb, src2d, r, wait_key=None, tag=""):
            reg = r % 2

            def dma_fn(r=r, reg=reg):
                @SYNC
                def _(sync):
                    wf(sync, "s", f"wst{reg}")
                    c_ldw.inc(sync.dma_start(wst3[:, reg * C:(reg + 1) * C],
                                             src2d.ap()[:, r * C:(r + 1) * C]))
                shw[f"{tag}ld{r}"] = c_ldw.n

            def cast_fn(r=r, reg=reg):
                @V
                def _(vector):
                    FL.wait(vector, "v", c_ldw, shw[f"{tag}ld{r}"])
                    if wait_key:
                        wf(vector, "v", wait_key)
                    vq.inc(vector.tensor_copy(dst_sb[:, r * C:(r + 1) * C],
                                              wst3[:, reg * C:(reg + 1) * C]))
                free[f"wst{reg}"] = (vq, vq.n)
                shw[f"{tag}rdy{r}"] = vq.n
            return [("S", dma_fn), ("V", cast_fn)]

        for r in range(8):
            for eng_, fn_ in load_w_steps(cbT, cb_t, r, tag="cb"):
                fn_()
        cb_ready = shw["cbrdy7"]

        for ip in range((KC // 2) if nstage >= 1 else 0):
            marks_x = [0, 0]
            for j in range(2):
                i = ip * 2 + j

                @SYNC
                def _(sync, i=i, j=j):
                    wf(sync, "s", f"xt{j}")
                    c_ldx[j].inc(sync.dma_start(xt[j][:], xb.ap()[i * 128:(i + 1) * 128, :]))
                marks_x[j] = c_ldx[j].n

            @V
            def _(vector):
                for j in range(2):
                    FL.wait(vector, "v", c_ldx[j], marks_x[j])
                    FL.wait(vector, "v", aq, free["lnstat"][1] if "lnstat" in free else 0)
                    vq.inc(vector.reduce_sum(lnsum[:], xt[j][:], axis=AX.X))
                    vq.inc(vector.tensor_mul(xsq[:], xt[j][:], xt[j][:]))
                    FL.wait(vector, "v", vq, vq.n)
                    vq.inc(vector.reduce_sum(xss[:], xsq[:], axis=AX.X))
                    vq.inc(vector.tensor_scalar_mul(lnmean[:, j:j + 1], lnsum[:], 1.0 / C))
                    FL.wait(vector, "v", vq, vq.n)
                    vq.inc(vector.tensor_scalar_mul(lnex2[:, j:j + 1], xss[:], 1.0 / C))
                FL.wait(vector, "v", vq, vq.n)
                vq.inc(vector.tensor_mul(lnm2[:], lnmean[:], lnmean[:]))
                FL.wait(vector, "v", vq, vq.n)
                vq.inc(vector.tensor_sub(lnvar[:], lnex2[:], lnm2[:]))
            mark_var = vq.n

            @A
            def _(scalar):
                FL.wait(scalar, "a", vq, mark_var)
                aq.inc(scalar.activation(lnl[:], lnvar[:], AF.Ln, bias=epsLN[:], scale=1.0))
                FL.wait(scalar, "a", aq, aq.n)
                aq.inc(scalar.activation(lnr[:], lnl[:], AF.Exp, scale=-0.5))
            mark_rstd = aq.n
            free["lnstat"] = (aq, aq.n)

            for j in range(2):
                i = ip * 2 + j
                own = i < NOWN

                @V
                def _(vector, i=i, j=j):
                    FL.wait(vector, "v", aq, mark_rstd)
                    wf(vector, "v", f"xn{j}")
                    vq.inc(vector.tensor_scalar(gld[j][:], xt[j][:],
                                                lnmean[:, j:j + 1], lnr[:, j:j + 1],
                                                ALU.subtract, ALU.mult))
                mark_xn = vq.n
                free[f"xt{j}"] = (vq, vq.n)

                for cc in range(8):
                    slot = cc % 2

                    @T
                    def _(tensor, cc=cc, j=j, slot=slot):
                        FL.wait(tensor, "t", vq, mark_xn)
                        wf(tensor, "t", f"ps1T{slot}")
                        wf(tensor, "t", f"xnTst{j}_dma")
                        wf(tensor, "t", f"xnTst{j}_ffn")
                        tq.inc(tensor.transpose(psTs1[slot],
                                                gld[j][:, cc * 128:(cc + 1) * 128],
                                                identf[:]))
                    mark_t = tq.n
                    if cc == 7:
                        free[f"xn{j}"] = (tq, tq.n)

                    @V
                    def _(vector, cc=cc, j=j, slot=slot):
                        FL.wait(vector, "v", tq, mark_t)
                        vq.inc(vector.tensor_copy(xnT_st[j][:, cc * 128:(cc + 1) * 128],
                                                  psTs1[slot]))
                    free[f"ps1T{slot}"] = (vq, vq.n)
                mark_xnT = vq.n

                @GP
                def _(gp, i=i, j=j):
                    FL.wait(gp, "g", vq, mark_xnT)
                    c_stx[j].inc(gp.dma_start(
                        xnT_h.ap()[:, :, i * 128:(i + 1) * 128].rearrange("c p t -> p c t"),
                        xnT_st[j].rearrange("p (c t) -> p c t", t=128)))
                free[f"xnTst{j}_dma"] = (c_stx[j], c_stx[j].n)

                if own:
                    @T
                    def _(tensor, j=j):
                        FL.wait(tensor, "t", vq, mark_xnT)
                        FL.wait(tensor, "t", vq, cb_ready)
                        wf(tensor, "t", "psB_v")
                        last = None
                        for half in range(2):
                            for cc in range(8):
                                last = tensor.matmul(
                                    psB[:, half * 512:(half + 1) * 512],
                                    xnT_st[j][:, cc * 128:(cc + 1) * 128],
                                    cbT[:, cc * C + half * 512: cc * C + half * 512 + 512],
                                    start=(cc == 0), stop=(cc == 7))
                        tq.inc(last)
                    mark_ffn = tq.n
                    free[f"xnTst{j}_ffn"] = (tq, tq.n)

                    @V
                    def _(vector, j=j):
                        FL.wait(vector, "v", tq, mark_ffn)
                        vq.inc(vector.tensor_mul(fo[:], psB[:], scbB[:]))
                        FL.wait(vector, "v", vq, vq.n)
                        wf(vector, "v", "fo2_st")
                        vq.inc(vector.tensor_add(fo2[:], fo[:], fbiasB[:]))
                    free["psB_v"] = (vq, vq.n)
                    mark_g = vq.n

                    @GP
                    def _(gp, i=i, j=j):
                        FL.wait(gp, "g", vq, mark_g)
                        c_stg[j].inc(gp.dma_start(g_h.ap()[i * 128:(i + 1) * 128, :], fo2[:]))
                    free["fo2_st"] = (c_stg[j], c_stg[j].n)

        stg_totals = [c_stg[0].n, c_stg[1].n]
        stx_totals = [c_stx[0].n, c_stx[1].n]
        free["wbT"] = (tq, tq.n)
        free["vbuf0"] = (tq, tq.n)
        free["outt0"] = (vq, vq.n)
        free["xt0"] = (vq, vq.n)
        free["xt1"] = (vq, vq.n)

        ssil = [scbB, fbiasB]
        free["ssil0"] = (vq, vq.n)
        free["ssil1"] = (vq, vq.n)

        free.update({
            "kT0": (tq, 0), "kT1": (tq, 0),
            "qT0": (tq, 0), "qT1": (tq, 0),
            "vbuf0": (tq, 0), "vbuf1": (tq, 0),
            "negMt0": (aq, 0), "negMt1": (aq, 0),
            "rms16": (aq, 0), "psQK": (vq, 0),
        })
        for sl in range(NLAG):
            free[f"qkvlag{sl}"] = (vq, 0)
        for sl in range(NSLAB):
            free[f"rms_sl{sl % 4}"] = (aq, 0)
            free[f"rstd_sl{sl % 2}"] = (vq, 0)
        free["kqhat0"] = (tq, 0)
        free["kqhat1"] = (tq, 0)
        kq_hat2 = [kq_hat, kq_hat]
        rms64 = sb("rms64", [128, 64], F32)
        rstd32 = sb("rstd32", [128, 32], F32)
        mark_negM_p = [0, 0]
        d01 = gld[0][0:1, 0:1024]
        r01 = gld[1][0:1, 0:1024]
        lnscr = s1t[0:1, 0:1024]
        sh = {}
        tslot = [0]

        def wtprep_steps(hp):
            steps = []
            for r in range(3):
                reg = r % 2

                def dma_fn(hp=hp, r=r, reg=reg):
                    @SYNC
                    def _(sync):
                        wf(sync, "s", f"wst{reg}")
                        c_ldw.inc(sync.dma_start(wst3[:, reg * C:(reg + 1) * C],
                                                 qkv_t.ap()[hp, :, r * C:(r + 1) * C]))
                    sh[f"wld{r}"] = c_ldw.n
                steps.append(("S", dma_fn))

                def cast_fn(r=r, reg=reg):
                    @V
                    def _(vector):
                        FL.wait(vector, "v", c_ldw, sh[f"wld{r}"])
                        wf(vector, "v", "wt_buf")
                        vq.inc(vector.tensor_copy(wt_buf[:, r * C:(r + 1) * C],
                                                  wst3[:, reg * C:(reg + 1) * C]))
                    free[f"wst{reg}"] = (vq, vq.n)
                    sh["wt_ready"] = vq.n
                steps.append(("V", cast_fn))
            return steps

        def s3_streams(hp):
            p = hp % 2
            kT, qT, vbuf = kTp[p], qTp[p], vbufp[p]
            st1 = [[] for _ in range(NSLAB)]
            st2 = [[] for _ in range(NSLAB)]
            tail = []

            def init_fn():
                @V
                def _(vector):
                    wf(vector, "v", "maxacc")
                    vq.inc(vector.memset(macc128[:], 0.0))
                    wf(vector, "v", f"vbuf{p}")
                    vq.inc(vector.memset(
                        vbuf.rearrange("p (c j) -> p c j", j=65)[:, :, 64:65], 1.0))
                sh[f"vones{hp}"] = vq.n
            st1[0].append(("V", init_fn))

            for s in range(NSLAB):
                jsl = s % 2

                def dma_fn(s=s, jsl=jsl):
                    @SYNC
                    def _(sync):
                        wf(sync, "s", f"xslab{jsl}")
                        FL.wait(sync, "s", c_stx[0], stx_totals[0])
                        FL.wait(sync, "s", c_stx[1], stx_totals[1])
                        for cc in range(8):
                            c_lds[jsl].inc(sync.dma_start(
                                xslab[jsl][:, cc * 512:(cc + 1) * 512],
                                xnT_h.ap()[cc, :, s * 512:(s + 1) * 512]))
                    sh[f"slab{s}"] = c_lds[jsl].n
                st1[s].append(("S", dma_fn))

                for c4 in range(4):
                    i = s * 4 + c4
                    slot = (s % NLAG) * 4 + c4

                    def qkv_fn(s=s, c4=c4, i=i, jsl=jsl):
                        @T
                        def _(tensor):
                            FL.wait(tensor, "t", c_lds[jsl], sh[f"slab{s}"])
                            FL.wait(tensor, "t", vq, sh["wt_ready"])
                            FL.wait(tensor, "t", vq, sh[f"vones{hp}"])
                            wf(tensor, "t", "psQK")
                            last = None
                            for cc in range(8):
                                last = tensor.matmul(
                                    psQK[:, 0:384],
                                    xslab[jsl][:, cc * 512 + c4 * 128:
                                               cc * 512 + c4 * 128 + 128],
                                    wt_buf[:, cc * 384: cc * 384 + 384],
                                    start=(cc == 0), stop=(cc == 7))
                            tq.inc(last)
                        sh[f"qkv{i}"] = tq.n
                        if c4 == 3:
                            free[f"xslab{jsl}"] = (tq, tq.n)
                        if s == NSLAB - 1 and c4 == 3:
                            setF("wt_buf", tq)
                    st1[s].append(("T", qkv_fn))

                    def evac_fn(s=s, c4=c4, i=i, slot=slot):
                        @V
                        def _(vector):
                            FL.wait(vector, "v", tq, sh[f"qkv{i}"])
                            wf(vector, "v", f"qkvlag{slot // 4}")
                            vq.inc(vector.tensor_copy(
                                qkvlag[:, slot * 384:(slot + 1) * 384],
                                psQK[:, 0:384]))
                            FL.wait(vector, "v", vq, vq.n)
                            vq.inc(vector.tensor_mul(
                                sqt[:],
                                qkvlag[:, slot * 384: slot * 384 + 256],
                                qkvlag[:, slot * 384: slot * 384 + 256]))
                            FL.wait(vector, "v", vq, vq.n)
                            wf(vector, "v", f"rms_sl{s % 4}")
                            vq.inc(vector.reduce_sum(
                                rms64[:, (s % 4) * 16 + c4 * 4:
                                      (s % 4) * 16 + (c4 + 1) * 4],
                                sqt.rearrange("p (g d) -> p g d", d=DH),
                                axis=AX.X))
                        free["psQK"] = (vq, vq.n)
                        sh[f"rms{i}"] = vq.n
                    st1[s].append(("V", evac_fn))

                def rstd_ln_fn(s=s):
                    scr = rstdL if s % 2 == 0 else rstd16
                    @A
                    def _(scalar):
                        FL.wait(scalar, "a", aq, aq.n)
                        FL.wait(scalar, "a", vq, sh[f"rms{s * 4 + 3}"])
                        aq.inc(scalar.activation(
                            scr[:], rms64[:, (s % 4) * 16:(s % 4) * 16 + 16],
                            AF.Ln, bias=epsRMS[:], scale=1.0 / DH))
                    free[f"rms_sl{s % 4}"] = (aq, aq.n)
                    sh[f"rstdln{s}"] = aq.n
                st2[s].append(("A1", rstd_ln_fn))

                def rstd_exp_fn(s=s):
                    scr = rstdL if s % 2 == 0 else rstd16
                    @A
                    def _(scalar):
                        FL.wait(scalar, "a", aq, sh[f"rstdln{s}"])
                        wf(scalar, "a", f"rstd_sl{s % 2}")
                        aq.inc(scalar.activation(
                            rstd32[:, (s % 2) * 16:(s % 2) * 16 + 16],
                            scr[:], AF.Exp, scale=-0.5))
                    sh[f"rstd{s}"] = aq.n
                st2[s].append(("A2", rstd_exp_fn))

                for c4 in range(4):
                    i = s * 4 + c4
                    slot = (s % NLAG) * 4 + c4
                    own = i < NOWN
                    kpar = i % 2

                    def norm_fn(s=s, c4=c4, i=i, slot=slot, kpar=kpar, p=p):
                        @V
                        def _(vector):
                            FL.wait(vector, "v", aq, sh[f"rstd{s}"])
                            wf(vector, "v", f"kqhat{kpar}")
                            base = slot * 384
                            rb_ = (s % 2) * 16 + c4 * 4
                            for g_ in range(4):
                                vq.inc(vector.tensor_scalar(
                                    tmn[:, g_ * 64:(g_ + 1) * 64],
                                    qkvlag[:, base + g_ * 64: base + (g_ + 1) * 64],
                                    rstd32[:, rb_ + g_: rb_ + g_ + 1], None, ALU.mult))
                            FL.wait(vector, "v", vq, vq.n)
                            vq.inc(vector.tensor_mul(kq_hat2[kpar][:], tmn[:], wmul[:]))
                            dst = vbufp[p].rearrange("p (h k j) -> p h k j", h=2, j=65)[
                                :, :, i:i + 1, 0:64].rearrange("p h k j -> p (h k) j")
                            srcv = qkvlag[:, base + 256: base + 384].rearrange(
                                "p (h j) -> p h j", h=2)
                            vq.inc(vector.tensor_copy(dst, srcv))
                        if c4 == 3:
                            free[f"qkvlag{slot // 4}"] = (vq, vq.n)
                        if s == NSLAB - 1 and c4 == 3:
                            free[f"rstd_sl{s % 2}"] = (vq, vq.n)
                        else:
                            free[f"rstd_sl{s % 2}"] = (vq, vq.n)
                        sh[f"norm{i}"] = vq.n
                    st2[s].append(("V", norm_fn))

                    def tpk_fn(i=i, kpar=kpar, p=p):
                        @T
                        def _(tensor):
                            sl = 0
                            sh[f"tslk{i}"] = sl
                            FL.wait(tensor, "t", vq, sh[f"norm{i}"])
                            wf(tensor, "t", f"psT{sl}")
                            wf(tensor, "t", f"kT{p}")
                            tq.inc(tensor.transpose(psTs[sl], kq_hat2[kpar][:, 0:128],
                                                    identb[:]))
                        sh[f"tpk{i}"] = tq.n
                    st2[s].append(("T", tpk_fn))

                    def evk_fn(i=i, p=p, own=own, kpar=kpar):
                        @V
                        def _(vector):
                            sl = sh[f"tslk{i}"]
                            FL.wait(vector, "v", tq, sh[f"tpk{i}"])
                            vq.inc(vector.tensor_copy(
                                kTp[p][:, i * 128:(i + 1) * 128], psTs[sl]))
                            free["psT0"] = (vq, vq.n)
                            if not own:
                                vq.inc(vector.tensor_mul(sqm[:], kq_hat2[kpar][:],
                                                         kq_hat2[kpar][:]))
                                FL.wait(vector, "v", vq, vq.n)
                                vq.inc(vector.reduce_sum(
                                    m4[:], sqm.rearrange("p (g d) -> p g d", d=DH),
                                    axis=AX.X))
                                FL.wait(vector, "v", vq, vq.n)
                                wf(vector, "v", "maxacc")
                                vq.inc(vector.tensor_max(macc128[:, 0:2],
                                                         macc128[:, 0:2], m4[:, 0:2]))
                                free["maxacc"] = (vq, vq.n)
                        if not own:
                            free["kqhat0"] = (tq, sh[f"tpk{i}"])
                        sh[f"evk{i}"] = vq.n
                    st2[s].append(("V", evk_fn))

                    if own:
                        def tpq_fn(i=i, kpar=kpar, p=p):
                            @T
                            def _(tensor):
                                sl = 0
                                sh[f"tslq{i}"] = sl
                                FL.wait(tensor, "t", vq, sh[f"evk{i}"])
                                wf(tensor, "t", f"psT{sl}")
                                wf(tensor, "t", f"qT{p}")
                                tq.inc(tensor.transpose(psTs[sl],
                                                        kq_hat2[kpar][:, 128:256],
                                                        identb[:]))
                            sh[f"tpq{i}"] = tq.n
                        st2[s].append(("T", tpq_fn))

                        def evq_fn(i=i, p=p, kpar=kpar):
                            @V
                            def _(vector):
                                sl = sh[f"tslq{i}"]
                                FL.wait(vector, "v", tq, sh[f"tpq{i}"])
                                vq.inc(vector.tensor_copy(
                                    qTp[p][:, i * 128:(i + 1) * 128], psTs[sl]))
                                free["psT0"] = (vq, vq.n)
                                vq.inc(vector.tensor_mul(sqm[:], kq_hat2[kpar][:],
                                                         kq_hat2[kpar][:]))
                                FL.wait(vector, "v", vq, vq.n)
                                vq.inc(vector.reduce_sum(
                                    m4[:], sqm.rearrange("p (g d) -> p g d", d=DH),
                                    axis=AX.X))
                                FL.wait(vector, "v", vq, vq.n)
                                wf(vector, "v", "maxacc")
                                vq.inc(vector.tensor_max(macc128[:], macc128[:], m4[:]))
                                free["maxacc"] = (vq, vq.n)
                            free["kqhat0"] = (tq, sh[f"tpq{i}"])
                        st2[s].append(("V", evq_fn))

            def tail1(p=p):
                sh[f"kq_ready{p}"] = vq.n

                @V
                def _(vector):
                    FL.wait(vector, "v", vq, vq.n)
                    vq.inc(vector.tensor_copy(m4b[:], macc128[:]))
                sh["mab"] = vq.n
            tail.append(("V", tail1))

            def tail2():
                @T
                def _(tensor):
                    FL.wait(tensor, "t", vq, sh["mab"])
                    wf(tensor, "t", "psT0")
                    tq.inc(tensor.transpose(psT0[0:4, 0:128], m4b[:], identb[:]))
                sh["mat"] = tq.n
            tail.append(("T", tail2))

            def tail3():
                @V
                def _(vector):
                    FL.wait(vector, "v", tq, sh["mat"])
                    vq.inc(vector.reduce_max(mx4[:], psT0[0:4, 0:128], axis=AX.X))
                    free["psT0"] = (vq, vq.n)
                    FL.wait(vector, "v", vq, vq.n)
                    vq.inc(vector.tensor_copy(maxaccb[:], mx4[:]))
                sh["mab2"] = vq.n
            tail.append(("V", tail3))

            def tail4():
                @T
                def _(tensor):
                    FL.wait(tensor, "t", vq, sh["mab2"])
                    wf(tensor, "t", "psT0")
                    tq.inc(tensor.transpose(psT0[0:1, 0:4], maxaccb[:], identb[0:4, 0:4]))
                sh["mat2"] = tq.n
            tail.append(("T", tail4))

            def tail5():
                @V
                def _(vector):
                    FL.wait(vector, "v", tq, sh["mat2"])
                    vq.inc(vector.tensor_copy(mrow[:], psT0[0:1, 0:4]))
                    free["psT0"] = (vq, vq.n)
                    FL.wait(vector, "v", vq, vq.n)
                    vq.inc(vector.tensor_mul(mab[:], mrow[0:1, 0:2], mrow[0:1, 2:4]))
                    FL.wait(vector, "v", vq, vq.n)
                    vq.inc(vector.reduce_max(mab[0:1, 0:1], mab[:], axis=AX.X))
                sh["mabf"] = vq.n
            tail.append(("V", tail5))

            def tail6():
                @A
                def _(scalar):
                    FL.wait(scalar, "a", vq, sh["mabf"])
                    aq.inc(scalar.activation(mln[0:1, 0:1], mab[0:1, 0:1], AF.Ln,
                                             bias=epsTINY[:], scale=1.0))
                    FL.wait(scalar, "a", aq, aq.n)
                    aq.inc(scalar.activation(negm2[0:1, 0:1], mln[0:1, 0:1],
                                             AF.Exp, scale=0.5))
                sh["mh"] = aq.n
            tail.append(("A", tail6))

            def tail7():
                @T
                def _(tensor):
                    FL.wait(tensor, "t", aq, sh["mh"])
                    wf(tensor, "t", "psQK")
                    tq.inc(tensor.matmul(psQK[:, 384:385], onesf[:], negm2[0:1, 0:1],
                                         start=True, stop=True))
                sh["mq"] = tq.n
            tail.append(("T", tail7))

            def tail8(p=p):
                @V
                def _(vector):
                    FL.wait(vector, "v", tq, sh["mq"])
                    wf(vector, "v", f"negMt{p}")
                    vq.inc(vector.tensor_scalar_mul(negMt[:, p:p + 1],
                                                    psQK[:, 384:385], -1.0))
                free["psQK"] = (vq, vq.n)
                mark_negM_p[p] = vq.n
            tail.append(("V", tail8))
            return st1, st2, tail

        def run_steps(steps):
            for eng_, fn in steps:
                fn()

        prev_ofree = (vq, 0)
        prev_bankfree = (vq, 0)
        prev_exp_end = 0

        if nstage >= 3:
            run_steps(wtprep_steps(0))
            st1_, st2_, tail_ = s3_streams(0)
            for s in range(NSLAB):
                run_steps(st1_[s])
                run_steps(st2_[s])
            run_steps(tail_)

        for w in range(HP if nstage >= 4 else 0):
            p_r = w % 2
            kT, qT, vbuf = kTp[p_r], qTp[p_r], vbufp[p_r]
            if w + 1 < HP and nstage >= 3:
                nxt_pre = wtprep_steps(w + 1)
                st1_, st2_, tail_ = s3_streams(w + 1)
                qb_assign = {i: [] for i in range(NQB)}
                qb_a = {i: [] for i in range(NQB)}
                qb_assign[0].extend(nxt_pre)
                for s in range(NSLAB):
                    steps = st1_[s]
                    di = 1 if s == 0 else 0
                    dma_step = steps[di]
                    rest = steps[:di] + steps[di + 1:]
                    if s >= 2:
                        qb_assign[max(0, s // 2 - 1)].append(dma_step)
                    else:
                        qb_assign[0].append(dma_step)
                    qb_assign[min(s // 2, NQB - 1)].extend(rest)
                lo_lists = []
                for s in range(NSLAB):
                    dst_qb = s // 2 + 1
                    if dst_qb < NQB:
                        qb_a[dst_qb - 1].extend(
                            (eng_, fn) for eng_, fn in st2_[s]
                            if eng_ in ("A", "A1", "A2"))
                        qb_assign[dst_qb].extend(
                            (eng_, fn) for eng_, fn in st2_[s]
                            if eng_ not in ("A", "A1", "A2"))
                    else:
                        lo_lists.append(st2_[s])
                def _groups(steps):
                    return [steps[0:1]] + [steps[k:k + 3]
                                           for k in range(1, len(steps), 3)]
                leftover = [st for pair in
                            itertools.zip_longest(*map(_groups, lo_lists))
                            for grp in pair if grp is not None
                            for st in grp] if lo_lists else []
                tail_steps = leftover + tail_
            else:
                qb_assign = {i: [] for i in range(NQB)}
                qb_a = {i: [] for i in range(NQB)}
                leftover = []
                tail_steps = []
                if nstage >= 5:
                    for r in range(8):
                        qb_assign[min(r, NQB - 1)].extend(
                            load_w_steps(projT, proj_t, r, wait_key="wbT", tag="pj"))

            for qb in range(NQB):
                exp_mark = {kc: 0 for kc in range(KC)}
                pending = qb_assign[qb]
                a_steps = ([fn for e_, fn in qb_a[qb] if e_ == "A1"]
                           + [fn for e_, fn in qb_a[qb] if e_ != "A1"])
                other = [(eng_, fn) for eng_, fn in pending
                         if eng_ not in ("A", "A1", "A2")]

                aq_base = aq.n
                for kc in range(KC):
                    exp_mark[kc] = aq_base + kc + 1
                aq.reserve(KC)
                mark_r = aq.reserve(2)

                s_mark = {}
                pv_mark = {}
                psS = (psA, psB)
                noct = 32
                per = (len(other) + noct - 1) // noct if other else 0
                oc_steps = [other[o * per:(o + 1) * per] for o in range(noct)]

                for oc in range(noct):
                    kc_lo, kc_hi = oc * (KC // noct), (oc + 1) * (KC // noct)
                    if oc == noct - 1:
                        kc_hi = KC + 1
                    kc_hi = min(kc_hi, KC + 1)

                    @T
                    def _(tensor, qb=qb, kc_lo=kc_lo, kc_hi=kc_hi, p_r=p_r):
                        FL.wait(tensor, "t", vq, sh.get(f"kq_ready{p_r}", 0) or mark_negM_p[p_r])
                        for kc in range(kc_lo, kc_hi):
                            if kc < KC:
                                par = kc % 2
                                FL.wait(tensor, "t", aq,
                                        exp_mark.get(kc - 2, 0) if kc >= 2 else prev_exp_end)
                                tensor.matmul(psS[par][:, 0:512],
                                              kT[0:64, kc * 128:(kc + 1) * 128],
                                              qT[0:64, qb * 512:(qb + 1) * 512],
                                              start=True, stop=True)
                                tq.inc(tensor.matmul(psS[par][:, 512:1024],
                                                     kT[64:128, kc * 128:(kc + 1) * 128],
                                                     qT[64:128, qb * 512:(qb + 1) * 512],
                                                     start=True, stop=True,
                                                     tile_position=(64, 0)))
                                s_mark[kc] = tq.n
                            if kc >= 1:
                                kp = kc - 1
                                par = kp % 2
                                FL.wait(tensor, "t", aq, exp_mark[kp])
                                FL.wait(tensor, "t", vq, prev_ofree[1])
                                tensor.matmul(psO0[:],
                                              vbuf[:, kp * 65: kp * 65 + 65],
                                              ptb[par][:, 0:512],
                                              start=(kp == 0), stop=(kp == KC - 1))
                                tq.inc(tensor.matmul(
                                    psO1[:],
                                    vbuf[:, (KC + kp) * 65: (KC + kp) * 65 + 65],
                                    ptb[par][:, 512:1024],
                                    start=(kp == 0), stop=(kp == KC - 1)))
                                pv_mark[kp] = tq.n
                    for eng_, fn in oc_steps[oc]:
                        fn()
                mark_pv_last = pv_mark[KC - 1]

                @V
                def _(vector, qb=qb, hp=w):
                    FL.wait(vector, "v", tq, mark_pv_last)
                    vq.inc(vector.tensor_copy(d01[0:1, 0:512], psO0[64:65, :]))
                    vq.inc(vector.tensor_copy(d01[0:1, 512:1024], psO1[64:65, :]))
                mark_d = vq.n

                @T
                def _(tensor, qb=qb):
                    FL.wait(tensor, "t", aq, mark_r)
                    wf(tensor, "t", "psQK")
                    tensor.matmul(psQK[0:64, 0:512], onesf[0:1, 0:64],
                                  r01[0:1, 0:512], start=True, stop=True)
                    tq.inc(tensor.matmul(psQK[64:128, 0:512], onesf[0:1, 0:64],
                                         r01[0:1, 512:1024],
                                         start=True, stop=True, tile_position=(0, 64)))
                mark_bc = tq.n

                @V
                def _(vector, qb=qb, hp=w):
                    FL.wait(vector, "v", tq, mark_bc)
                    vq.inc(vector.tensor_copy(rb0[:], psQK[0:64, 0:512]))
                    vq.inc(vector.tensor_copy(rb1[:], psQK[64:128, 0:512]))
                    FL.wait(vector, "v", vq, vq.n)
                    jo = qb % 2
                    wf(vector, "v", f"obuf{jo}")
                    vq.inc(vector.tensor_mul(obuf[jo][0:64, :], psO0[0:64, :], rb0[:]))
                    vq.inc(vector.tensor_mul(obuf[jo][64:128, :], psO1[0:64, :], rb1[:]))
                free["psQK"] = (vq, vq.n)
                prev_ofree = (vq, vq.n)
                mark_ob = vq.n

                @A
                def _(scalar, qb=qb, p_r=p_r):
                    FL.wait(scalar, "a", vq, mark_negM_p[p_r])
                    for kc in range(KC):
                        par = kc % 2
                        FL.wait(scalar, "a", tq, s_mark[kc])
                        FL.wait(scalar, "a", tq, pv_mark.get(kc - 2, 0))
                        aq.inc_pre(scalar.activation(ptb[par][:], psS[par][:], AF.Exp,
                                                     bias=negMt[:, p_r:p_r + 1],
                                                     scale=1.0))
                    FL.wait(scalar, "a", vq, mark_d)
                    aq.inc_pre(scalar.activation(lnscr, d01, AF.Ln, scale=1.0))
                    FL.wait(scalar, "a", aq, mark_r - 1)
                    aq.inc_pre(scalar.activation(r01, lnscr, AF.Exp, scale=-1.0))
                FL.f.setdefault("a", {})[id(aq)] = max(
                    FL.f.get("a", {}).get(id(aq), 0), mark_r)
                prev_exp_end = mark_r

                for fn in a_steps:
                    fn()

                @GP
                def _(gp, qb=qb, hp=w):
                    FL.wait(gp, "g", vq, mark_ob)
                    jo = qb % 2
                    c_sto[jo].inc(gp.dma_start(oT_h.ap()[hp, :, qb * 512:(qb + 1) * 512],
                                               obuf[jo][:]))
                free[f"obuf{qb % 2}"] = (c_sto[qb % 2], c_sto[qb % 2].n)

            run_steps(tail_steps)
            free[f"kT{p_r}"] = (tq, tq.n)
            free[f"qT{p_r}"] = (tq, tq.n)
            free[f"vbuf{p_r}"] = (tq, tq.n)
            free[f"negMt{p_r}"] = (aq, aq.n)

        sto_totals = [c_sto[0].n, c_sto[1].n]

        if "pjrdy7" not in shw:
            for r in range(8):
                for eng_, fn_ in load_w_steps(projT, proj_t, r,
                                              wait_key="wbT", tag="pj"):
                    fn_()
        mark_pj = shw["pjrdy7"]

        for i in range(NOWN if nstage >= 5 else 0):
            j = i % 2

            @SYNC
            def _(sync, i=i, j=j):
                FL.wait(sync, "s", c_sto[0], sto_totals[0])
                FL.wait(sync, "s", c_sto[1], sto_totals[1])
                FL.wait(sync, "s", c_stg[0], stg_totals[0])
                FL.wait(sync, "s", c_stg[1], stg_totals[1])
                wf(sync, "s", f"ostage{j}")
                for hp in range(HP):
                    c_ld5[j].inc(sync.dma_start(ostage[j][:, hp * 128:(hp + 1) * 128],
                                                oT_h.ap()[hp, :, i * 128:(i + 1) * 128]))
                wf(sync, "s", f"gld{j}")
                c_ld5[j].inc(sync.dma_start(gld[j][:], g_h.ap()[i * 128:(i + 1) * 128, :]))
                wf(sync, "s", f"xt{j}")
                c_ld5[j].inc(sync.dma_start(xld[j][:], xb.ap()[i * 128:(i + 1) * 128, :]))
            mark_ld = c_ld5[j].n

            @T
            def _(tensor, i=i, j=j):
                FL.wait(tensor, "t", c_ld5[j], mark_ld)
                FL.wait(tensor, "t", vq, mark_pj)
                FL.wait(tensor, "t", vq, free["psA_s5"][1] if "psA_s5" in free else 0)
                last = None
                for half in range(2):
                    for cc in range(8):
                        last = tensor.matmul(
                            psA[:, half * 512:(half + 1) * 512],
                            ostage[j][:, cc * 128:(cc + 1) * 128],
                            projT[:, cc * C + half * 512: cc * C + half * 512 + 512],
                            start=(cc == 0), stop=(cc == 7))
                tq.inc(last)
            free[f"ostage{j}"] = (tq, tq.n)
            mark_proj = tq.n

            @A
            def _(scalar, i=i, j=j):
                FL.wait(scalar, "a", c_ld5[j], mark_ld)
                wf(scalar, "a", f"ssil{j}")
                aq.inc(scalar.activation(ssil[j][:], gld[j][:], AF.Silu, scale=1.0))
            free[f"gld{j}"] = (aq, aq.n)
            mark_sil = aq.n

            @V
            def _(vector, i=i, j=j):
                FL.wait(vector, "v", tq, mark_proj)
                vq.inc(vector.tensor_add(s1t[:], psA[:], projbB[:]))
                FL.wait(vector, "v", vq, vq.n)
                FL.wait(vector, "v", aq, mark_sil)
                vq.inc(vector.tensor_mul(s2t[:], s1t[:], ssil[j][:]))
                FL.wait(vector, "v", vq, vq.n)
                wf(vector, "v", f"outt{j}")
                vq.inc(vector.tensor_add(outt[j][:], s2t[:], xld[j][:]))
            free["psA_s5"] = (vq, vq.n)
            free[f"ssil{j}"] = (vq, vq.n)
            free[f"xt{j}"] = (vq, vq.n)
            mark_out = vq.n

            @GP
            def _(gp, i=i, j=j):
                FL.wait(gp, "g", vq, mark_out)
                c_stu[j].inc(gp.dma_start(out.ap()[i * 128:(i + 1) * 128, :], outt[j][:]))
            free[f"outt{j}"] = (c_stu[j], c_stu[j].n)

        stu_totals = [c_stu[0].n, c_stu[1].n]

        @GP
        def _(gp):
            gp.wait_ge(c_stu[0].sem, stu_totals[0])
            gp.wait_ge(c_stu[1].sem, stu_totals[1])

        @SYNC
        def _(sync):
            sync.wait_ge(c_stu[0].sem, stu_totals[0])
            sync.wait_ge(c_stu[1].sem, stu_totals[1])

    return nc



def _shard_inputs(L0_attn_x, L0_attn_qkv_w, L0_attn_proj_w, L0_attn_proj_b,
                  L0_attn_q_norm_w, L0_attn_k_norm_w, L0_ffn_CB, L0_ffn_SCB,
                  L0_ffn_bias):
    x = np.ascontiguousarray(np.asarray(L0_attn_x, dtype=np.float32))
    B, N, Cc = x.shape
    W = np.asarray(L0_attn_qkv_w, np.float32)
    kw = W[C:2 * C].reshape(HP, 128, 8, 128)
    qw = W[0:C].reshape(HP, 128, 8, 128)
    vw = W[2 * C:3 * C].reshape(HP, 128, 8, 128)
    g3 = np.stack([kw, qw, vw], axis=0)
    qkv_t = np.ascontiguousarray(
        g3.transpose(1, 4, 3, 0, 2).reshape(HP, 128, 3 * 8 * 128))
    pw = np.asarray(L0_attn_proj_w, np.float32).reshape(8, 128, 8, 128)
    proj_t = np.ascontiguousarray(pw.transpose(3, 2, 0, 1).reshape(128, 8 * Cc))
    cw = np.asarray(L0_ffn_CB, np.float32).reshape(8, 128, 8, 128)
    cb_t = np.ascontiguousarray(cw.transpose(3, 2, 0, 1).reshape(128, 8 * Cc))
    common = {
        "qkv_t": qkv_t,
        "proj_t": proj_t,
        "proj_b": np.asarray(L0_attn_proj_b, np.float32).reshape(1, Cc),
        "qnw": np.asarray(L0_attn_q_norm_w, np.float32).reshape(1, DH),
        "knw": np.asarray(L0_attn_k_norm_w, np.float32).reshape(1, DH),
        "cb_t": cb_t,
        "scb": np.asarray(L0_ffn_SCB, np.float32).reshape(1, Cc),
        "fbias": np.asarray(L0_ffn_bias, np.float32).reshape(1, Cc),
        "ident_in": np.eye(128, dtype=np.float32).astype(bfdt),
        "identf_in": np.eye(128, dtype=np.float32),
    }
    in_maps = []
    half = N // 2
    for c in range(8):
        b, h = c // 2, c % 2
        own = x[b, h * half:(h + 1) * half]
        other = x[b, (1 - h) * half:(2 - h) * half]
        m = dict(common)
        m["xb"] = np.ascontiguousarray(np.concatenate([own, other], axis=0))
        in_maps.append(m)
    return in_maps, (B, N, Cc)


_CACHE = {}


def kernel(**inputs):
    in_maps, (B, N, Cc) = _shard_inputs(**inputs)
    key = (N,)
    if key not in _CACHE:
        _CACHE[key] = build(T_ALL=N, T_OWN=N // 2)
    nc = _CACHE[key]
    res = run_bass_kernel_spmd(nc, in_maps, core_ids=list(range(8)))
    half = N // 2
    outp = np.empty((B, N, Cc), dtype=np.float32)
    for c in range(8):
        b, h = c // 2, c % 2
        outp[b, h * half:(h + 1) * half] = res.results[c]["out"]
    return outp
